# version 04
# baseline (speedup 1.0000x reference)
import numpy as np
import concourse.bass as bass
import concourse.mybir as mybir
from concourse.bass_utils import run_bass_kernel_spmd
from contextlib import ExitStack

F32 = mybir.dt.float32
BF16 = mybir.dt.bfloat16
AF = mybir.ActivationFunctionType
ALU = mybir.AluOpType
AX = mybir.AxisListType

T = 2048
NT = 16
NTB = 4
NCH = 32
CH = 64
NH = 32
HD = 64
DM = 1024
DI = 2048
KD = 8
NSLAB = 17
XN_W = 2051

DEBUG = False
STREAMS = ("pe", "dve", "act", "pool", "sdma", "gdma")
BODY = {"pe": "tensor", "dve": "vector", "act": "scalar",
        "pool": "gpsimd", "gdma": "gpsimd", "sdma": "sync"}


class PB:
    def __init__(self):
        self.ops = []
        self.lastw = {}
        self.readers = {}

    def emit(self, stream, fn, r=(), w=(), dep=(), tag="", sync=False):
        i = len(self.ops)
        deps = set(dep)
        for k in r:
            lw = self.lastw.get(k)
            if lw is not None:
                deps.add(lw)
        for k in w:
            lw = self.lastw.get(k)
            if lw is not None:
                deps.add(lw)
            deps.update(self.readers.get(k, ()))
        self.ops.append(dict(stream=stream, fn=fn, deps=deps, tag=tag, sync=sync,
                             is_dma=stream in ("sdma", "gdma")))
        for k in r:
            self.readers.setdefault(k, set()).add(i)
        for k in w:
            self.lastw[k] = i
            self.readers[k] = set()
        return i

    def barrier(self):
        last = {}
        for i, op in enumerate(self.ops):
            last[op["stream"]] = i
        for s in STREAMS:
            deps = [last[t] for t in STREAMS if t != s and t in last]
            self.emit(s, lambda e: e.nop(), dep=deps, tag="bar")

    def finalize(self, block, sems):
        ops = self.ops
        need = [False] * len(ops)
        for i, op in enumerate(ops):
            kept = []
            for d in op["deps"]:
                dop = ops[d]
                same_body = BODY[dop["stream"]] == BODY[op["stream"]]
                if same_body and not dop["is_dma"] and not op["sync"]:
                    continue
                kept.append(d)
                need[d] = True
            op["kept"] = kept
        for i, op in enumerate(ops):
            if op["tag"] == "out" or op["is_dma"]:
                need[i] = True
        cnt = {s: 0 for s in STREAMS}
        for i, op in enumerate(ops):
            if need[i]:
                cnt[op["stream"]] += 16 if op["is_dma"] else 1
            op["val"] = cnt[op["stream"]]
            op["inc"] = need[i]
        final_sd = cnt["sdma"]

        body_ops = {}
        for i, op in enumerate(ops):
            body_ops.setdefault(BODY[op["stream"]], []).append(i)

        def run_body(eng, idxs):
            water = {}
            for i in idxs:
                op = ops[i]
                for d in op["kept"]:
                    dop = ops[d]
                    v = dop["val"]
                    if water.get(dop["stream"], 0) < v:
                        eng.wait_ge(sems[dop["stream"]], v)
                        water[dop["stream"]] = v
                ins = op["fn"](eng)
                if op["inc"]:
                    ins.then_inc(sems[op["stream"]], 16 if op["is_dma"] else 1)

        @block.tensor
        def _(t):
            run_body(t, body_ops.get("tensor", []))

        @block.vector
        def _(v):
            run_body(v, body_ops.get("vector", []))

        @block.scalar
        def _(s):
            run_body(s, body_ops.get("scalar", []))

        @block.gpsimd
        def _(g):
            run_body(g, body_ops.get("gpsimd", []))

        @block.sync
        def _(sy):
            run_body(sy, body_ops.get("sync", []))
            if final_sd:
                sy.wait_ge(sems["sdma"], final_sd)


def _mkv(base_ap, dims, off):
    newap = [list(base_ap.ap[0])] + [[s, c] for (s, c) in dims]
    return bass.AP(base_ap.tensor, base_ap.offset + off, newap)


def build_nc():
    nc = bass.Bass("TRN2", target_bir_lowering=False)
    din = lambda n, s: nc.dram_tensor(n, s, F32, kind="ExternalInput").ap()

    u_in = din("u", [T, DM])
    up_in = din("up", [T, DM])
    winT = din("winT", [DM, 4256])
    convw = din("convw", [2176, 4])
    convb = din("convb", [128, NSLAB])
    dtb = din("dtb", [NH])
    alog = din("alog", [NH])
    dd = din("dd", [NH])
    nw = din("nw", [128, 16])
    init_in = din("init", [64, NH * HD])
    woutT = din("woutT", [DI, DM])
    wrout = din("wrout", [DM, 8])
    w1_in = din("w1", [8, DM, 4096])
    b1_in = din("b1", [8, 4096])
    w2_in = din("w2", [8, 4096, DM])
    b2_in = din("b2", [8, DM])
    c_ident = din("c_ident", [128, 128])
    c_trib = din("c_trib", [128, 128])
    c_totb = din("c_totb", [128, 128])
    c_i64 = din("c_i64", [128, DI])
    c_ones = din("c_ones", [1, 512])
    c_gmask = din("c_gmask", [NH, T])
    sel_in = din("sel", [64, 1])
    sel1m_in = din("sel1m", [64, 1])

    out_d = nc.dram_tensor("out", [T, DM], F32, kind="ExternalOutput").ap()

    zs_d = nc.dram_tensor("zs_d", [T, DI], BF16, kind="Internal").ap()
    yfm_d = nc.dram_tensor("yfm_d", [DI, T], BF16, kind="Internal").ap()
    g_d = nc.dram_tensor("g_d", [NH, T], F32, kind="Internal").ap()
    tot_d = nc.dram_tensor("tot_d", [1, NCH * NH], F32, kind="Internal").ap()
    a_d = nc.dram_tensor("a_d", [1, NH], F32, kind="Internal").ap()

    pb = PB()
    with ExitStack() as ctx:
        sb = lambda n, s, d: ctx.enter_context(nc.sbuf_tensor(n, s, d))
        xn = sb("xn", [128, KD * XN_W], BF16)
        arena1 = sb("arena1", [128, KD * XN_W], BF16)
        arena2 = sb("arena2", [128, 32768], BF16)
        arena3 = sb("arena3", [128, 10240], F32)
        dtraw = arena3[:, 0:512]
        scrA = arena3[:, 512:1024]
        scrB = arena3[:, 1024:1536]
        dt_tm = arena3[:, 1536:2048]
        cs_sb = arena3[:, 2048:2560]
        tot_sb = arena3[:, 2560:3072]
        sdo_sb = arena3[:, 3072:3584]
        ds_sb = arena3[:, 3584:4096]
        dtds = arena3[:, 4096:4608]
        la = arena3[0:64, 4608:5632]
        dtb_rep = sb("dtb_rep", [128, NH], F32)
        A_rep = sb("A_rep", [128, NH], F32)
        D_rep = sb("D_rep", [128, NH], F32)
        arow = sb("arow", [1, NH], F32)
        acolf = sb("acolf", [NH, 1], F32)
        dtbcol = sb("dtbcol", [NH, 1], F32)
        cw_sb = sb("cw_sb", [128, NSLAB * 4], F32)
        cb_sb = sb("cb_sb", [128, NSLAB], F32)
        pre0 = sb("pre0", [128, 515], BF16)
        pre1 = sb("pre1", [128, 515], BF16)
        acc0 = sb("acc0", [128, 512], F32)
        acc1 = arena3[:, 7680:8192]
        xs0 = sb("xs0", [128, 512], BF16)
        xs1 = sb("xs1", [128, 512], BF16)
        wsl0 = sb("wsl0", [128, 1024], BF16)
        wsl1 = sb("wsl1", [128, 1024], BF16)
        bc_fm = sb("bc_fm", [128, T], BF16)
        c_fm0 = sb("c_fm0", [64, T], BF16)
        bct = sb("bct", [128, 512], BF16)
        u_sb0 = sb("u_sb0", [128, DM], F32)
        u_sb1 = u_sb0
        sq_scr = sb("sq_scr", [128, DM], F32)
        stats = sb("stats", [128, 16], F32)
        id_bf = sb("id_bf", [128, 128], BF16)
        trib = sb("trib", [128, 128], F32)
        totb = sb("totb", [128, 128], F32)
        i64rep = sb("i64rep", [128, DI], BF16)
        diagD = sb("diagD", [128, DI], BF16)
        ones_bf = sb("ones_bf", [1, 512], BF16)
        sel_sb = sb("sel_sb", [64, 1], F32)
        sel1m_sb = sb("sel1m_sb", [64, 1], F32)
        w_dt = sb("w_dt", [128, KD * 32], BF16)
        w_r = sb("w_r", [128, KD * 8], BF16)
        logits = arena3[:, 1152:1280]
        probs = arena3[:, 1280:1408]
        cnt_sb = arena3[:, 1024:1152]
        tmpc = sb("tmpc", [128, 16], F32)
        wgt = arena3[:, 1408:1536]
        wgt_bf = sb("wgt_bf", [128, 128], BF16)
        a3b = arena3[:].bitcast(BF16)
        S0 = a3b[0:64, 11264:13312]
        S1 = a3b[0:64, 13312:15360]
        wgtf = a3b[0:8, 0:2048]
        b2_bf = a3b[0:8, 8192:9216]
        b1q = a3b[0:1, 9216:10240]
        hq0 = a3b[:, 4096:6144]
        hq1 = a3b[:, 6144:8192]
        nw_sb = sb("nw_sb", [128, 16], F32)
        PS = ctx.enter_context(nc.psum_tensor("PS", [128, 4096], F32))
        PSB = PS.bitcast(BF16)

        sems = {s: ctx.enter_context(nc.semaphore("sem_" + s)) for s in STREAMS}
        block = ctx.enter_context(nc.Block())

        xnp = arena1[:, 0:KD * XN_W]
        a1f = arena1[:].bitcast(F32)
        dtfm_a = a1f[:, 0:2048][0:NH, :]
        dtfm_b = a1f[:, 2048:4096][0:NH, :]
        gmask_sb = a1f[:, 4096:6144][0:NH, :]
        ma = a1f[:, 0:8192]
        xr = arena2[:, 0:8192]
        yr = arena2[:, 8192:16384]
        lt_b = arena2[:, 16384:18432]
        gl_b = arena2[:, 18432:20480]
        zb_b = arena2[:, 20480:22528]
        x2b = arena2[:, 22528:24576]
        xdb = arena2[:, 24576:26624]
        gr_b = arena2[:].bitcast(F32)[:, 13312:15360]
        yft = arena2[:, 30720:32768]
        w1q = (arena2[:, 0:8192], arena2[:, 8192:16384])
        w2q = (arena2[:, 16384:24576], arena2[:, 24576:32768])
        w_z = arena2[:, 0:16384]
        zs_b0 = arena2[:, 16384:18432]
        zs_b1 = arena2[:, 18432:20480]
        wout_bf = arena2[:, 0:16384]
        yft2 = arena2[:, 16384:18432]
        out_sb0 = arena2[:].bitcast(F32)[:, 9216:10240]
        out_sb1 = arena2[:].bitcast(F32)[:, 10240:11264]

        def dve(fn, r=(), w=(), tag="", sync=False):
            return pb.emit("dve", fn, r, w, tag=tag, sync=sync)

        def act(fn, r=(), w=()):
            return pb.emit("act", fn, r, w)

        def pool(fn, r=(), w=()):
            return pb.emit("pool", fn, r, w)

        def pe(fn, r=(), w=()):
            return pb.emit("pe", fn, r, w)

        def sdma(fn, r=(), w=(), tag=""):
            return pb.emit("sdma", fn, r, w, tag=tag)

        def gdma(fn, r=(), w=()):
            return pb.emit("gdma", fn, r, w)

        def psk(lo, hi):
            return [("PS", b) for b in range(lo // 512, (hi + 511) // 512)]

        def tap(name, ap, r, dt=None):
            if not DEBUG:
                return
            shape = list(ap.shape)
            t = nc.dram_tensor("dbg_" + name, shape, ap.dtype, kind="ExternalOutput").ap()
            sdma(lambda e, ap=ap, t=t: e.dma_start(out=t, in_=ap), r=r, tag="out")

        def tapd(name, dram_ap, r):
            if not DEBUG:
                return
            shape = list(dram_ap.shape)
            t = nc.dram_tensor("dbg_" + name, shape, dram_ap.dtype, kind="ExternalOutput").ap()
            sdma(lambda e, t=t, dram_ap=dram_ap: e.dma_start(out=t, in_=dram_ap), r=r, tag="out")

        gdma(lambda e: e.dma_start(out=id_bf[:], in_=c_ident), w=[("id",)])
        sdma(lambda e: e.dma_start(out=trib[:], in_=c_trib), w=[("trib",)])
        sdma(lambda e: e.dma_start(out=totb[:], in_=c_totb), w=[("totb",)])
        gdma(lambda e: e.dma_start(out=i64rep[:], in_=c_i64), w=[("i64",)])
        gdma(lambda e: e.dma_start(out=ones_bf[:], in_=c_ones), w=[("ones",)])
        sdma(lambda e: e.dma_start(out=sel_sb[:], in_=sel_in), w=[("sel",)])
        sdma(lambda e: e.dma_start(out=sel1m_sb[:], in_=sel1m_in), w=[("sel1m",)])
        sdma(lambda e: e.dma_start(out=dtb_rep[:], in_=dtb.unsqueeze(0).to_broadcast([128, NH])),
             w=[("dtbrep",)])
        sdma(lambda e: e.dma_start(out=dtbcol[:], in_=dtb.unsqueeze(1)), w=[("dtbcol",)])
        sdma(lambda e: e.dma_start(out=D_rep[:], in_=dd.unsqueeze(0).to_broadcast([128, NH])),
             w=[("drep",)])
        sdma(lambda e: e.dma_start(out=arow[:], in_=alog.unsqueeze(0)), w=[("arow",)])
        sdma(lambda e: e.dma_start(out=acolf[:], in_=alog.unsqueeze(1)), w=[("acolf",)])
        gdma(lambda e: e.dma_start(out=w_dt[:].rearrange("p (k h) -> p k h", h=32),
                                   in_=winT[:, 4224:4256].rearrange("(k p) h -> p k h", p=128)),
             w=[("wdt",)])
        gdma(lambda e: e.dma_start(out=w_r[:].rearrange("p (k h) -> p k h", h=8),
                                   in_=wrout.rearrange("(k p) h -> p k h", p=128)),
             w=[("wr",)])
        sdma(lambda e: e.dma_start(out=cw_sb[:].rearrange("p (s k) -> p s k", k=4),
                                   in_=convw.rearrange("(s p) k -> p s k", p=128)),
             w=[("cw",)])
        sdma(lambda e: e.dma_start(out=cb_sb[:], in_=convb), w=[("cb",)])
        sdma(lambda e: e.dma_start(out=nw_sb[:], in_=nw), w=[("nw",)])
        act(lambda e: e.activation(arow[:], arow[:], AF.Exp), r=[("arow",)], w=[("arow",)])
        dve(lambda e: e.tensor_scalar_mul(arow[:], arow[:], -1.0), r=[("arow",)], w=[("arow",)])
        sdma(lambda e: e.dma_start(out=a_d, in_=arow[:]), r=[("arow",)], w=[("a_d",)])
        sdma(lambda e: e.dma_start(out=A_rep[:], in_=a_d.to_broadcast([128, NH])),
             r=[("a_d",)], w=[("arep",)])
        act(lambda e: e.activation(acolf[:], acolf[:], AF.Exp), r=[("acolf",)], w=[("acolf",)])
        dve(lambda e: e.tensor_scalar_mul(acolf[:], acolf[:], -1.0), r=[("acolf",)], w=[("acolf",)])
        dve(lambda e: e.tensor_tensor(
            diagD[:].rearrange("p (h l) -> p h l", l=HD),
            i64rep[:].rearrange("p (h l) -> p h l", l=HD),
            D_rep[:].unsqueeze(2).to_broadcast([128, NH, HD]), ALU.mult),
            r=[("i64",), ("drep",)], w=[("diagD",)])

        def xn_col(k, t):
            return k * XN_W + 3 + t

        def layernorm(u_ap, xn_buf, xnkey):
            ubufs = (u_sb0, u_sb1)
            ukeys = (("u0",), ("u0",))
            for tt in range(NT):
                ub, uk = ubufs[tt % 2], ukeys[tt % 2]
                sdma(lambda e, tt=tt, ub=ub: e.dma_start(
                    out=ub[:], in_=u_ap[tt * 128:(tt + 1) * 128, :]), w=[uk])
                dve(lambda e, ub=ub: e.reduce_sum(stats[:, 0:1], ub[:], axis=AX.X, negate=True),
                    r=[uk], w=[("st0",)])
                dve(lambda e: e.tensor_scalar_mul(stats[:, 0:1], stats[:, 0:1], 1.0 / DM),
                    r=[("st0",)], w=[("st0",)], sync=True)
                dve(lambda e, ub=ub: e.tensor_scalar_add(ub[:], ub[:], stats[:, 0:1]),
                    r=[uk, ("st0",)], w=[uk], sync=True)
                act(lambda e, ub=ub: e.activation(sq_scr[:], ub[:], AF.Square,
                                                  accum_out=stats[:, 1:2]),
                    r=[uk], w=[("sq",), ("st1",)])
                dve(lambda e: e.tensor_scalar(stats[:, 2:3], stats[:, 1:2], 1.0 / DM, 1e-5,
                                              ALU.mult, ALU.add),
                    r=[("st1",)], w=[("st2",)])
                act(lambda e: e.activation(stats[:, 3:4], stats[:, 2:3], AF.Sqrt),
                    r=[("st2",)], w=[("st3",)])
                dve(lambda e: e.reciprocal(stats[:, 4:5], stats[:, 3:4]),
                    r=[("st3",)], w=[("st4",)])
                dve(lambda e, ub=ub: e.tensor_scalar_mul(
                    sq_scr[:].bitcast(BF16)[:, 0:1024], ub[:], stats[:, 4:5]),
                    r=[uk, ("st4",)], w=[("sq",)], sync=True)
                for kb in range(0, KD, 4):
                    bank = 6 + (kb // 4)
                    for k in range(kb, kb + 4):
                        pe(lambda e, k=k, kb=kb, bank=bank: e.transpose(
                            PSB[:, bank * 1024 + (k - kb) * 128: bank * 1024 + (k - kb + 1) * 128],
                            sq_scr[:].bitcast(BF16)[:, k * 128:(k + 1) * 128], id_bf[:]),
                           r=[("sq",), ("id",)], w=[("PS", bank)])
                    dve(lambda e, tt=tt, kb=kb, bank=bank, xn_buf=xn_buf: e.tensor_copy(
                        _mkv(xn_buf[:], [(XN_W, 4), (1, 128)], kb * XN_W + 3 + tt * 128),
                        PSB[:, bank * 1024: bank * 1024 + 512]),
                        r=[("PS", bank)], w=[xnkey])

        def softplus_tm():
            dtbv = _mkv(dtb_rep[:], [(0, NT), (1, NH)], 0)
            dve(lambda e: e.tensor_tensor(scrA[:].rearrange("p (t h) -> p t h", h=NH),
                                          dtraw[:].rearrange("p (t h) -> p t h", h=NH),
                                          dtbv, ALU.add),
                r=[("dtraw",), ("dtbrep",)], w=[("scrA",)])
            act(lambda e: e.activation(scrB[:], scrA[:], AF.Abs),
                r=[("scrA",)], w=[("scrB",)])
            act(lambda e: e.activation(scrB[:], scrB[:], AF.Exp, scale=-1.0),
                r=[("scrB",)], w=[("scrB",)])
            act(lambda e: e.activation(scrB[:], scrB[:], AF.Ln, bias=1.0),
                r=[("scrB",)], w=[("scrB",)])
            dve(lambda e: e.tensor_scalar_max(scrA[:], scrA[:], 0.0),
                r=[("scrA",)], w=[("scrA",)])
            dve(lambda e: e.tensor_add(dt_tm[:], scrA[:], scrB[:]),
                r=[("scrA",), ("scrB",)], w=[("dt_tm",)])

        def dt_router_pass(xn_buf, xnkey, main):
            for tt in range(NT):
                for k in range(KD):
                    pe(lambda e, k=k, tt=tt, xn_buf=xn_buf: e.matmul(
                        PS[:, tt * 32:(tt + 1) * 32],
                        _mkv(xn_buf[:], [(1, 128)], xn_col(k, tt * 128)),
                        w_dt[:, k * 32:(k + 1) * 32],
                        start=(k == 0 and tt == 0), stop=(k == KD - 1)),
                       r=[xnkey, ("wdt",)], w=[("PS", 0)])
                    if main:
                        pe(lambda e, k=k, tt=tt, xn_buf=xn_buf: e.matmul(
                            PS[:, 1024 + tt * 8: 1024 + (tt + 1) * 8],
                            _mkv(xn_buf[:], [(1, 128)], xn_col(k, tt * 128)),
                            w_r[:, k * 8:(k + 1) * 8],
                            start=(k == 0 and tt == 0), stop=(k == KD - 1)),
                           r=[xnkey, ("wr",)], w=[("PS", 2)])
            dve(lambda e: e.tensor_copy(dtraw[:], PS[:, 0:512]), r=[("PS", 0)], w=[("dtraw",)])
            softplus_tm()
            arepv = _mkv(A_rep[:], [(0, NT), (1, NH)], 0)
            dve(lambda e: e.tensor_tensor(scrA[:].rearrange("p (t h) -> p t h", h=NH),
                                          dt_tm[:].rearrange("p (t h) -> p t h", h=NH),
                                          arepv, ALU.mult),
                r=[("dt_tm",), ("arep",)], w=[("scrA",)])
            for j in range(NT):
                pe(lambda e, j=j: e.matmul(PS[:, j * 32:(j + 1) * 32], trib[:],
                                           scrA[:, j * 32:(j + 1) * 32],
                                           start=(j == 0), stop=True),
                   r=[("trib",), ("scrA",)], w=[("PS", 0)])
                pe(lambda e, j=j: e.matmul(PS[:, 512 + j * 32: 512 + (j + 1) * 32], totb[:],
                                           scrA[:, j * 32:(j + 1) * 32],
                                           start=(j == 0), stop=True),
                   r=[("totb",), ("scrA",)], w=[("PS", 1)])
            dve(lambda e: e.tensor_copy(cs_sb[:], PS[:, 0:512]), r=[("PS", 0)], w=[("cs",)])
            dve(lambda e: e.tensor_copy(tot_sb[:], PS[:, 512:1024]), r=[("PS", 1)], w=[("tot",)])
            if main:
                act(lambda e: e.activation(sdo_sb[:], cs_sb[:], AF.Exp), r=[("cs",)], w=[("sdo",)])
            pool(lambda e: e.tensor_sub(ds_sb[:], tot_sb[:], cs_sb[:]),
                 r=[("tot",), ("cs",)], w=[("ds",)])
            act(lambda e: e.activation(ds_sb[:], ds_sb[:], AF.Exp), r=[("ds",)], w=[("ds",)])
            pool(lambda e: e.tensor_mul(dtds[:], dt_tm[:], ds_sb[:]),
                 r=[("dt_tm",), ("ds",)], w=[("dtds",)])
            sdma(lambda e: e.dma_start(
                out=bass.AP(tot_d.tensor, tot_d.offset, [[1, 1], [64, NT], [1, NH]]),
                in_=tot_sb[0:1, :].rearrange("o (j h) -> o j h", h=NH)),
                r=[("tot",)], w=[("tot_d",)])
            sdma(lambda e: e.dma_start(
                out=bass.AP(tot_d.tensor, tot_d.offset + NH, [[1, 1], [64, NT], [1, NH]]),
                in_=tot_sb[64:65, :].rearrange("o (j h) -> o j h", h=NH)),
                r=[("tot",)], w=[("tot_d",)])
            sdma(lambda e: e.dma_start(out=la[:], in_=tot_d.to_broadcast([64, NCH * NH])),
                 r=[("tot_d",)], w=[("la",)])
            act(lambda e: e.activation(la[:], la[:], AF.Exp), r=[("la",)], w=[("la",)])

        def router_softmax_wgt():
            dve(lambda e: e.tensor_copy(logits[:], PS[:, 1024:1152]),
                r=[("PS", 2)], w=[("logits",), ("scrB",)])
            for tt in range(NT):
                dve(lambda e, tt=tt: e.reduce_max(stats[:, 5:6], logits[:, tt * 8:(tt + 1) * 8],
                                                  axis=AX.X),
                    r=[("logits",)], w=[("st5",)])
                dve(lambda e, tt=tt: e.tensor_scalar_sub(probs[:, tt * 8:(tt + 1) * 8],
                                                         logits[:, tt * 8:(tt + 1) * 8],
                                                         stats[:, 5:6]),
                    r=[("logits",), ("st5",)], w=[("probs",)], sync=True)
            act(lambda e: e.activation(probs[:], probs[:], AF.Exp), r=[("probs",)], w=[("probs",)])
            for tt in range(NT):
                dve(lambda e, tt=tt: e.reduce_sum(stats[:, 6:7], probs[:, tt * 8:(tt + 1) * 8],
                                                  axis=AX.X),
                    r=[("probs",)], w=[("st6",)])
                dve(lambda e: e.reciprocal(stats[:, 7:8], stats[:, 6:7]), r=[("st6",)], w=[("st7",)],
                    sync=True)
                dve(lambda e, tt=tt: e.tensor_scalar_mul(probs[:, tt * 8:(tt + 1) * 8],
                                                         probs[:, tt * 8:(tt + 1) * 8],
                                                         stats[:, 7:8]),
                    r=[("probs",), ("st7",)], w=[("probs",)], sync=True)
            dve(lambda e: e.memset(cnt_sb[:], 0.0), w=[("cnt",), ("scrB",)])
            for ee in range(8):
                for e2 in range(8):
                    if e2 == ee:
                        continue
                    op = ALU.is_ge if e2 < ee else ALU.is_gt
                    dve(lambda e, ee=ee, e2=e2, op=op: e.tensor_tensor(
                        tmpc[:], _mkv(probs[:], [(8, NT)], e2),
                        _mkv(probs[:], [(8, NT)], ee), op),
                        r=[("probs",)], w=[("tmpc",)])
                    pool(lambda e, ee=ee: e.tensor_add(
                        _mkv(cnt_sb[:], [(8, NT)], ee),
                        _mkv(cnt_sb[:], [(8, NT)], ee), tmpc[:]),
                        r=[("cnt",), ("tmpc",)], w=[("cnt",)])
            dve(lambda e: e.tensor_single_scalar(wgt[:], cnt_sb[:], 1.5, ALU.is_lt),
                r=[("cnt",)], w=[("wgt",)])
            dve(lambda e: e.tensor_mul(wgt[:], wgt[:], probs[:]),
                r=[("wgt",), ("probs",)], w=[("wgt",)])
            dve(lambda e: e.tensor_copy(wgt_bf[:], wgt[:]), r=[("wgt",)], w=[("wgtbf",)])
            for tt in range(NT):
                bank = 6 + (tt % 2)
                pe(lambda e, tt=tt, bank=bank: e.transpose(
                    PSB[0:8, bank * 1024: bank * 1024 + 128],
                    wgt_bf[:, tt * 8:(tt + 1) * 8], id_bf[:]),
                   r=[("wgtbf",), ("id",)], w=[("PS", bank)])
                dve(lambda e, tt=tt, bank=bank: e.tensor_copy(
                    wgtf[:, tt * 128:(tt + 1) * 128],
                    PSB[0:8, bank * 1024: bank * 1024 + 128]),
                    r=[("PS", bank)], w=[("wgtf",), ("dtraw",), ("scrA",)])

        def dtfm_g_pass(xn_buf, xnkey):
            sdma(lambda e: e.dma_start(out=gmask_sb[:], in_=c_gmask),
                 w=[("gmask",), ("xnp",)])
            for k in range(KD):
                for n in range(4):
                    pe(lambda e, k=k, n=n, xn_buf=xn_buf: e.matmul(
                        PS[0:NH, n * 512:(n + 1) * 512],
                        w_dt[:, k * 32:(k + 1) * 32],
                        _mkv(xn_buf[:], [(1, 512)], xn_col(k, n * 512)),
                        start=(k == 0), stop=(k == KD - 1)),
                       r=[xnkey, ("wdt",)], w=[("PS", n)])
            dve(lambda e: e.tensor_copy(dtfm_a[:], PS[0:NH, 0:2048]),
                r=psk(0, 2048), w=[("dtfma",), ("xnp",)])
            tap("dtfmraw", dtfm_a[:], [("dtfma",)])
            dve(lambda e: e.tensor_scalar_add(dtfm_a[:], dtfm_a[:], dtbcol[:, 0:1]),
                r=[("dtfma",), ("dtbcol",)], w=[("dtfma",)])
            tap("dtfm1", dtfm_a[:], [("dtfma",)])
            act(lambda e: e.activation(dtfm_b[:], dtfm_a[:], AF.Abs),
                r=[("dtfma",)], w=[("dtfmb",), ("xnp",)])
            act(lambda e: e.activation(dtfm_b[:], dtfm_b[:], AF.Exp, scale=-1.0),
                r=[("dtfmb",)], w=[("dtfmb",)])
            act(lambda e: e.activation(dtfm_b[:], dtfm_b[:], AF.Ln, bias=1.0),
                r=[("dtfmb",)], w=[("dtfmb",)])
            tap("dtfm2", dtfm_b[:], [("dtfmb",)])
            dve(lambda e: e.tensor_scalar_max(dtfm_a[:], dtfm_a[:], 0.0),
                r=[("dtfma",)], w=[("dtfma",)])
            dve(lambda e: e.tensor_add(dtfm_a[:], dtfm_a[:], dtfm_b[:]),
                r=[("dtfma",), ("dtfmb",)], w=[("dtfma",)])
            tap("dtfm3", dtfm_a[:], [("dtfma",)])
            dve(lambda e: e.tensor_scalar_mul(dtfm_a[:], dtfm_a[:], acolf[:, 0:1]),
                r=[("dtfma",), ("acolf",)], w=[("dtfma",)])
            tap("dtfm4", dtfm_a[:], [("dtfma",)])
            act(lambda e: e.activation(dtfm_a[:], dtfm_a[:], AF.Exp),
                r=[("dtfma",)], w=[("dtfma",)])
            tap("dtfm5", dtfm_a[:], [("dtfma",)])
            dve(lambda e: e.tensor_mul(dtfm_a[:], dtfm_a[:], gmask_sb[:]),
                r=[("dtfma",), ("gmask",)], w=[("dtfma",)])
            tap("gmask", gmask_sb[:], [("gmask",)])
            tap("dtfm6", dtfm_a[:], [("dtfma",)])
            sdma(lambda e: e.dma_start(out=g_d, in_=dtfm_a[:]), r=[("dtfma",)], w=[("g_d",)])

        def z_pass(xn_buf, xnkey):
            gdma(lambda e: e.dma_start(
                out=w_z[:].rearrange("p (k z) -> p k z", z=2048),
                in_=winT[:, 0:DI].rearrange("(k p) z -> p k z", p=128)),
                w=[("w_z",)])
            zbufs = (zs_b0, zs_b1)
            zkeys = (("zs0",), ("zs1",))
            for tt in range(NT):
                par = tt % 2
                pscol = par * 2048
                pkeys = psk(pscol, pscol + 2048)
                for k in range(KD):
                    for n in range(4):
                        pe(lambda e, k=k, tt=tt, n=n, pscol=pscol, xn_buf=xn_buf: e.matmul(
                            PS[:, pscol + n * 512: pscol + (n + 1) * 512],
                            _mkv(xn_buf[:], [(1, 128)], xn_col(k, tt * 128)),
                            w_z[:, k * 2048 + n * 512: k * 2048 + (n + 1) * 512],
                            start=(k == 0), stop=(k == KD - 1)),
                           r=[xnkey, ("w_z",)], w=[pkeys[n]])
                zbuf, zkey = zbufs[par], zkeys[par]
                act(lambda e, pscol=pscol, zbuf=zbuf: e.activation(
                    zbuf[:], PS[:, pscol:pscol + 2048], AF.Silu),
                    r=pkeys, w=[zkey])
                sdma(lambda e, tt=tt, zbuf=zbuf: e.dma_start(
                    out=zs_d[tt * 128:(tt + 1) * 128, :], in_=zbuf[:]),
                    r=[zkey], w=[("zs_d", tt)])

        def half_pipeline(u_ap, xn_buf, xnkey, main):
            layernorm(u_ap, xn_buf, xnkey)
            if not main:
                for k in range(KD):
                    dve(lambda e, k=k, xn_buf=xn_buf: e.memset(
                        _mkv(xn_buf[:], [(1, 3)], k * XN_W), 0.0),
                        w=[xnkey])
            else:
                dve(lambda e: e.tensor_copy(_mkv(xn_buf[:], [(XN_W, KD), (1, 3)], 0),
                                            _mkv(xnp[:], [(XN_W, KD), (1, 3)], XN_W - 3)),
                    r=[("xnp",)], w=[xnkey])
            dt_router_pass(xn_buf, xnkey, main)
            if main:
                tap("xn", xn_buf[:], [xnkey])
                tap("dtraw", dtraw[:], [("dtraw",)])
                tap("dt_tm", dt_tm[:], [("dt_tm",)])
                tap("cs", cs_sb[:], [("cs",)])
                tap("tot", tot_sb[:], [("tot",)])
                tap("sdo", sdo_sb[:], [("sdo",)])
                tap("ds", ds_sb[:], [("ds",)])
                tap("dtds", dtds[:], [("dtds",)])
                tap("la", la[:], [("la",)])
                router_softmax_wgt()
                tap("logits", logits[:], [("logits",)])
                tap("probs", probs[:], [("probs",)])
                tap("wgt", wgt[:], [("wgt",)])
                tap("wgtf", wgtf[:], [("wgtf",)])
                dtfm_g_pass(xn_buf, xnkey)
                tapd("g_d", g_d, [("g_d",)])
                z_pass(xn_buf, xnkey)
                tapd("zs_d", zs_d, [("zs_d", t) for t in range(NT)])
                pb.barrier()
            if not main:
                gdma(lambda e: e.dma_start(out=S0[:], in_=init_in), w=[("S", 0)])
            else:
                gdma(lambda e: e.dma_start(out=S1[:], in_=init_in), w=[("S", 1)])
                dve(lambda e: e.tensor_scalar_mul(S1[:], S1[:], sel1m_sb[:, 0:1]),
                    r=[("S", 1), ("sel1m",)], w=[("S", 1)])
                dve(lambda e: e.scalar_tensor_tensor(S0[:], S0[:], sel_sb[:, 0:1], S1[:],
                                                     ALU.mult, ALU.add),
                    r=[("S", 0), ("sel",), ("S", 1)], w=[("S", 0)])
                tap("Sinit", S0[:], [("S", 0)])

            wsls = (wsl0, wsl1)
            pres = (pre0, pre1)
            accs = (acc0, acc1)
            xss = (xs0, xs1)
            for tb in range(NTB):
                for slab in range(NSLAB):
                    sp = slab % 2
                    wk = ("wsl", sp)
                    pk = ("pre", sp)
                    ak = ("acc", sp)
                    xk2 = ("xs", sp)
                    gdma(lambda e, slab=slab, sp=sp: e.dma_start(
                        out=wsls[sp][:].rearrange("p (k c) -> p k c", c=128),
                        in_=winT[:, 2048 + slab * 128: 2048 + (slab + 1) * 128]
                        .rearrange("(k p) c -> p k c", p=128)),
                        w=[wk])
                    for k in range(KD):
                        pe(lambda e, k=k, tb=tb, sp=sp, xn_buf=xn_buf: e.matmul(
                            PS[:, 512:515],
                            wsls[sp][:, k * 128:(k + 1) * 128],
                            _mkv(xn_buf[:], [(1, 3)], k * XN_W + tb * 512),
                            start=(k == 0), stop=(k == KD - 1)),
                           r=[wk, xnkey], w=[("PS", 1)])
                        pe(lambda e, k=k, tb=tb, sp=sp, xn_buf=xn_buf: e.matmul(
                            PS[:, 0:512],
                            wsls[sp][:, k * 128:(k + 1) * 128],
                            _mkv(xn_buf[:], [(1, 512)], xn_col(k, tb * 512)),
                            start=(k == 0), stop=(k == KD - 1)),
                           r=[wk, xnkey], w=[("PS", 0)])
                    dve(lambda e, sp=sp: e.tensor_copy(pres[sp][:, 0:3], PS[:, 512:515]),
                        r=[("PS", 1)], w=[pk])
                    dve(lambda e, sp=sp: e.tensor_copy(pres[sp][:, 3:515], PS[:, 0:512]),
                        r=[("PS", 0)], w=[pk])
                    dve(lambda e, sp=sp, slab=slab: e.tensor_scalar(
                        accs[sp][:], pres[sp][:, 0:512], cw_sb[:, slab * 4: slab * 4 + 1],
                        None, ALU.mult),
                        r=[pk, ("cw",)], w=[ak])
                    for kk in range(1, 4):
                        dve(lambda e, sp=sp, slab=slab, kk=kk: e.scalar_tensor_tensor(
                            accs[sp][:], pres[sp][:, kk:kk + 512],
                            cw_sb[:, slab * 4 + kk: slab * 4 + kk + 1],
                            accs[sp][:], ALU.mult, ALU.add),
                            r=[pk, ("cw",), ak], w=[ak])
                    bank = 6 + sp
                    if slab < 16:
                        act(lambda e, sp=sp, slab=slab: e.activation(
                            xss[sp][:], accs[sp][:], AF.Silu, bias=cb_sb[:, slab:slab + 1]),
                            r=[ak, ("cb",)], w=[xk2])
                        for q in range(4):
                            pe(lambda e, sp=sp, q=q, bank=bank: e.transpose(
                                PSB[:, bank * 1024 + q * 128: bank * 1024 + (q + 1) * 128],
                                xss[sp][:, q * 128:(q + 1) * 128], id_bf[:]),
                               r=[xk2, ("id",)], w=[("PS", bank)])
                        dve(lambda e, slab=slab, bank=bank: e.tensor_copy(
                            _mkv(xr[:], [(2048, 4), (1, 128)], slab * 128),
                            PSB[:, bank * 1024: bank * 1024 + 512]),
                            r=[("PS", bank)], w=[("xr",)])
                    else:
                        act(lambda e, sp=sp, slab=slab, tb=tb: e.activation(
                            bc_fm[:, tb * 512:(tb + 1) * 512], accs[sp][:], AF.Silu,
                            bias=cb_sb[:, slab:slab + 1]),
                            r=[ak, ("cb",)], w=[("bcfm",)])
                        for q in range(4):
                            pe(lambda e, tb=tb, q=q, bank=bank: e.transpose(
                                PSB[:, bank * 1024 + q * 128: bank * 1024 + (q + 1) * 128],
                                bc_fm[:, tb * 512 + q * 128: tb * 512 + (q + 1) * 128],
                                id_bf[:]),
                               r=[("bcfm",), ("id",)], w=[("PS", bank)])
                        dve(lambda e, bank=bank: e.tensor_copy(
                            bct[:], PSB[:, bank * 1024: bank * 1024 + 512]),
                            r=[("PS", bank)], w=[("bct",)])
                        sdma(lambda e, tb=tb: e.dma_start(
                            out=c_fm0[:, tb * 512:(tb + 1) * 512],
                            in_=bc_fm[64:128, tb * 512:(tb + 1) * 512]),
                            r=[("bcfm",)], w=[("cfm0",)])

                if main and tb == 0:
                    tap("xr0", xr[:], [("xr",)])
                    tap("bct0", bct[:], [("bct",)])
                    tap("bcfm", bc_fm[:], [("bcfm",)])
                    tap("cfm0", c_fm0[:], [("cfm0",)])
                for j2 in range(8):
                    c = tb * 8 + j2
                    par = c % 2
                    base = par * 64
                    tile = c // 2
                    slot = tile % 4
                    kSc = ("S", c % 2)
                    kSn = ("S", (c + 1) % 2)
                    if main:
                        sdma(lambda e, c=c, base=base: e.dma_start(
                            out=zb_b[base:base + 64, :], in_=zs_d[c * 64:(c + 1) * 64, :]),
                            r=[("zs_d", c // 2)], w=[("zb", par)])
                        sdma(lambda e, c=c, base=base: e.dma_start(
                            out=gr_b[base:base + 64, :].rearrange("p (h l) -> p h l", l=64),
                            in_=bass.AP(g_d.tensor, g_d.offset + c * 64,
                                        [[0, 64], [T, NH], [1, 64]])),
                            r=[("g_d",)], w=[("gr", par)])
                        pool(lambda e, base=base, slot=slot, tile=tile: e.tensor_tensor(
                            x2b[base:base + 64, :].rearrange("p (h q) -> p h q", q=HD),
                            _mkv(xr[base:base + 64, :], [(64, NH), (1, HD)], slot * 2048),
                            _mkv(dt_tm[base:base + 64, :], [(1, NH), (0, HD)], tile * 32),
                            ALU.mult),
                            r=[("xr",), ("dt_tm",)], w=[("x2", par)])
                    pool(lambda e, base=base, slot=slot, tile=tile: e.tensor_tensor(
                        xdb[base:base + 64, :].rearrange("p (h q) -> p h q", q=HD),
                        _mkv(xr[base:base + 64, :], [(64, NH), (1, HD)], slot * 2048),
                        _mkv(dtds[base:base + 64, :], [(1, NH), (0, HD)], tile * 32),
                        ALU.mult),
                        r=[("xr",), ("dtds",)], w=[("xd", par)])
                    for n in range(4):
                        pe(lambda e, base=base, slot=slot, n=n: e.matmul(
                            PS[0:64, 2048 + n * 512: 2048 + (n + 1) * 512],
                            bct[base:base + 64, slot * 128: slot * 128 + 64],
                            xdb[base:base + 64, n * 512:(n + 1) * 512],
                            start=True, stop=True, tile_position=(base, 0)),
                           r=[("bct",), ("xd", par)], w=[("PS", 4 + n)])
                    if main:
                        pe(lambda e, c=c, base=base: e.matmul(
                            PS[base:base + 64, 0:64],
                            bc_fm[0:64, c * 64:(c + 1) * 64],
                            c_fm0[:, c * 64:(c + 1) * 64],
                            start=True, stop=True, tile_position=(0, base)),
                           r=[("bcfm",), ("cfm0",)], w=[("PS", 0)])
                        dve(lambda e, base=base: e.tensor_tensor_scan(
                            lt_b[base:base + 64, :], gr_b[base:base + 64, :],
                            i64rep[base:base + 64, :], 0.0, ALU.mult, ALU.add),
                            r=[("gr", par), ("i64",)], w=[("lt", par)])
                        dve(lambda e, base=base: e.tensor_tensor(
                            gl_b[base:base + 64, :].rearrange("p (h l) -> p h l", l=64),
                            lt_b[base:base + 64, :].rearrange("p (h l) -> p h l", l=64),
                            _mkv(PS[base:base + 64, :], [(0, NH), (1, 64)], 0),
                            ALU.mult),
                            r=[("lt", par), ("PS", 0)], w=[("gl", par)])
                        for h in range(NH):
                            pe(lambda e, c=c, base=base, h=h: e.matmul(
                                PS[base:base + 64, h * 64:(h + 1) * 64],
                                c_fm0[:, c * 64:(c + 1) * 64],
                                (S0, S1)[c % 2][:, h * 64:(h + 1) * 64],
                                start=(h % 8 == 0), stop=False, tile_position=(0, base)),
                               r=[("cfm0",), kSc], w=[("PS", h // 8)])
                        dve(lambda e, base=base, tile=tile: e.tensor_tensor(
                            PS[base:base + 64, 0:2048].rearrange("p (h q) -> p h q", q=HD),
                            PS[base:base + 64, 0:2048].rearrange("p (h q) -> p h q", q=HD),
                            _mkv(sdo_sb[base:base + 64, :], [(1, NH), (0, HD)], tile * 32),
                            ALU.mult),
                            r=psk(0, 2048) + [("sdo",)], w=psk(0, 2048))
                        for h in range(NH):
                            pe(lambda e, base=base, h=h: e.matmul(
                                PS[base:base + 64, h * 64:(h + 1) * 64],
                                gl_b[base:base + 64, h * 64:(h + 1) * 64],
                                x2b[base:base + 64, h * 64:(h + 1) * 64],
                                start=False, stop=False, tile_position=(base, base)),
                               r=[("gl", par), ("x2", par)], w=[("PS", h // 8)])
                        for h in range(NH):
                            pe(lambda e, base=base, h=h, slot=slot: e.matmul(
                                PS[base:base + 64, h * 64:(h + 1) * 64],
                                diagD[base:base + 64, h * 64:(h + 1) * 64],
                                _mkv(xr[base:base + 64, :], [(1, 64)], slot * 2048 + h * 64),
                                start=False, stop=True, tile_position=(base, base)),
                               r=[("diagD",), ("xr",)], w=[("PS", h // 8)])
                        dve(lambda e, base=base, slot=slot: e.tensor_tensor(
                            _mkv(yr[base:base + 64, :], [(1, 2048)], slot * 2048),
                            PS[base:base + 64, 0:2048], zb_b[base:base + 64, :],
                            ALU.mult),
                            r=psk(0, 2048) + [("zb", par)], w=[("yr", slot)])
                    pool(lambda e, c=c: e.tensor_tensor(
                        (S0, S1)[(c + 1) % 2][:].rearrange("p (h q) -> p h q", q=HD),
                        (S0, S1)[c % 2][:].rearrange("p (h q) -> p h q", q=HD),
                        _mkv(la[:], [(1, NH), (0, HD)], c * 32),
                        ALU.mult),
                        r=[kSc, ("la",)], w=[kSn])
                    dve(lambda e, c=c: e.tensor_add(
                        (S0, S1)[(c + 1) % 2][:], (S0, S1)[(c + 1) % 2][:],
                        PS[0:64, 2048:4096]),
                        r=[kSn] + psk(2048, 4096), w=[kSn])

                if main and tb == 0:
                    tap("yr0", yr[:], [("yr", 0), ("yr", 1), ("yr", 2), ("yr", 3)])
                    tap("S8", S0[:], [("S", 0)])
                    tap("lt7", lt_b[:], [("lt", 0), ("lt", 1)])
                    tap("gl7", gl_b[:], [("gl", 0), ("gl", 1)])
                    tap("x2b7", x2b[:], [("x2", 0), ("x2", 1)])
                    tap("xdb7", xdb[:], [("xd", 0), ("xd", 1)])
                if main:
                    for q in range(4):
                        tt = tb * 4 + q
                        yk = ("yr", q)
                        act(lambda e, q=q: e.activation(
                            sq_scr[:].bitcast(BF16)[:, 0:2048],
                            yr[:, q * 2048:(q + 1) * 2048], AF.Square,
                            accum_out=stats[:, 8:9]),
                            r=[yk], w=[("sq",), ("st8",)])
                        dve(lambda e: e.tensor_scalar(stats[:, 9:10], stats[:, 8:9],
                                                      1.0 / DI, 1e-5, ALU.mult, ALU.add),
                            r=[("st8",)], w=[("st9",)])
                        act(lambda e: e.activation(stats[:, 10:11], stats[:, 9:10], AF.Sqrt),
                            r=[("st9",)], w=[("st10",)])
                        dve(lambda e: e.reciprocal(stats[:, 11:12], stats[:, 10:11]),
                            r=[("st10",)], w=[("st11",)])
                        dve(lambda e, q=q: e.tensor_scalar_mul(
                            yr[:, q * 2048:(q + 1) * 2048],
                            yr[:, q * 2048:(q + 1) * 2048], stats[:, 11:12]),
                            r=[yk, ("st11",)], w=[yk], sync=True)
                        for kb in range(0, 16, 4):
                            bank = 6 + (kb // 4) % 2
                            for k in range(kb, kb + 4):
                                pe(lambda e, q=q, k=k, kb=kb, bank=bank: e.transpose(
                                    PSB[:, bank * 1024 + (k - kb) * 128:
                                        bank * 1024 + (k - kb + 1) * 128],
                                    yr[:, q * 2048 + k * 128: q * 2048 + (k + 1) * 128],
                                    id_bf[:]),
                                   r=[yk, ("id",)], w=[("PS", bank)])
                            dve(lambda e, kb=kb, bank=bank: e.tensor_copy(
                                yft[:, kb * 128:(kb + 4) * 128],
                                PSB[:, bank * 1024: bank * 1024 + 512]),
                                r=[("PS", bank)], w=[("yft",)])
                        sdma(lambda e, tt=tt: e.dma_start(
                            out=bass.AP(yfm_d.tensor, yfm_d.offset + tt * 128,
                                        [[T, 128], [128 * T, 16], [1, 128]]),
                            in_=yft[:].rearrange("p (k c) -> p k c", c=128)),
                            r=[("yft",)], w=[("yfm", tt)])

        half_pipeline(up_in, xnp, ("xnp",), main=False)
        pb.barrier()
        half_pipeline(u_in, xn, ("xn",), main=True)
        tapd("yfm_d", yfm_d, [("yfm", t) for t in range(NT)])
        pb.barrier()

        gdma(lambda e: e.dma_start(out=b2_bf[:], in_=b2_in), w=[("b2",)])
        for th in range(2):
            for q in range(8):
                tt = th * 8 + q
                pcol = 1024 + (q % 2) * 1024
                pkeys = psk(pcol, pcol + 1024)
                for n in range(2):
                    pe(lambda e, tt=tt, n=n, pcol=pcol: e.matmul(
                        PS[:, pcol + n * 512: pcol + (n + 1) * 512],
                        wgtf[:, tt * 128:(tt + 1) * 128],
                        b2_bf[:, n * 512:(n + 1) * 512],
                        start=True, stop=True),
                       r=[("wgtf",), ("b2",)], w=[pkeys[n]])
                dve(lambda e, q=q, pcol=pcol: e.tensor_copy(
                    ma[:, q * DM:(q + 1) * DM], PS[:, pcol:pcol + 1024]),
                    r=pkeys, w=[("ma", q)])
                if DEBUG and th == 0 and q == 0:
                    tap("mab2", ma[:, 0:1024], [("ma", 0)])
            for ee in range(8):
                for qq in range(4):
                    wp = qq % 2
                    k1, k2 = ("w1q", wp), ("w2q", wp)
                    gdma(lambda e, ee=ee, qq=qq, wp=wp: e.dma_start(
                        out=w1q[wp][:].rearrange("p (k j) -> p k j", j=1024),
                        in_=bass.AP(w1_in.tensor,
                                    w1_in.offset + ee * DM * 4096 + qq * 1024,
                                    [[4096, 128], [128 * 4096, 8], [1, 1024]])),
                        w=[k1])
                    gdma(lambda e, ee=ee, qq=qq, wp=wp: e.dma_start(
                        out=w2q[wp][:].rearrange("p (k j) -> p k j", j=1024),
                        in_=bass.AP(w2_in.tensor,
                                    w2_in.offset + ee * 4096 * DM + qq * 1024 * DM,
                                    [[DM, 128], [128 * DM, 8], [1, 1024]])),
                        w=[k2])
                    gdma(lambda e, ee=ee, qq=qq: e.dma_start(
                        out=b1q[:], in_=b1_in[ee:ee + 1, qq * 1024:(qq + 1) * 1024]),
                        w=[("b1",)])
                    for t2 in range(4):
                        tok0 = th * 1024 + t2 * 256
                        hp = t2 % 2
                        hqb = (hq0, hq1)[hp]
                        hk = ("hq", hp)
                        for wv in range(4):
                            pw = wv % 2
                            pcol = pw * 512
                            for mj in (wv * 2, wv * 2 + 1):
                                col = pcol + (mj - wv * 2) * 256
                                for k in range(KD):
                                    pe(lambda e, k=k, mj=mj, col=col, tok0=tok0, wp=wp: e.matmul(
                                        PS[:, col:col + 256],
                                        w1q[wp][:, k * 1024 + mj * 128: k * 1024 + (mj + 1) * 128],
                                        _mkv(xn[:], [(1, 256)], xn_col(k, tok0)),
                                        start=(k == 0 and mj % 2 == 0), stop=False),
                                       r=[k1, ("xn",)], w=[("PS", pw)])
                                pe(lambda e, mj=mj, col=col: e.matmul(
                                    PS[:, col:col + 256],
                                    b1q[0:1, mj * 128:(mj + 1) * 128],
                                    ones_bf[0:1, 0:256],
                                    start=False, stop=(mj % 2 == 1)),
                                   r=[("b1",), ("ones",)], w=[("PS", pw)])
                            act(lambda e, wv=wv, pcol=pcol, hqb=hqb, pw=pw: e.activation(
                                hqb[:, wv * 512:(wv + 1) * 512], PS[:, pcol:pcol + 512],
                                AF.Gelu_apprx_tanh),
                                r=[("PS", pw)], w=[hk])
                        if DEBUG and ee == 0 and qq == 0 and t2 == 0 and th == 0:
                            tap("hq00", hqb[:], [hk])
                        for tsub in range(2):
                            ocol = 2048 + tsub * 1024
                            okeys = psk(ocol, ocol + 1024)
                            for mj in range(8):
                                for n in range(2):
                                    pe(lambda e, mj=mj, n=n, tsub=tsub, ocol=ocol, wp=wp,
                                       qq=qq, hqb=hqb: e.matmul(
                                        PS[:, ocol + n * 512: ocol + (n + 1) * 512],
                                        hqb[:, mj * 256 + tsub * 128: mj * 256 + (tsub + 1) * 128],
                                        w2q[wp][:, mj * 1024 + n * 512: mj * 1024 + (n + 1) * 512],
                                        start=(mj == 0), stop=(mj == 7)),
                                       r=[hk, k2], w=[okeys[n]])
                            tt = th * 8 + t2 * 2 + tsub
                            qm = t2 * 2 + tsub
                            dve(lambda e, tt=tt, qm=qm, ocol=ocol, ee=ee: e.scalar_tensor_tensor(
                                ma[:, qm * DM:(qm + 1) * DM],
                                PS[:, ocol:ocol + 1024],
                                wgt[:, tt * 8 + ee: tt * 8 + ee + 1],
                                ma[:, qm * DM:(qm + 1) * DM],
                                ALU.mult, ALU.add),
                                r=okeys + [("wgt",), ("ma", qm)], w=[("ma", qm)])
                            if DEBUG and th == 0 and ee == 0 and t2 == 0 and tsub == 0 and qq == 3:
                                tap("mae0", ma[:, 0:1024], [("ma", 0)])
            if DEBUG and th == 0:
                tap("ma0", ma[:], [("ma", q) for q in range(8)])
            pb.barrier()
            gdma(lambda e: e.dma_start(
                out=wout_bf[:].rearrange("p (k c) -> p k c", c=1024),
                in_=woutT.rearrange("(k p) c -> p k c", p=128)),
                w=[("woutbf",)])
            for k in range(16):
                dve(lambda e, k=k: e.tensor_scalar_mul(
                    wout_bf[:, k * 1024:(k + 1) * 1024],
                    wout_bf[:, k * 1024:(k + 1) * 1024], nw_sb[:, k:k + 1]),
                    r=[("woutbf",), ("nw",)], w=[("woutbf",)])
            for q in range(8):
                tt = th * 8 + q
                par = q % 2
                pcol = par * 1024
                pkeys = psk(pcol, pcol + 1024)
                ub = (u_sb0, u_sb1)[par]
                ob = (out_sb0, out_sb1)[par]
                uk = ("u0",)
                ok = (("o0",), ("o1",))[par]
                sdma(lambda e, tt=tt: e.dma_start(
                    out=yft2[:].rearrange("p (k c) -> p k c", c=128),
                    in_=bass.AP(yfm_d.tensor, yfm_d.offset + tt * 128,
                                [[T, 128], [128 * T, 16], [1, 128]])),
                    r=[("yfm", tt)], w=[("yft2",)])
                sdma(lambda e, tt=tt, ub=ub: e.dma_start(
                    out=ub[:], in_=u_in[tt * 128:(tt + 1) * 128, :]), w=[uk])
                for k in range(16):
                    for n in range(2):
                        pe(lambda e, k=k, n=n, pcol=pcol: e.matmul(
                            PS[:, pcol + n * 512: pcol + (n + 1) * 512],
                            yft2[:, k * 128:(k + 1) * 128],
                            wout_bf[:, k * 1024 + n * 512: k * 1024 + (n + 1) * 512],
                            start=(k == 0), stop=(k == 15)),
                           r=[("yft2",), ("woutbf",)], w=[pkeys[n]])
                dve(lambda e, q=q, ub=ub, ob=ob: e.tensor_add(
                    ob[:], ub[:], ma[:, q * DM:(q + 1) * DM]),
                    r=[uk, ("ma", q)], w=[ok])
                dve(lambda e, ob=ob, pcol=pcol: e.tensor_add(
                    ob[:], ob[:], PS[:, pcol:pcol + 1024]),
                    r=[ok] + pkeys, w=[ok])
                sdma(lambda e, tt=tt, ob=ob: e.dma_start(
                    out=out_d[tt * 128:(tt + 1) * 128, :], in_=ob[:]),
                    r=[ok], w=[("outd", tt)], tag="out")
            if th == 0:
                pb.barrier()

        pb.finalize(block, sems)
    return nc


_NC_CACHE = None


def _get_nc():
    global _NC_CACHE
    if _NC_CACHE is None:
        _NC_CACHE = build_nc()
    return _NC_CACHE


def _consts():
    ident = np.eye(128, dtype=np.float32)
    trib = np.zeros((128, 128), np.float32)
    totb = np.zeros((128, 128), np.float32)
    for h2 in range(2):
        sl = slice(h2 * 64, (h2 + 1) * 64)
        trib[sl, sl] = np.triu(np.ones((64, 64), np.float32))
        totb[sl, sl] = 1.0
    i64 = np.tile(np.eye(64, dtype=np.float32), (2, 32))
    ones = np.ones((1, 512), np.float32)
    return ident, trib, totb, i64, ones


def kernel(L0_attn_u, L0_attn_in_proj_w, L0_attn_conv_w, L0_attn_conv_b,
           L0_attn_init_states, L0_attn_dt_bias, L0_attn_A_log, L0_attn_D,
           L0_attn_norm_w, L0_attn_out_proj_w, L0_ffn_w_router, L0_ffn_w1,
           L0_ffn_b1, L0_ffn_w2, L0_ffn_b2):
    f = lambda x: np.ascontiguousarray(np.asarray(x, dtype=np.float32))
    u = f(L0_attn_u)
    winT = np.ascontiguousarray(f(L0_attn_in_proj_w).T)
    init_s = np.ascontiguousarray(
        f(L0_attn_init_states).transpose(2, 0, 1).reshape(64, NH * HD))
    woutT = np.ascontiguousarray(f(L0_attn_out_proj_w).T)
    ident, trib, totb, i64, ones = _consts()
    gmask = np.ones((NH, T), np.float32)
    gmask[:, ::64] = 0.0

    convb_st = np.ascontiguousarray(f(L0_attn_conv_b).reshape(NSLAB, 128).T)
    nw_st = np.ascontiguousarray(f(L0_attn_norm_w).reshape(16, 128).T)
    shared = dict(winT=winT, convw=f(L0_attn_conv_w), convb=convb_st,
                  dtb=f(L0_attn_dt_bias), alog=f(L0_attn_A_log), dd=f(L0_attn_D),
                  nw=nw_st, init=init_s, woutT=woutT,
                  wrout=f(L0_ffn_w_router), w1=f(L0_ffn_w1), b1=f(L0_ffn_b1),
                  w2=f(L0_ffn_w2), b2=f(L0_ffn_b2), c_ident=ident, c_trib=trib,
                  c_totb=totb, c_i64=i64, c_ones=ones, c_gmask=gmask)
    in_maps = []
    zeros_u = np.zeros((T, DM), np.float32)
    for b in range(4):
        for half in range(2):
            m = dict(shared)
            m["u"] = np.ascontiguousarray(u[b, half * T:(half + 1) * T])
            m["up"] = np.ascontiguousarray(u[b, 0:T]) if half == 1 else zeros_u
            s = 1.0 if half == 1 else 0.0
            m["sel"] = np.full((64, 1), s, np.float32)
            m["sel1m"] = np.full((64, 1), 1.0 - s, np.float32)
            in_maps.append(m)

    nc = _get_nc()
    res = run_bass_kernel_spmd(nc, in_maps, core_ids=list(range(8)))
    out = np.empty((4, 4096, 1024), np.float32)
    for b in range(4):
        for half in range(2):
            out[b, half * T:(half + 1) * T] = res.results[b * 2 + half]["out"]
    return out




# version 05
# speedup vs baseline: 1.0338x; 1.0338x over previous
import numpy as np
import concourse.bass as bass
import concourse.mybir as mybir
from concourse.bass_utils import run_bass_kernel_spmd
from contextlib import ExitStack

F32 = mybir.dt.float32
BF16 = mybir.dt.bfloat16
AF = mybir.ActivationFunctionType
ALU = mybir.AluOpType
AX = mybir.AxisListType

T = 2048
NT = 16
NTB = 4
NCH = 32
CH = 64
NH = 32
HD = 64
DM = 1024
DI = 2048
KD = 8
NSLAB = 17
XN_W = 2051

DEBUG = False
STREAMS = ("pe", "dve", "act", "pool", "sdma", "gdma")
BODY = {"pe": "tensor", "dve": "vector", "act": "scalar",
        "pool": "gpsimd", "gdma": "gpsimd", "sdma": "sync"}


class PB:
    def __init__(self):
        self.ops = []
        self.lastw = {}
        self.readers = {}

    def emit(self, stream, fn, r=(), w=(), dep=(), tag="", sync=False):
        i = len(self.ops)
        deps = set(dep)
        for k in r:
            lw = self.lastw.get(k)
            if lw is not None:
                deps.add(lw)
        for k in w:
            lw = self.lastw.get(k)
            if lw is not None:
                deps.add(lw)
            deps.update(self.readers.get(k, ()))
        self.ops.append(dict(stream=stream, fn=fn, deps=deps, tag=tag, sync=sync,
                             is_dma=stream in ("sdma", "gdma")))
        for k in r:
            self.readers.setdefault(k, set()).add(i)
        for k in w:
            self.lastw[k] = i
            self.readers[k] = set()
        return i

    def barrier(self):
        last = {}
        for i, op in enumerate(self.ops):
            last[op["stream"]] = i
        for s in STREAMS:
            deps = [last[t] for t in STREAMS if t != s and t in last]
            self.emit(s, lambda e: e.nop(), dep=deps, tag="bar")

    def finalize(self, block, sems):
        ops = self.ops
        need = [False] * len(ops)
        for i, op in enumerate(ops):
            kept = []
            for d in op["deps"]:
                dop = ops[d]
                same_body = BODY[dop["stream"]] == BODY[op["stream"]]
                if same_body and not dop["is_dma"] and not op["sync"]:
                    continue
                kept.append(d)
                need[d] = True
            op["kept"] = kept
        for i, op in enumerate(ops):
            if op["tag"] == "out" or op["is_dma"]:
                need[i] = True
        cnt = {s: 0 for s in STREAMS}
        for i, op in enumerate(ops):
            if need[i]:
                cnt[op["stream"]] += 16 if op["is_dma"] else 1
            op["val"] = cnt[op["stream"]]
            op["inc"] = need[i]
        final_sd = cnt["sdma"]

        body_ops = {}
        for i, op in enumerate(ops):
            body_ops.setdefault(BODY[op["stream"]], []).append(i)

        def run_body(eng, idxs):
            water = {}
            for i in idxs:
                op = ops[i]
                for d in op["kept"]:
                    dop = ops[d]
                    v = dop["val"]
                    if water.get(dop["stream"], 0) < v:
                        eng.wait_ge(sems[dop["stream"]], v)
                        water[dop["stream"]] = v
                ins = op["fn"](eng)
                if op["inc"]:
                    ins.then_inc(sems[op["stream"]], 16 if op["is_dma"] else 1)

        @block.tensor
        def _(t):
            run_body(t, body_ops.get("tensor", []))

        @block.vector
        def _(v):
            run_body(v, body_ops.get("vector", []))

        @block.scalar
        def _(s):
            run_body(s, body_ops.get("scalar", []))

        @block.gpsimd
        def _(g):
            run_body(g, body_ops.get("gpsimd", []))

        @block.sync
        def _(sy):
            run_body(sy, body_ops.get("sync", []))
            if final_sd:
                sy.wait_ge(sems["sdma"], final_sd)


def _mkv(base_ap, dims, off):
    newap = [list(base_ap.ap[0])] + [[s, c] for (s, c) in dims]
    return bass.AP(base_ap.tensor, base_ap.offset + off, newap)


def build_nc():
    nc = bass.Bass("TRN2", target_bir_lowering=False)
    din = lambda n, s: nc.dram_tensor(n, s, F32, kind="ExternalInput").ap()

    u_in = din("u", [T, DM])
    up_in = din("up", [T, DM])
    winT = din("winT", [DM, 4256])
    convw = din("convw", [2176, 4])
    convb = din("convb", [128, NSLAB])
    dtb = din("dtb", [NH])
    alog = din("alog", [NH])
    dd = din("dd", [NH])
    nw = din("nw", [128, 16])
    init_in = din("init", [64, NH * HD])
    woutT = din("woutT", [DI, DM])
    wrout = din("wrout", [DM, 8])
    w1_in = din("w1", [8, DM, 4096])
    b1_in = din("b1", [8, 4096])
    w2_in = din("w2", [8, 4096, DM])
    b2_in = din("b2", [8, DM])
    c_ident = din("c_ident", [128, 128])
    c_trib = din("c_trib", [128, 128])
    c_totb = din("c_totb", [128, 128])
    c_i64 = din("c_i64", [128, DI])
    c_ones = din("c_ones", [1, 512])
    c_gmask = din("c_gmask", [NH, T])
    sel_in = din("sel", [64, 1])
    sel1m_in = din("sel1m", [64, 1])

    out_d = nc.dram_tensor("out", [T, DM], F32, kind="ExternalOutput").ap()

    zs_d = nc.dram_tensor("zs_d", [T, DI], BF16, kind="Internal").ap()
    yfm_d = nc.dram_tensor("yfm_d", [DI, T], BF16, kind="Internal").ap()
    g_d = nc.dram_tensor("g_d", [NH, T], F32, kind="Internal").ap()
    tot_d = nc.dram_tensor("tot_d", [1, NCH * NH], F32, kind="Internal").ap()
    a_d = nc.dram_tensor("a_d", [1, NH], F32, kind="Internal").ap()

    pb = PB()
    with ExitStack() as ctx:
        sb = lambda n, s, d: ctx.enter_context(nc.sbuf_tensor(n, s, d))
        xn = sb("xn", [128, KD * XN_W], BF16)
        arena1 = sb("arena1", [128, KD * XN_W], BF16)
        arena2 = sb("arena2", [128, 32768], BF16)
        arena3 = sb("arena3", [128, 10240], F32)
        dtraw = arena3[:, 0:512]
        scrA = arena3[:, 512:1024]
        scrB = arena3[:, 1024:1536]
        dt_tm = arena3[:, 1536:2048]
        cs_sb = arena3[:, 2048:2560]
        tot_sb = arena3[:, 2560:3072]
        sdo_sb = arena3[:, 3072:3584]
        ds_sb = arena3[:, 3584:4096]
        dtds = arena3[:, 4096:4608]
        la = arena3[0:64, 4608:5632]
        dtb_rep = sb("dtb_rep", [128, NH], F32)
        A_rep = sb("A_rep", [128, NH], F32)
        D_rep = sb("D_rep", [128, NH], F32)
        arow = sb("arow", [1, NH], F32)
        acolf = sb("acolf", [NH, 1], F32)
        dtbcol = sb("dtbcol", [NH, 1], F32)
        cw_sb = sb("cw_sb", [128, NSLAB * 4], F32)
        cb_sb = sb("cb_sb", [128, NSLAB], F32)
        pre0 = sb("pre0", [128, 515], BF16)
        pre1 = sb("pre1", [128, 515], BF16)
        acc0 = sb("acc0", [128, 512], F32)
        acc1 = acc0
        xs0 = sb("xs0", [128, 512], BF16)
        xs1 = sb("xs1", [128, 512], BF16)
        wsl0 = sb("wsl0", [128, 1024], BF16)
        wsl1 = sb("wsl1", [128, 1024], BF16)
        bc_fm = sb("bc_fm", [128, T], BF16)
        c_fm0 = sb("c_fm0", [64, T], BF16)
        bct = sb("bct", [128, 512], BF16)
        u_sb0 = sb("u_sb0", [128, DM], F32)
        u_sb1 = u_sb0
        sq_scr = sb("sq_scr", [128, DM], F32)
        stats = sb("stats", [128, 16], F32)
        id_bf = sb("id_bf", [128, 128], BF16)
        trib = sb("trib", [128, 128], F32)
        totb = sb("totb", [128, 128], F32)
        i64rep = sb("i64rep", [128, DI], BF16)
        diagD = sb("diagD", [128, DI], BF16)
        ones_bf = sb("ones_bf", [1, 512], BF16)
        sel_sb = sb("sel_sb", [64, 1], F32)
        sel1m_sb = sb("sel1m_sb", [64, 1], F32)
        w_dt = sb("w_dt", [128, KD * 32], BF16)
        w_r = sb("w_r", [128, KD * 8], BF16)
        logits = arena3[:, 1152:1280]
        probs = arena3[:, 1280:1408]
        cnt_sb = arena3[:, 1024:1152]
        tmpc = sb("tmpc", [128, 16], F32)
        wgt = arena3[:, 1408:1536]
        wgt_bf = sb("wgt_bf", [128, 128], BF16)
        a3b = arena3[:].bitcast(BF16)
        S0 = a3b[0:64, 11264:13312]
        S1 = a3b[0:64, 13312:15360]
        wgtf = a3b[0:8, 0:2048]
        b2_bf = a3b[0:8, 8192:9216]
        b1q = a3b[0:1, 9216:10240]
        hq0 = a3b[:, 4096:6144]
        hq1 = a3b[:, 6144:8192]
        nw_sb = sb("nw_sb", [128, 16], F32)
        PS = ctx.enter_context(nc.psum_tensor("PS", [128, 4096], F32))
        PSB = PS.bitcast(BF16)

        sems = {s: ctx.enter_context(nc.semaphore("sem_" + s)) for s in STREAMS}
        block = ctx.enter_context(nc.Block())

        xnp = arena1[:, 0:KD * XN_W]
        a1f = arena1[:].bitcast(F32)
        dtfm_a = a1f[:, 0:2048][0:NH, :]
        dtfm_b = a1f[:, 2048:4096][0:NH, :]
        gmask_sb = a1f[:, 4096:6144][0:NH, :]
        ma = a1f[:, 0:8192]
        xr = arena2[:, 0:8192]
        yr = arena2[:, 8192:16384]
        lt_b = arena2[:, 16384:18432]
        gl_b = arena2[:, 18432:20480]
        zb_b = arena2[:, 20480:22528]
        x2b = arena2[:, 22528:24576]
        xdb = arena2[:, 24576:26624]
        gr_b = arena2[:].bitcast(F32)[:, 13312:15360]
        yft = arena2[:, 30720:32768]
        w1q = (arena2[:, 0:8192], arena2[:, 8192:16384])
        w2q = (arena2[:, 16384:24576], arena2[:, 24576:32768])
        w_z = arena2[:, 0:16384]
        zs_b0 = arena2[:, 16384:18432]
        zs_b1 = arena2[:, 18432:20480]
        wout_bf = arena2[:, 0:16384]
        yft2 = arena2[:, 16384:18432]
        out_sb0 = arena2[:].bitcast(F32)[:, 9216:10240]
        out_sb1 = arena2[:].bitcast(F32)[:, 10240:11264]

        def dve(fn, r=(), w=(), tag="", sync=False):
            return pb.emit("dve", fn, r, w, tag=tag, sync=sync)

        def act(fn, r=(), w=()):
            return pb.emit("act", fn, r, w)

        def pool(fn, r=(), w=()):
            return pb.emit("pool", fn, r, w)

        def pe(fn, r=(), w=()):
            return pb.emit("pe", fn, r, w)

        def sdma(fn, r=(), w=(), tag=""):
            return pb.emit("sdma", fn, r, w, tag=tag)

        def gdma(fn, r=(), w=()):
            return pb.emit("gdma", fn, r, w)

        def psk(lo, hi):
            return [("PS", b) for b in range(lo // 512, (hi + 511) // 512)]

        def tap(name, ap, r, dt=None):
            if not DEBUG:
                return
            shape = list(ap.shape)
            t = nc.dram_tensor("dbg_" + name, shape, ap.dtype, kind="ExternalOutput").ap()
            sdma(lambda e, ap=ap, t=t: e.dma_start(out=t, in_=ap), r=r, tag="out")

        def tapd(name, dram_ap, r):
            if not DEBUG:
                return
            shape = list(dram_ap.shape)
            t = nc.dram_tensor("dbg_" + name, shape, dram_ap.dtype, kind="ExternalOutput").ap()
            sdma(lambda e, t=t, dram_ap=dram_ap: e.dma_start(out=t, in_=dram_ap), r=r, tag="out")

        gdma(lambda e: e.dma_start(out=id_bf[:], in_=c_ident), w=[("id",)])
        sdma(lambda e: e.dma_start(out=trib[:], in_=c_trib), w=[("trib",)])
        sdma(lambda e: e.dma_start(out=totb[:], in_=c_totb), w=[("totb",)])
        gdma(lambda e: e.dma_start(out=i64rep[:], in_=c_i64), w=[("i64",)])
        gdma(lambda e: e.dma_start(out=ones_bf[:], in_=c_ones), w=[("ones",)])
        sdma(lambda e: e.dma_start(out=sel_sb[:], in_=sel_in), w=[("sel",)])
        sdma(lambda e: e.dma_start(out=sel1m_sb[:], in_=sel1m_in), w=[("sel1m",)])
        sdma(lambda e: e.dma_start(out=dtb_rep[:], in_=dtb.unsqueeze(0).to_broadcast([128, NH])),
             w=[("dtbrep",)])
        sdma(lambda e: e.dma_start(out=dtbcol[:], in_=dtb.unsqueeze(1)), w=[("dtbcol",)])
        sdma(lambda e: e.dma_start(out=D_rep[:], in_=dd.unsqueeze(0).to_broadcast([128, NH])),
             w=[("drep",)])
        sdma(lambda e: e.dma_start(out=arow[:], in_=alog.unsqueeze(0)), w=[("arow",)])
        sdma(lambda e: e.dma_start(out=acolf[:], in_=alog.unsqueeze(1)), w=[("acolf",)])
        gdma(lambda e: e.dma_start(out=w_dt[:].rearrange("p (k h) -> p k h", h=32),
                                   in_=winT[:, 4224:4256].rearrange("(k p) h -> p k h", p=128)),
             w=[("wdt",)])
        gdma(lambda e: e.dma_start(out=w_r[:].rearrange("p (k h) -> p k h", h=8),
                                   in_=wrout.rearrange("(k p) h -> p k h", p=128)),
             w=[("wr",)])
        sdma(lambda e: e.dma_start(out=cw_sb[:].rearrange("p (s k) -> p s k", k=4),
                                   in_=convw.rearrange("(s p) k -> p s k", p=128)),
             w=[("cw",)])
        sdma(lambda e: e.dma_start(out=cb_sb[:], in_=convb), w=[("cb",)])
        sdma(lambda e: e.dma_start(out=nw_sb[:], in_=nw), w=[("nw",)])
        act(lambda e: e.activation(arow[:], arow[:], AF.Exp), r=[("arow",)], w=[("arow",)])
        dve(lambda e: e.tensor_scalar_mul(arow[:], arow[:], -1.0), r=[("arow",)], w=[("arow",)])
        sdma(lambda e: e.dma_start(out=a_d, in_=arow[:]), r=[("arow",)], w=[("a_d",)])
        sdma(lambda e: e.dma_start(out=A_rep[:], in_=a_d.to_broadcast([128, NH])),
             r=[("a_d",)], w=[("arep",)])
        act(lambda e: e.activation(acolf[:], acolf[:], AF.Exp), r=[("acolf",)], w=[("acolf",)])
        dve(lambda e: e.tensor_scalar_mul(acolf[:], acolf[:], -1.0), r=[("acolf",)], w=[("acolf",)])
        dve(lambda e: e.tensor_tensor(
            diagD[:].rearrange("p (h l) -> p h l", l=HD),
            i64rep[:].rearrange("p (h l) -> p h l", l=HD),
            D_rep[:].unsqueeze(2).to_broadcast([128, NH, HD]), ALU.mult),
            r=[("i64",), ("drep",)], w=[("diagD",)])

        def xn_col(k, t):
            return k * XN_W + 3 + t

        def layernorm(u_ap, xn_buf, xnkey):
            ubufs = (u_sb0, u_sb1)
            ukeys = (("u0",), ("u0",))
            for tt in range(NT):
                ub, uk = ubufs[tt % 2], ukeys[tt % 2]
                sdma(lambda e, tt=tt, ub=ub: e.dma_start(
                    out=ub[:], in_=u_ap[tt * 128:(tt + 1) * 128, :]), w=[uk])
                dve(lambda e, ub=ub: e.reduce_sum(stats[:, 0:1], ub[:], axis=AX.X, negate=True),
                    r=[uk], w=[("st0",)])
                dve(lambda e: e.tensor_scalar_mul(stats[:, 0:1], stats[:, 0:1], 1.0 / DM),
                    r=[("st0",)], w=[("st0",)], sync=True)
                dve(lambda e, ub=ub: e.tensor_scalar_add(ub[:], ub[:], stats[:, 0:1]),
                    r=[uk, ("st0",)], w=[uk], sync=True)
                act(lambda e, ub=ub: e.activation(sq_scr[:], ub[:], AF.Square,
                                                  accum_out=stats[:, 1:2]),
                    r=[uk], w=[("sq",), ("st1",)])
                dve(lambda e: e.tensor_scalar(stats[:, 2:3], stats[:, 1:2], 1.0 / DM, 1e-5,
                                              ALU.mult, ALU.add),
                    r=[("st1",)], w=[("st2",)])
                act(lambda e: e.activation(stats[:, 3:4], stats[:, 2:3], AF.Sqrt),
                    r=[("st2",)], w=[("st3",)])
                dve(lambda e: e.reciprocal(stats[:, 4:5], stats[:, 3:4]),
                    r=[("st3",)], w=[("st4",)])
                dve(lambda e, ub=ub: e.tensor_scalar_mul(
                    sq_scr[:].bitcast(BF16)[:, 0:1024], ub[:], stats[:, 4:5]),
                    r=[uk, ("st4",)], w=[("sq",)], sync=True)
                for kb in range(0, KD, 4):
                    bank = 6 + (kb // 4)
                    for k in range(kb, kb + 4):
                        pe(lambda e, k=k, kb=kb, bank=bank: e.transpose(
                            PSB[:, bank * 1024 + (k - kb) * 128: bank * 1024 + (k - kb + 1) * 128],
                            sq_scr[:].bitcast(BF16)[:, k * 128:(k + 1) * 128], id_bf[:]),
                           r=[("sq",), ("id",)], w=[("PS", bank)])
                    dve(lambda e, tt=tt, kb=kb, bank=bank, xn_buf=xn_buf: e.tensor_copy(
                        _mkv(xn_buf[:], [(XN_W, 4), (1, 128)], kb * XN_W + 3 + tt * 128),
                        PSB[:, bank * 1024: bank * 1024 + 512]),
                        r=[("PS", bank)], w=[xnkey])

        def softplus_tm():
            dtbv = _mkv(dtb_rep[:], [(0, NT), (1, NH)], 0)
            dve(lambda e: e.tensor_tensor(scrA[:].rearrange("p (t h) -> p t h", h=NH),
                                          dtraw[:].rearrange("p (t h) -> p t h", h=NH),
                                          dtbv, ALU.add),
                r=[("dtraw",), ("dtbrep",)], w=[("scrA",)])
            act(lambda e: e.activation(scrB[:], scrA[:], AF.Abs),
                r=[("scrA",)], w=[("scrB",)])
            act(lambda e: e.activation(scrB[:], scrB[:], AF.Exp, scale=-1.0),
                r=[("scrB",)], w=[("scrB",)])
            act(lambda e: e.activation(scrB[:], scrB[:], AF.Ln, bias=1.0),
                r=[("scrB",)], w=[("scrB",)])
            dve(lambda e: e.tensor_scalar_max(scrA[:], scrA[:], 0.0),
                r=[("scrA",)], w=[("scrA",)])
            dve(lambda e: e.tensor_add(dt_tm[:], scrA[:], scrB[:]),
                r=[("scrA",), ("scrB",)], w=[("dt_tm",)])

        def dt_router_pass(xn_buf, xnkey, main):
            for tt in range(NT):
                for k in range(KD):
                    pe(lambda e, k=k, tt=tt, xn_buf=xn_buf: e.matmul(
                        PS[:, tt * 32:(tt + 1) * 32],
                        _mkv(xn_buf[:], [(1, 128)], xn_col(k, tt * 128)),
                        w_dt[:, k * 32:(k + 1) * 32],
                        start=(k == 0 and tt == 0), stop=(k == KD - 1)),
                       r=[xnkey, ("wdt",)], w=[("PS", 0)])
                    if main:
                        pe(lambda e, k=k, tt=tt, xn_buf=xn_buf: e.matmul(
                            PS[:, 1024 + tt * 8: 1024 + (tt + 1) * 8],
                            _mkv(xn_buf[:], [(1, 128)], xn_col(k, tt * 128)),
                            w_r[:, k * 8:(k + 1) * 8],
                            start=(k == 0 and tt == 0), stop=(k == KD - 1)),
                           r=[xnkey, ("wr",)], w=[("PS", 2)])
            dve(lambda e: e.tensor_copy(dtraw[:], PS[:, 0:512]), r=[("PS", 0)], w=[("dtraw",)])
            softplus_tm()
            arepv = _mkv(A_rep[:], [(0, NT), (1, NH)], 0)
            dve(lambda e: e.tensor_tensor(scrA[:].rearrange("p (t h) -> p t h", h=NH),
                                          dt_tm[:].rearrange("p (t h) -> p t h", h=NH),
                                          arepv, ALU.mult),
                r=[("dt_tm",), ("arep",)], w=[("scrA",)])
            for j in range(NT):
                pe(lambda e, j=j: e.matmul(PS[:, j * 32:(j + 1) * 32], trib[:],
                                           scrA[:, j * 32:(j + 1) * 32],
                                           start=(j == 0), stop=True),
                   r=[("trib",), ("scrA",)], w=[("PS", 0)])
                pe(lambda e, j=j: e.matmul(PS[:, 512 + j * 32: 512 + (j + 1) * 32], totb[:],
                                           scrA[:, j * 32:(j + 1) * 32],
                                           start=(j == 0), stop=True),
                   r=[("totb",), ("scrA",)], w=[("PS", 1)])
            dve(lambda e: e.tensor_copy(cs_sb[:], PS[:, 0:512]), r=[("PS", 0)], w=[("cs",)])
            dve(lambda e: e.tensor_copy(tot_sb[:], PS[:, 512:1024]), r=[("PS", 1)], w=[("tot",)])
            if main:
                act(lambda e: e.activation(sdo_sb[:], cs_sb[:], AF.Exp), r=[("cs",)], w=[("sdo",)])
            pool(lambda e: e.tensor_sub(ds_sb[:], tot_sb[:], cs_sb[:]),
                 r=[("tot",), ("cs",)], w=[("ds",)])
            act(lambda e: e.activation(ds_sb[:], ds_sb[:], AF.Exp), r=[("ds",)], w=[("ds",)])
            pool(lambda e: e.tensor_mul(dtds[:], dt_tm[:], ds_sb[:]),
                 r=[("dt_tm",), ("ds",)], w=[("dtds",)])
            sdma(lambda e: e.dma_start(
                out=bass.AP(tot_d.tensor, tot_d.offset, [[1, 1], [64, NT], [1, NH]]),
                in_=tot_sb[0:1, :].rearrange("o (j h) -> o j h", h=NH)),
                r=[("tot",)], w=[("tot_d",)])
            sdma(lambda e: e.dma_start(
                out=bass.AP(tot_d.tensor, tot_d.offset + NH, [[1, 1], [64, NT], [1, NH]]),
                in_=tot_sb[64:65, :].rearrange("o (j h) -> o j h", h=NH)),
                r=[("tot",)], w=[("tot_d",)])
            sdma(lambda e: e.dma_start(out=la[:], in_=tot_d.to_broadcast([64, NCH * NH])),
                 r=[("tot_d",)], w=[("la",)])
            act(lambda e: e.activation(la[:], la[:], AF.Exp), r=[("la",)], w=[("la",)])

        def router_softmax_wgt():
            dve(lambda e: e.tensor_copy(logits[:], PS[:, 1024:1152]),
                r=[("PS", 2)], w=[("logits",), ("scrB",)])
            for tt in range(NT):
                dve(lambda e, tt=tt: e.reduce_max(stats[:, 5:6], logits[:, tt * 8:(tt + 1) * 8],
                                                  axis=AX.X),
                    r=[("logits",)], w=[("st5",)])
                dve(lambda e, tt=tt: e.tensor_scalar_sub(probs[:, tt * 8:(tt + 1) * 8],
                                                         logits[:, tt * 8:(tt + 1) * 8],
                                                         stats[:, 5:6]),
                    r=[("logits",), ("st5",)], w=[("probs",)], sync=True)
            act(lambda e: e.activation(probs[:], probs[:], AF.Exp), r=[("probs",)], w=[("probs",)])
            for tt in range(NT):
                dve(lambda e, tt=tt: e.reduce_sum(stats[:, 6:7], probs[:, tt * 8:(tt + 1) * 8],
                                                  axis=AX.X),
                    r=[("probs",)], w=[("st6",)])
                dve(lambda e: e.reciprocal(stats[:, 7:8], stats[:, 6:7]), r=[("st6",)], w=[("st7",)],
                    sync=True)
                dve(lambda e, tt=tt: e.tensor_scalar_mul(probs[:, tt * 8:(tt + 1) * 8],
                                                         probs[:, tt * 8:(tt + 1) * 8],
                                                         stats[:, 7:8]),
                    r=[("probs",), ("st7",)], w=[("probs",)], sync=True)
            dve(lambda e: e.memset(cnt_sb[:], 0.0), w=[("cnt",), ("scrB",)])
            for ee in range(8):
                for e2 in range(8):
                    if e2 == ee:
                        continue
                    op = ALU.is_ge if e2 < ee else ALU.is_gt
                    dve(lambda e, ee=ee, e2=e2, op=op: e.tensor_tensor(
                        tmpc[:], _mkv(probs[:], [(8, NT)], e2),
                        _mkv(probs[:], [(8, NT)], ee), op),
                        r=[("probs",)], w=[("tmpc",)])
                    pool(lambda e, ee=ee: e.tensor_add(
                        _mkv(cnt_sb[:], [(8, NT)], ee),
                        _mkv(cnt_sb[:], [(8, NT)], ee), tmpc[:]),
                        r=[("cnt",), ("tmpc",)], w=[("cnt",)])
            dve(lambda e: e.tensor_single_scalar(wgt[:], cnt_sb[:], 1.5, ALU.is_lt),
                r=[("cnt",)], w=[("wgt",)])
            dve(lambda e: e.tensor_mul(wgt[:], wgt[:], probs[:]),
                r=[("wgt",), ("probs",)], w=[("wgt",)])
            dve(lambda e: e.tensor_copy(wgt_bf[:], wgt[:]), r=[("wgt",)], w=[("wgtbf",)])
            for tt in range(NT):
                bank = 6 + (tt % 2)
                pe(lambda e, tt=tt, bank=bank: e.transpose(
                    PSB[0:8, bank * 1024: bank * 1024 + 128],
                    wgt_bf[:, tt * 8:(tt + 1) * 8], id_bf[:]),
                   r=[("wgtbf",), ("id",)], w=[("PS", bank)])
                dve(lambda e, tt=tt, bank=bank: e.tensor_copy(
                    wgtf[:, tt * 128:(tt + 1) * 128],
                    PSB[0:8, bank * 1024: bank * 1024 + 128]),
                    r=[("PS", bank)], w=[("wgtf",), ("dtraw",), ("scrA",)])

        def dtfm_g_pass(xn_buf, xnkey):
            sdma(lambda e: e.dma_start(out=gmask_sb[:], in_=c_gmask),
                 w=[("gmask",), ("xnp",)])
            for k in range(KD):
                for n in range(4):
                    pe(lambda e, k=k, n=n, xn_buf=xn_buf: e.matmul(
                        PS[0:NH, n * 512:(n + 1) * 512],
                        w_dt[:, k * 32:(k + 1) * 32],
                        _mkv(xn_buf[:], [(1, 512)], xn_col(k, n * 512)),
                        start=(k == 0), stop=(k == KD - 1)),
                       r=[xnkey, ("wdt",)], w=[("PS", n)])
            dve(lambda e: e.tensor_copy(dtfm_a[:], PS[0:NH, 0:2048]),
                r=psk(0, 2048), w=[("dtfma",), ("xnp",)])
            tap("dtfmraw", dtfm_a[:], [("dtfma",)])
            dve(lambda e: e.tensor_scalar_add(dtfm_a[:], dtfm_a[:], dtbcol[:, 0:1]),
                r=[("dtfma",), ("dtbcol",)], w=[("dtfma",)])
            tap("dtfm1", dtfm_a[:], [("dtfma",)])
            act(lambda e: e.activation(dtfm_b[:], dtfm_a[:], AF.Abs),
                r=[("dtfma",)], w=[("dtfmb",), ("xnp",)])
            act(lambda e: e.activation(dtfm_b[:], dtfm_b[:], AF.Exp, scale=-1.0),
                r=[("dtfmb",)], w=[("dtfmb",)])
            act(lambda e: e.activation(dtfm_b[:], dtfm_b[:], AF.Ln, bias=1.0),
                r=[("dtfmb",)], w=[("dtfmb",)])
            tap("dtfm2", dtfm_b[:], [("dtfmb",)])
            dve(lambda e: e.tensor_scalar_max(dtfm_a[:], dtfm_a[:], 0.0),
                r=[("dtfma",)], w=[("dtfma",)])
            dve(lambda e: e.tensor_add(dtfm_a[:], dtfm_a[:], dtfm_b[:]),
                r=[("dtfma",), ("dtfmb",)], w=[("dtfma",)])
            tap("dtfm3", dtfm_a[:], [("dtfma",)])
            dve(lambda e: e.tensor_scalar_mul(dtfm_a[:], dtfm_a[:], acolf[:, 0:1]),
                r=[("dtfma",), ("acolf",)], w=[("dtfma",)])
            tap("dtfm4", dtfm_a[:], [("dtfma",)])
            act(lambda e: e.activation(dtfm_a[:], dtfm_a[:], AF.Exp),
                r=[("dtfma",)], w=[("dtfma",)])
            tap("dtfm5", dtfm_a[:], [("dtfma",)])
            dve(lambda e: e.tensor_mul(dtfm_a[:], dtfm_a[:], gmask_sb[:]),
                r=[("dtfma",), ("gmask",)], w=[("dtfma",)])
            tap("gmask", gmask_sb[:], [("gmask",)])
            tap("dtfm6", dtfm_a[:], [("dtfma",)])
            sdma(lambda e: e.dma_start(out=g_d, in_=dtfm_a[:]), r=[("dtfma",)], w=[("g_d",)])

        def z_pass(xn_buf, xnkey):
            gdma(lambda e: e.dma_start(
                out=w_z[:].rearrange("p (k z) -> p k z", z=2048),
                in_=winT[:, 0:DI].rearrange("(k p) z -> p k z", p=128)),
                w=[("w_z",)])
            zbufs = (zs_b0, zs_b1)
            zkeys = (("zs0",), ("zs1",))
            for tt in range(NT):
                par = tt % 2
                pscol = par * 2048
                pkeys = psk(pscol, pscol + 2048)
                for k in range(KD):
                    for n in range(4):
                        pe(lambda e, k=k, tt=tt, n=n, pscol=pscol, xn_buf=xn_buf: e.matmul(
                            PS[:, pscol + n * 512: pscol + (n + 1) * 512],
                            _mkv(xn_buf[:], [(1, 128)], xn_col(k, tt * 128)),
                            w_z[:, k * 2048 + n * 512: k * 2048 + (n + 1) * 512],
                            start=(k == 0), stop=(k == KD - 1)),
                           r=[xnkey, ("w_z",)], w=[pkeys[n]])
                zbuf, zkey = zbufs[par], zkeys[par]
                act(lambda e, pscol=pscol, zbuf=zbuf: e.activation(
                    zbuf[:], PS[:, pscol:pscol + 2048], AF.Silu),
                    r=pkeys, w=[zkey])
                sdma(lambda e, tt=tt, zbuf=zbuf: e.dma_start(
                    out=zs_d[tt * 128:(tt + 1) * 128, :], in_=zbuf[:]),
                    r=[zkey], w=[("zs_d", tt)])

        def half_pipeline(u_ap, xn_buf, xnkey, main):
            layernorm(u_ap, xn_buf, xnkey)
            if not main:
                for k in range(KD):
                    dve(lambda e, k=k, xn_buf=xn_buf: e.memset(
                        _mkv(xn_buf[:], [(1, 3)], k * XN_W), 0.0),
                        w=[xnkey])
            else:
                dve(lambda e: e.tensor_copy(_mkv(xn_buf[:], [(XN_W, KD), (1, 3)], 0),
                                            _mkv(xnp[:], [(XN_W, KD), (1, 3)], XN_W - 3)),
                    r=[("xnp",)], w=[xnkey])
            dt_router_pass(xn_buf, xnkey, main)
            if main:
                tap("xn", xn_buf[:], [xnkey])
                tap("dtraw", dtraw[:], [("dtraw",)])
                tap("dt_tm", dt_tm[:], [("dt_tm",)])
                tap("cs", cs_sb[:], [("cs",)])
                tap("tot", tot_sb[:], [("tot",)])
                tap("sdo", sdo_sb[:], [("sdo",)])
                tap("ds", ds_sb[:], [("ds",)])
                tap("dtds", dtds[:], [("dtds",)])
                tap("la", la[:], [("la",)])
                router_softmax_wgt()
                tap("logits", logits[:], [("logits",)])
                tap("probs", probs[:], [("probs",)])
                tap("wgt", wgt[:], [("wgt",)])
                tap("wgtf", wgtf[:], [("wgtf",)])
                dtfm_g_pass(xn_buf, xnkey)
                tapd("g_d", g_d, [("g_d",)])
                z_pass(xn_buf, xnkey)
                tapd("zs_d", zs_d, [("zs_d", t) for t in range(NT)])
                pb.barrier()
            if not main:
                gdma(lambda e: e.dma_start(out=S0[:], in_=init_in), w=[("S", 0)])
            else:
                gdma(lambda e: e.dma_start(out=S1[:], in_=init_in), w=[("S", 1)])
                dve(lambda e: e.tensor_scalar_mul(S1[:], S1[:], sel1m_sb[:, 0:1]),
                    r=[("S", 1), ("sel1m",)], w=[("S", 1)])
                dve(lambda e: e.scalar_tensor_tensor(S0[:], S0[:], sel_sb[:, 0:1], S1[:],
                                                     ALU.mult, ALU.add),
                    r=[("S", 0), ("sel",), ("S", 1)], w=[("S", 0)])
                tap("Sinit", S0[:], [("S", 0)])

            wsls = (wsl0, wsl1)
            pres = (pre0, pre1)
            accs = (acc0, acc1)
            xss = (xs0, xs1)
            for tb in range(NTB):
                for slab in range(NSLAB):
                    sp = slab % 2
                    wk = ("wsl", sp)
                    pk = ("pre", sp)
                    ak = ("acc",)
                    xk2 = ("xs", sp)
                    gdma(lambda e, slab=slab, sp=sp: e.dma_start(
                        out=wsls[sp][:].rearrange("p (k c) -> p k c", c=128),
                        in_=winT[:, 2048 + slab * 128: 2048 + (slab + 1) * 128]
                        .rearrange("(k p) c -> p k c", p=128)),
                        w=[wk])
                    for k in range(KD):
                        pe(lambda e, k=k, tb=tb, sp=sp, xn_buf=xn_buf: e.matmul(
                            PS[:, 512:515],
                            wsls[sp][:, k * 128:(k + 1) * 128],
                            _mkv(xn_buf[:], [(1, 3)], k * XN_W + tb * 512),
                            start=(k == 0), stop=(k == KD - 1)),
                           r=[wk, xnkey], w=[("PS", 1)])
                        pe(lambda e, k=k, tb=tb, sp=sp, xn_buf=xn_buf: e.matmul(
                            PS[:, 0:512],
                            wsls[sp][:, k * 128:(k + 1) * 128],
                            _mkv(xn_buf[:], [(1, 512)], xn_col(k, tb * 512)),
                            start=(k == 0), stop=(k == KD - 1)),
                           r=[wk, xnkey], w=[("PS", 0)])
                    dve(lambda e, sp=sp: e.tensor_copy(pres[sp][:, 0:3], PS[:, 512:515]),
                        r=[("PS", 1)], w=[pk])
                    dve(lambda e, sp=sp: e.tensor_copy(pres[sp][:, 3:515], PS[:, 0:512]),
                        r=[("PS", 0)], w=[pk])
                    dve(lambda e, sp=sp, slab=slab: e.tensor_scalar(
                        accs[sp][:], pres[sp][:, 0:512], cw_sb[:, slab * 4: slab * 4 + 1],
                        None, ALU.mult),
                        r=[pk, ("cw",)], w=[ak])
                    for kk in range(1, 4):
                        dve(lambda e, sp=sp, slab=slab, kk=kk: e.scalar_tensor_tensor(
                            accs[sp][:], pres[sp][:, kk:kk + 512],
                            cw_sb[:, slab * 4 + kk: slab * 4 + kk + 1],
                            accs[sp][:], ALU.mult, ALU.add),
                            r=[pk, ("cw",), ak], w=[ak])
                    bank = 6 + sp
                    if slab < 16:
                        act(lambda e, sp=sp, slab=slab: e.activation(
                            xss[sp][:], accs[sp][:], AF.Silu, bias=cb_sb[:, slab:slab + 1]),
                            r=[ak, ("cb",)], w=[xk2])
                        for q in range(4):
                            pe(lambda e, sp=sp, q=q, bank=bank: e.transpose(
                                PSB[:, bank * 1024 + q * 128: bank * 1024 + (q + 1) * 128],
                                xss[sp][:, q * 128:(q + 1) * 128], id_bf[:]),
                               r=[xk2, ("id",)], w=[("PS", bank)])
                        dve(lambda e, slab=slab, bank=bank: e.tensor_copy(
                            _mkv(xr[:], [(2048, 4), (1, 128)], slab * 128),
                            PSB[:, bank * 1024: bank * 1024 + 512]),
                            r=[("PS", bank)], w=[("xr",)])
                    else:
                        act(lambda e, sp=sp, slab=slab, tb=tb: e.activation(
                            bc_fm[:, tb * 512:(tb + 1) * 512], accs[sp][:], AF.Silu,
                            bias=cb_sb[:, slab:slab + 1]),
                            r=[ak, ("cb",)], w=[("bcfm",)])
                        for q in range(4):
                            pe(lambda e, tb=tb, q=q, bank=bank: e.transpose(
                                PSB[:, bank * 1024 + q * 128: bank * 1024 + (q + 1) * 128],
                                bc_fm[:, tb * 512 + q * 128: tb * 512 + (q + 1) * 128],
                                id_bf[:]),
                               r=[("bcfm",), ("id",)], w=[("PS", bank)])
                        dve(lambda e, bank=bank: e.tensor_copy(
                            bct[:], PSB[:, bank * 1024: bank * 1024 + 512]),
                            r=[("PS", bank)], w=[("bct",)])
                        sdma(lambda e, tb=tb: e.dma_start(
                            out=c_fm0[:, tb * 512:(tb + 1) * 512],
                            in_=bc_fm[64:128, tb * 512:(tb + 1) * 512]),
                            r=[("bcfm",)], w=[("cfm0",)])

                if main and tb == 0:
                    tap("xr0", xr[:], [("xr",)])
                    tap("bct0", bct[:], [("bct",)])
                    tap("bcfm", bc_fm[:], [("bcfm",)])
                    tap("cfm0", c_fm0[:], [("cfm0",)])
                for j2 in range(8):
                    c = tb * 8 + j2
                    par = c % 2
                    base = par * 64
                    tile = c // 2
                    slot = tile % 4
                    kSc = ("S", c % 2)
                    kSn = ("S", (c + 1) % 2)
                    if main:
                        sdma(lambda e, c=c, base=base: e.dma_start(
                            out=zb_b[base:base + 64, :], in_=zs_d[c * 64:(c + 1) * 64, :]),
                            r=[("zs_d", c // 2)], w=[("zb", par)])
                        sdma(lambda e, c=c, base=base: e.dma_start(
                            out=gr_b[base:base + 64, :].rearrange("p (h l) -> p h l", l=64),
                            in_=bass.AP(g_d.tensor, g_d.offset + c * 64,
                                        [[0, 64], [T, NH], [1, 64]])),
                            r=[("g_d",)], w=[("gr", par)])
                        pool(lambda e, base=base, slot=slot, tile=tile: e.tensor_tensor(
                            x2b[base:base + 64, :].rearrange("p (h q) -> p h q", q=HD),
                            _mkv(xr[base:base + 64, :], [(64, NH), (1, HD)], slot * 2048),
                            _mkv(dt_tm[base:base + 64, :], [(1, NH), (0, HD)], tile * 32),
                            ALU.mult),
                            r=[("xr",), ("dt_tm",)], w=[("x2", par)])
                    pool(lambda e, base=base, slot=slot, tile=tile: e.tensor_tensor(
                        xdb[base:base + 64, :].rearrange("p (h q) -> p h q", q=HD),
                        _mkv(xr[base:base + 64, :], [(64, NH), (1, HD)], slot * 2048),
                        _mkv(dtds[base:base + 64, :], [(1, NH), (0, HD)], tile * 32),
                        ALU.mult),
                        r=[("xr",), ("dtds",)], w=[("xd", par)])
                    for n in range(4):
                        pe(lambda e, base=base, slot=slot, n=n: e.matmul(
                            PS[0:64, 2048 + n * 512: 2048 + (n + 1) * 512],
                            bct[base:base + 64, slot * 128: slot * 128 + 64],
                            xdb[base:base + 64, n * 512:(n + 1) * 512],
                            start=True, stop=True, tile_position=(base, 0)),
                           r=[("bct",), ("xd", par)], w=[("PS", 4 + n)])
                    if main:
                        pe(lambda e, c=c, base=base: e.matmul(
                            PS[base:base + 64, 0:64],
                            bc_fm[0:64, c * 64:(c + 1) * 64],
                            c_fm0[:, c * 64:(c + 1) * 64],
                            start=True, stop=True, tile_position=(0, base)),
                           r=[("bcfm",), ("cfm0",)], w=[("PS", 0)])
                        dve(lambda e, base=base: e.tensor_tensor_scan(
                            lt_b[base:base + 64, :], gr_b[base:base + 64, :],
                            i64rep[base:base + 64, :], 0.0, ALU.mult, ALU.add),
                            r=[("gr", par), ("i64",)], w=[("lt", par)])
                        dve(lambda e, base=base: e.tensor_tensor(
                            gl_b[base:base + 64, :].rearrange("p (h l) -> p h l", l=64),
                            lt_b[base:base + 64, :].rearrange("p (h l) -> p h l", l=64),
                            _mkv(PS[base:base + 64, :], [(0, NH), (1, 64)], 0),
                            ALU.mult),
                            r=[("lt", par), ("PS", 0)], w=[("gl", par)])
                        for h in range(NH):
                            pe(lambda e, c=c, base=base, h=h: e.matmul(
                                PS[base:base + 64, h * 64:(h + 1) * 64],
                                c_fm0[:, c * 64:(c + 1) * 64],
                                (S0, S1)[c % 2][:, h * 64:(h + 1) * 64],
                                start=(h % 8 == 0), stop=False, tile_position=(0, base)),
                               r=[("cfm0",), kSc], w=[("PS", h // 8)])
                        dve(lambda e, base=base, tile=tile: e.tensor_tensor(
                            PS[base:base + 64, 0:2048].rearrange("p (h q) -> p h q", q=HD),
                            PS[base:base + 64, 0:2048].rearrange("p (h q) -> p h q", q=HD),
                            _mkv(sdo_sb[base:base + 64, :], [(1, NH), (0, HD)], tile * 32),
                            ALU.mult),
                            r=psk(0, 2048) + [("sdo",)], w=psk(0, 2048))
                        for h in range(NH):
                            pe(lambda e, base=base, h=h: e.matmul(
                                PS[base:base + 64, h * 64:(h + 1) * 64],
                                gl_b[base:base + 64, h * 64:(h + 1) * 64],
                                x2b[base:base + 64, h * 64:(h + 1) * 64],
                                start=False, stop=False, tile_position=(base, base)),
                               r=[("gl", par), ("x2", par)], w=[("PS", h // 8)])
                        for h in range(NH):
                            pe(lambda e, base=base, h=h, slot=slot: e.matmul(
                                PS[base:base + 64, h * 64:(h + 1) * 64],
                                diagD[base:base + 64, h * 64:(h + 1) * 64],
                                _mkv(xr[base:base + 64, :], [(1, 64)], slot * 2048 + h * 64),
                                start=False, stop=True, tile_position=(base, base)),
                               r=[("diagD",), ("xr",)], w=[("PS", h // 8)])
                    pool(lambda e, c=c: e.tensor_tensor(
                        (S0, S1)[(c + 1) % 2][:].rearrange("p (h q) -> p h q", q=HD),
                        (S0, S1)[c % 2][:].rearrange("p (h q) -> p h q", q=HD),
                        _mkv(la[:], [(1, NH), (0, HD)], c * 32),
                        ALU.mult),
                        r=[kSc, ("la",)], w=[kSn])
                    dve(lambda e, c=c: e.tensor_add(
                        (S0, S1)[(c + 1) % 2][:], (S0, S1)[(c + 1) % 2][:],
                        PS[0:64, 2048:4096]),
                        r=[kSn] + psk(2048, 4096), w=[kSn])
                    if main:
                        dve(lambda e, base=base, slot=slot: e.tensor_tensor(
                            _mkv(yr[base:base + 64, :], [(1, 2048)], slot * 2048),
                            PS[base:base + 64, 0:2048], zb_b[base:base + 64, :],
                            ALU.mult),
                            r=psk(0, 2048) + [("zb", par)], w=[("yr", slot)])

                if main and tb == 0:
                    tap("yr0", yr[:], [("yr", 0), ("yr", 1), ("yr", 2), ("yr", 3)])
                    tap("S8", S0[:], [("S", 0)])
                    tap("lt7", lt_b[:], [("lt", 0), ("lt", 1)])
                    tap("gl7", gl_b[:], [("gl", 0), ("gl", 1)])
                    tap("x2b7", x2b[:], [("x2", 0), ("x2", 1)])
                    tap("xdb7", xdb[:], [("xd", 0), ("xd", 1)])
                if main:
                    for q in range(4):
                        tt = tb * 4 + q
                        yk = ("yr", q)
                        act(lambda e, q=q: e.activation(
                            sq_scr[:].bitcast(BF16)[:, 0:2048],
                            yr[:, q * 2048:(q + 1) * 2048], AF.Square,
                            accum_out=stats[:, 8:9]),
                            r=[yk], w=[("sq",), ("st8",)])
                        dve(lambda e: e.tensor_scalar(stats[:, 9:10], stats[:, 8:9],
                                                      1.0 / DI, 1e-5, ALU.mult, ALU.add),
                            r=[("st8",)], w=[("st9",)])
                        act(lambda e: e.activation(stats[:, 10:11], stats[:, 9:10], AF.Sqrt),
                            r=[("st9",)], w=[("st10",)])
                        dve(lambda e: e.reciprocal(stats[:, 11:12], stats[:, 10:11]),
                            r=[("st10",)], w=[("st11",)])
                        dve(lambda e, q=q: e.tensor_scalar_mul(
                            yr[:, q * 2048:(q + 1) * 2048],
                            yr[:, q * 2048:(q + 1) * 2048], stats[:, 11:12]),
                            r=[yk, ("st11",)], w=[yk], sync=True)
                        for kb in range(0, 16, 4):
                            bank = 6 + (kb // 4) % 2
                            for k in range(kb, kb + 4):
                                pe(lambda e, q=q, k=k, kb=kb, bank=bank: e.transpose(
                                    PSB[:, bank * 1024 + (k - kb) * 128:
                                        bank * 1024 + (k - kb + 1) * 128],
                                    yr[:, q * 2048 + k * 128: q * 2048 + (k + 1) * 128],
                                    id_bf[:]),
                                   r=[yk, ("id",)], w=[("PS", bank)])
                            dve(lambda e, kb=kb, bank=bank: e.tensor_copy(
                                yft[:, kb * 128:(kb + 4) * 128],
                                PSB[:, bank * 1024: bank * 1024 + 512]),
                                r=[("PS", bank)], w=[("yft",)])
                        sdma(lambda e, tt=tt: e.dma_start(
                            out=bass.AP(yfm_d.tensor, yfm_d.offset + tt * 128,
                                        [[T, 128], [128 * T, 16], [1, 128]]),
                            in_=yft[:].rearrange("p (k c) -> p k c", c=128)),
                            r=[("yft",)], w=[("yfm", tt)])

        half_pipeline(up_in, xnp, ("xnp",), main=False)
        pb.barrier()
        half_pipeline(u_in, xn, ("xn",), main=True)
        tapd("yfm_d", yfm_d, [("yfm", t) for t in range(NT)])
        pb.barrier()

        gdma(lambda e: e.dma_start(out=b2_bf[:], in_=b2_in), w=[("b2",)])
        for th in range(2):
            for q in range(8):
                tt = th * 8 + q
                pcol = 1024 + (q % 2) * 1024
                pkeys = psk(pcol, pcol + 1024)
                for n in range(2):
                    pe(lambda e, tt=tt, n=n, pcol=pcol: e.matmul(
                        PS[:, pcol + n * 512: pcol + (n + 1) * 512],
                        wgtf[:, tt * 128:(tt + 1) * 128],
                        b2_bf[:, n * 512:(n + 1) * 512],
                        start=True, stop=True),
                       r=[("wgtf",), ("b2",)], w=[pkeys[n]])
                dve(lambda e, q=q, pcol=pcol: e.tensor_copy(
                    ma[:, q * DM:(q + 1) * DM], PS[:, pcol:pcol + 1024]),
                    r=pkeys, w=[("ma", q)])
                if DEBUG and th == 0 and q == 0:
                    tap("mab2", ma[:, 0:1024], [("ma", 0)])
            for ee in range(8):
                for qq in range(4):
                    wp = qq % 2
                    k1, k2 = ("w1q", wp), ("w2q", wp)
                    gdma(lambda e, ee=ee, qq=qq, wp=wp: e.dma_start(
                        out=w1q[wp][:].rearrange("p (k j) -> p k j", j=1024),
                        in_=bass.AP(w1_in.tensor,
                                    w1_in.offset + ee * DM * 4096 + qq * 1024,
                                    [[4096, 128], [128 * 4096, 8], [1, 1024]])),
                        w=[k1])
                    gdma(lambda e, ee=ee, qq=qq, wp=wp: e.dma_start(
                        out=w2q[wp][:].rearrange("p (k j) -> p k j", j=1024),
                        in_=bass.AP(w2_in.tensor,
                                    w2_in.offset + ee * 4096 * DM + qq * 1024 * DM,
                                    [[DM, 128], [128 * DM, 8], [1, 1024]])),
                        w=[k2])
                    gdma(lambda e, ee=ee, qq=qq: e.dma_start(
                        out=b1q[:], in_=b1_in[ee:ee + 1, qq * 1024:(qq + 1) * 1024]),
                        w=[("b1",)])
                    def w1_block(t2, wp=wp, k1=k1):
                        tok0 = th * 1024 + t2 * 256
                        hp = t2 % 2
                        hqb = (hq0, hq1)[hp]
                        hk = ("hq", hp)
                        for wv in range(4):
                            pw = wv % 2
                            pcol = pw * 512
                            for mj in (wv * 2, wv * 2 + 1):
                                col = pcol + (mj - wv * 2) * 256
                                for k in range(KD):
                                    pe(lambda e, k=k, mj=mj, col=col, tok0=tok0, wp=wp: e.matmul(
                                        PS[:, col:col + 256],
                                        w1q[wp][:, k * 1024 + mj * 128: k * 1024 + (mj + 1) * 128],
                                        _mkv(xn[:], [(1, 256)], xn_col(k, tok0)),
                                        start=(k == 0 and mj % 2 == 0), stop=False),
                                       r=[k1, ("xn",)], w=[("PS", pw)])
                                pe(lambda e, mj=mj, col=col: e.matmul(
                                    PS[:, col:col + 256],
                                    b1q[0:1, mj * 128:(mj + 1) * 128],
                                    ones_bf[0:1, 0:256],
                                    start=False, stop=(mj % 2 == 1)),
                                   r=[("b1",), ("ones",)], w=[("PS", pw)])
                            act(lambda e, wv=wv, pcol=pcol, hqb=hqb, pw=pw: e.activation(
                                hqb[:, wv * 512:(wv + 1) * 512], PS[:, pcol:pcol + 512],
                                AF.Gelu_apprx_tanh),
                                r=[("PS", pw)], w=[hk])

                    def w2_block(t2, wp=wp, k2=k2, ee=ee, qq=qq):
                        hp = t2 % 2
                        hqb = (hq0, hq1)[hp]
                        hk = ("hq", hp)
                        for tsub in range(2):
                            ocol = 2048 + tsub * 1024
                            okeys = psk(ocol, ocol + 1024)
                            for mj in range(8):
                                for n in range(2):
                                    pe(lambda e, mj=mj, n=n, tsub=tsub, ocol=ocol, wp=wp,
                                       qq=qq, hqb=hqb: e.matmul(
                                        PS[:, ocol + n * 512: ocol + (n + 1) * 512],
                                        hqb[:, mj * 256 + tsub * 128: mj * 256 + (tsub + 1) * 128],
                                        w2q[wp][:, mj * 1024 + n * 512: mj * 1024 + (n + 1) * 512],
                                        start=(mj == 0), stop=(mj == 7)),
                                       r=[hk, k2], w=[okeys[n]])
                            tt = th * 8 + t2 * 2 + tsub
                            qm = t2 * 2 + tsub
                            dve(lambda e, tt=tt, qm=qm, ocol=ocol, ee=ee: e.scalar_tensor_tensor(
                                ma[:, qm * DM:(qm + 1) * DM],
                                PS[:, ocol:ocol + 1024],
                                wgt[:, tt * 8 + ee: tt * 8 + ee + 1],
                                ma[:, qm * DM:(qm + 1) * DM],
                                ALU.mult, ALU.add),
                                r=okeys + [("wgt",), ("ma", qm)], w=[("ma", qm)])

                    for t2 in range(5):
                        if t2 < 4:
                            w1_block(t2)
                        if t2 >= 1:
                            w2_block(t2 - 1)
            if DEBUG and th == 0:
                tap("ma0", ma[:], [("ma", q) for q in range(8)])
            pb.barrier()
            gdma(lambda e: e.dma_start(
                out=wout_bf[:].rearrange("p (k c) -> p k c", c=1024),
                in_=woutT.rearrange("(k p) c -> p k c", p=128)),
                w=[("woutbf",)])
            for k in range(16):
                dve(lambda e, k=k: e.tensor_scalar_mul(
                    wout_bf[:, k * 1024:(k + 1) * 1024],
                    wout_bf[:, k * 1024:(k + 1) * 1024], nw_sb[:, k:k + 1]),
                    r=[("woutbf",), ("nw",)], w=[("woutbf",)])
            for q in range(8):
                tt = th * 8 + q
                par = q % 2
                pcol = par * 1024
                pkeys = psk(pcol, pcol + 1024)
                ub = (u_sb0, u_sb1)[par]
                ob = (out_sb0, out_sb1)[par]
                uk = ("u0",)
                ok = (("o0",), ("o1",))[par]
                sdma(lambda e, tt=tt: e.dma_start(
                    out=yft2[:].rearrange("p (k c) -> p k c", c=128),
                    in_=bass.AP(yfm_d.tensor, yfm_d.offset + tt * 128,
                                [[T, 128], [128 * T, 16], [1, 128]])),
                    r=[("yfm", tt)], w=[("yft2",)])
                sdma(lambda e, tt=tt, ub=ub: e.dma_start(
                    out=ub[:], in_=u_in[tt * 128:(tt + 1) * 128, :]), w=[uk])
                for k in range(16):
                    for n in range(2):
                        pe(lambda e, k=k, n=n, pcol=pcol: e.matmul(
                            PS[:, pcol + n * 512: pcol + (n + 1) * 512],
                            yft2[:, k * 128:(k + 1) * 128],
                            wout_bf[:, k * 1024 + n * 512: k * 1024 + (n + 1) * 512],
                            start=(k == 0), stop=(k == 15)),
                           r=[("yft2",), ("woutbf",)], w=[pkeys[n]])
                dve(lambda e, q=q, ub=ub, ob=ob: e.tensor_add(
                    ob[:], ub[:], ma[:, q * DM:(q + 1) * DM]),
                    r=[uk, ("ma", q)], w=[ok])
                dve(lambda e, ob=ob, pcol=pcol: e.tensor_add(
                    ob[:], ob[:], PS[:, pcol:pcol + 1024]),
                    r=[ok] + pkeys, w=[ok])
                sdma(lambda e, tt=tt, ob=ob: e.dma_start(
                    out=out_d[tt * 128:(tt + 1) * 128, :], in_=ob[:]),
                    r=[ok], w=[("outd", tt)], tag="out")
            if th == 0:
                pb.barrier()

        pb.finalize(block, sems)
    return nc


_NC_CACHE = None


def _get_nc():
    global _NC_CACHE
    if _NC_CACHE is None:
        _NC_CACHE = build_nc()
    return _NC_CACHE


def _consts():
    ident = np.eye(128, dtype=np.float32)
    trib = np.zeros((128, 128), np.float32)
    totb = np.zeros((128, 128), np.float32)
    for h2 in range(2):
        sl = slice(h2 * 64, (h2 + 1) * 64)
        trib[sl, sl] = np.triu(np.ones((64, 64), np.float32))
        totb[sl, sl] = 1.0
    i64 = np.tile(np.eye(64, dtype=np.float32), (2, 32))
    ones = np.ones((1, 512), np.float32)
    return ident, trib, totb, i64, ones


def kernel(L0_attn_u, L0_attn_in_proj_w, L0_attn_conv_w, L0_attn_conv_b,
           L0_attn_init_states, L0_attn_dt_bias, L0_attn_A_log, L0_attn_D,
           L0_attn_norm_w, L0_attn_out_proj_w, L0_ffn_w_router, L0_ffn_w1,
           L0_ffn_b1, L0_ffn_w2, L0_ffn_b2):
    f = lambda x: np.ascontiguousarray(np.asarray(x, dtype=np.float32))
    u = f(L0_attn_u)
    winT = np.ascontiguousarray(f(L0_attn_in_proj_w).T)
    init_s = np.ascontiguousarray(
        f(L0_attn_init_states).transpose(2, 0, 1).reshape(64, NH * HD))
    woutT = np.ascontiguousarray(f(L0_attn_out_proj_w).T)
    ident, trib, totb, i64, ones = _consts()
    gmask = np.ones((NH, T), np.float32)
    gmask[:, ::64] = 0.0

    convb_st = np.ascontiguousarray(f(L0_attn_conv_b).reshape(NSLAB, 128).T)
    nw_st = np.ascontiguousarray(f(L0_attn_norm_w).reshape(16, 128).T)
    shared = dict(winT=winT, convw=f(L0_attn_conv_w), convb=convb_st,
                  dtb=f(L0_attn_dt_bias), alog=f(L0_attn_A_log), dd=f(L0_attn_D),
                  nw=nw_st, init=init_s, woutT=woutT,
                  wrout=f(L0_ffn_w_router), w1=f(L0_ffn_w1), b1=f(L0_ffn_b1),
                  w2=f(L0_ffn_w2), b2=f(L0_ffn_b2), c_ident=ident, c_trib=trib,
                  c_totb=totb, c_i64=i64, c_ones=ones, c_gmask=gmask)
    in_maps = []
    zeros_u = np.zeros((T, DM), np.float32)
    for b in range(4):
        for half in range(2):
            m = dict(shared)
            m["u"] = np.ascontiguousarray(u[b, half * T:(half + 1) * T])
            m["up"] = np.ascontiguousarray(u[b, 0:T]) if half == 1 else zeros_u
            s = 1.0 if half == 1 else 0.0
            m["sel"] = np.full((64, 1), s, np.float32)
            m["sel1m"] = np.full((64, 1), 1.0 - s, np.float32)
            in_maps.append(m)

    nc = _get_nc()
    res = run_bass_kernel_spmd(nc, in_maps, core_ids=list(range(8)))
    out = np.empty((4, 4096, 1024), np.float32)
    for b in range(4):
        for half in range(2):
            out[b, half * T:(half + 1) * T] = res.results[b * 2 + half]["out"]
    return out




# version 08
# speedup vs baseline: 1.0886x; 1.0529x over previous
import numpy as np
import concourse.bass as bass
import concourse.mybir as mybir
from concourse.bass_utils import run_bass_kernel_spmd
from contextlib import ExitStack

F32 = mybir.dt.float32
BF16 = mybir.dt.bfloat16
AF = mybir.ActivationFunctionType
ALU = mybir.AluOpType
AX = mybir.AxisListType

T = 2048
NT = 16
NTB = 4
NCH = 32
CH = 64
NH = 32
HD = 64
DM = 1024
DI = 2048
KD = 8
NSLAB = 17
XN_W = 2051

DEBUG = False
STREAMS = ("pe", "dve", "act", "pool", "sdma", "gdma")
BODY = {"pe": "tensor", "dve": "vector", "act": "scalar",
        "pool": "gpsimd", "gdma": "gpsimd", "sdma": "sync"}


class PB:
    def __init__(self):
        self.ops = []
        self.lastw = {}
        self.readers = {}

    def emit(self, stream, fn, r=(), w=(), dep=(), tag="", sync=False):
        i = len(self.ops)
        deps = set(dep)
        for k in r:
            lw = self.lastw.get(k)
            if lw is not None:
                deps.add(lw)
        for k in w:
            lw = self.lastw.get(k)
            if lw is not None:
                deps.add(lw)
            deps.update(self.readers.get(k, ()))
        self.ops.append(dict(stream=stream, fn=fn, deps=deps, tag=tag, sync=sync,
                             is_dma=stream in ("sdma", "gdma")))
        for k in r:
            self.readers.setdefault(k, set()).add(i)
        for k in w:
            self.lastw[k] = i
            self.readers[k] = set()
        return i

    def barrier(self):
        last = {}
        for i, op in enumerate(self.ops):
            last[op["stream"]] = i
        for s in STREAMS:
            deps = [last[t] for t in STREAMS if t != s and t in last]
            self.emit(s, lambda e: e.nop(), dep=deps, tag="bar")

    def finalize(self, block, sems):
        ops = self.ops
        need = [False] * len(ops)
        for i, op in enumerate(ops):
            kept = []
            for d in op["deps"]:
                dop = ops[d]
                same_body = BODY[dop["stream"]] == BODY[op["stream"]]
                if same_body and not dop["is_dma"] and not op["sync"]:
                    continue
                kept.append(d)
                need[d] = True
            op["kept"] = kept
        for i, op in enumerate(ops):
            if op["tag"] == "out" or op["is_dma"]:
                need[i] = True
        cnt = {s: 0 for s in STREAMS}
        for i, op in enumerate(ops):
            if need[i]:
                cnt[op["stream"]] += 16 if op["is_dma"] else 1
            op["val"] = cnt[op["stream"]]
            op["inc"] = need[i]
        final_sd = cnt["sdma"]

        body_ops = {}
        for i, op in enumerate(ops):
            body_ops.setdefault(BODY[op["stream"]], []).append(i)

        def run_body(eng, idxs):
            water = {}
            for i in idxs:
                op = ops[i]
                for d in op["kept"]:
                    dop = ops[d]
                    v = dop["val"]
                    if water.get(dop["stream"], 0) < v:
                        eng.wait_ge(sems[dop["stream"]], v)
                        water[dop["stream"]] = v
                ins = op["fn"](eng)
                if op["inc"]:
                    ins.then_inc(sems[op["stream"]], 16 if op["is_dma"] else 1)

        @block.tensor
        def _(t):
            run_body(t, body_ops.get("tensor", []))

        @block.vector
        def _(v):
            run_body(v, body_ops.get("vector", []))

        @block.scalar
        def _(s):
            run_body(s, body_ops.get("scalar", []))

        @block.gpsimd
        def _(g):
            run_body(g, body_ops.get("gpsimd", []))

        @block.sync
        def _(sy):
            run_body(sy, body_ops.get("sync", []))
            if final_sd:
                sy.wait_ge(sems["sdma"], final_sd)


def _mkv(base_ap, dims, off):
    newap = [list(base_ap.ap[0])] + [[s, c] for (s, c) in dims]
    return bass.AP(base_ap.tensor, base_ap.offset + off, newap)


def build_nc():
    nc = bass.Bass("TRN2", target_bir_lowering=False)
    din = lambda n, s: nc.dram_tensor(n, s, F32, kind="ExternalInput").ap()

    u_in = din("u", [T, DM])
    up_in = din("up", [T, DM])
    winT = din("winT", [DM, 4256])
    convw = din("convw", [2176, 4])
    convb = din("convb", [128, NSLAB])
    dtb = din("dtb", [NH])
    alog = din("alog", [NH])
    dd = din("dd", [NH])
    nw = din("nw", [128, 16])
    init_in = din("init", [64, NH * HD])
    woutT = din("woutT", [DI, DM])
    wrout = din("wrout", [DM, 8])
    w1_in = din("w1", [8, DM, 4096])
    b1_in = din("b1", [8, 128, 32])
    w2_in = din("w2", [8, 4096, DM])
    b2_in = din("b2", [8, DM])
    c_ident = din("c_ident", [128, 128])
    c_trib = din("c_trib", [128, 128])
    c_totb = din("c_totb", [128, 128])
    c_i64 = din("c_i64", [128, DI])
    c_ones = din("c_ones", [1, 512])
    c_gmask = din("c_gmask", [NH, T])
    sel_in = din("sel", [64, 1])
    sel1m_in = din("sel1m", [64, 1])

    out_d = nc.dram_tensor("out", [T, DM], F32, kind="ExternalOutput").ap()

    zs_d = nc.dram_tensor("zs_d", [T, DI], BF16, kind="Internal").ap()
    yfm_d = nc.dram_tensor("yfm_d", [DI, T], BF16, kind="Internal").ap()
    g_d = nc.dram_tensor("g_d", [NH, T], F32, kind="Internal").ap()
    tot_d = nc.dram_tensor("tot_d", [1, NCH * NH], F32, kind="Internal").ap()
    a_d = nc.dram_tensor("a_d", [1, NH], F32, kind="Internal").ap()

    pb = PB()
    with ExitStack() as ctx:
        sb = lambda n, s, d: ctx.enter_context(nc.sbuf_tensor(n, s, d))
        xn = sb("xn", [128, KD * XN_W], BF16)
        arena1 = sb("arena1", [128, KD * XN_W], BF16)
        arena2 = sb("arena2", [128, 32768], BF16)
        arena3 = sb("arena3", [128, 10240], F32)
        dtraw = arena3[:, 0:512]
        scrA = arena3[:, 512:1024]
        scrB = arena3[:, 1024:1536]
        dt_tm = arena3[:, 1536:2048]
        cs_sb = arena3[:, 2048:2560]
        tot_sb = arena3[:, 2560:3072]
        sdo_sb = arena3[:, 3072:3584]
        ds_sb = arena3[:, 3584:4096]
        dtds = arena3[:, 4096:4608]
        la = arena3[0:64, 4608:5632]
        dtb_rep = sb("dtb_rep", [128, NH], F32)
        A_rep = sb("A_rep", [128, NH], F32)
        D_rep = sb("D_rep", [128, NH], F32)
        arow = sb("arow", [1, NH], F32)
        acolf = sb("acolf", [NH, 1], F32)
        dtbcol = sb("dtbcol", [NH, 1], F32)
        cw_sb = sb("cw_sb", [128, NSLAB * 4], F32)
        cb_sb = sb("cb_sb", [128, NSLAB], F32)
        pre0 = sb("pre0", [128, 515], BF16)
        pre1 = sb("pre1", [128, 515], BF16)
        acc0 = sb("acc0", [128, 512], F32)
        acc1 = acc0
        xs0 = sb("xs0", [128, 512], BF16)
        xs1 = sb("xs1", [128, 512], BF16)
        wsl0 = sb("wsl0", [128, 1024], BF16)
        wsl1 = sb("wsl1", [128, 1024], BF16)
        bc_fm = sb("bc_fm", [128, T], BF16)
        c_fm0 = sb("c_fm0", [64, T], BF16)
        bct = sb("bct", [128, 512], BF16)
        u_sb0 = sb("u_sb0", [128, DM], F32)
        u_sb1 = u_sb0
        sq_scr = sb("sq_scr", [128, DM], F32)
        stats = sb("stats", [128, 16], F32)
        id_bf = sb("id_bf", [128, 128], BF16)
        trib = sb("trib", [128, 128], F32)
        totb = sb("totb", [128, 128], F32)
        i64rep = sb("i64rep", [128, DI], BF16)
        diagD = sb("diagD", [128, DI], BF16)
        ones_bf = sb("ones_bf", [1, 512], BF16)
        sel_sb = sb("sel_sb", [64, 1], F32)
        sel1m_sb = sb("sel1m_sb", [64, 1], F32)
        w_dt = sb("w_dt", [128, KD * 32], BF16)
        w_r = sb("w_r", [128, KD * 8], BF16)
        logits = arena3[:, 1152:1280]
        probs = arena3[:, 1280:1408]
        cnt_sb = arena3[:, 1024:1152]
        tmpc = sb("tmpc", [128, 16], F32)
        wgt = arena3[:, 1408:1536]
        wgt_bf = sb("wgt_bf", [128, 128], BF16)
        a3b = arena3[:].bitcast(BF16)
        S0 = a3b[0:64, 11264:13312]
        S1 = a3b[0:64, 13312:15360]
        wgtf = a3b[0:8, 0:2048]
        b2_bf = a3b[0:8, 8192:9216]
        b1c = arena3[:, 9728:9760]
        hq0 = a3b[:, 4096:8192]
        hq1 = a3b[:, 9216:13312]
        nw_sb = sb("nw_sb", [128, 16], F32)
        PS = ctx.enter_context(nc.psum_tensor("PS", [128, 4096], F32))
        PSB = PS.bitcast(BF16)

        sems = {s: ctx.enter_context(nc.semaphore("sem_" + s)) for s in STREAMS}
        block = ctx.enter_context(nc.Block())

        xnp = arena1[:, 0:KD * XN_W]
        a1f = arena1[:].bitcast(F32)
        dtfm_a = a1f[:, 0:2048][0:NH, :]
        dtfm_b = a1f[:, 2048:4096][0:NH, :]
        gmask_sb = a1f[:, 4096:6144][0:NH, :]
        ma = a1f[:, 0:8192]
        xr = arena2[:, 0:8192]
        yr = arena2[:, 8192:16384]
        lt_b = arena2[:, 16384:18432]
        bct1 = arena2[:, 17408:17920]
        gl_b = arena2[:, 18432:20480]
        zb_b = arena2[:, 20480:22528]
        x2b = arena2[:, 22528:24576]
        xdb = arena2[:, 24576:26624]
        gr_b = arena2[:].bitcast(F32)[:, 13312:15360]
        yft = arena2[:, 30720:32768]
        w1q = (arena2[:, 0:8192], arena2[:, 8192:16384])
        w2q = (arena2[:, 16384:24576], arena2[:, 24576:32768])
        w_z = arena2[:, 0:16384]
        zs_b0 = arena2[:, 16384:18432]
        zs_b1 = arena2[:, 18432:20480]
        wout_bf = arena2[:, 0:16384]
        yft2 = arena2[:, 16384:18432]
        out_sb0 = arena2[:].bitcast(F32)[:, 9216:10240]
        out_sb1 = arena2[:].bitcast(F32)[:, 10240:11264]

        def dve(fn, r=(), w=(), tag="", sync=False):
            return pb.emit("dve", fn, r, w, tag=tag, sync=sync)

        def act(fn, r=(), w=()):
            return pb.emit("act", fn, r, w)

        def pool(fn, r=(), w=()):
            return pb.emit("pool", fn, r, w)

        def pe(fn, r=(), w=()):
            return pb.emit("pe", fn, r, w)

        def sdma(fn, r=(), w=(), tag=""):
            return pb.emit("sdma", fn, r, w, tag=tag)

        def gdma(fn, r=(), w=()):
            return pb.emit("gdma", fn, r, w)

        def psk(lo, hi):
            return [("PS", b) for b in range(lo // 512, (hi + 511) // 512)]

        def tap(name, ap, r, dt=None):
            if not DEBUG:
                return
            shape = list(ap.shape)
            t = nc.dram_tensor("dbg_" + name, shape, ap.dtype, kind="ExternalOutput").ap()
            sdma(lambda e, ap=ap, t=t: e.dma_start(out=t, in_=ap), r=r, tag="out")

        def tapd(name, dram_ap, r):
            if not DEBUG:
                return
            shape = list(dram_ap.shape)
            t = nc.dram_tensor("dbg_" + name, shape, dram_ap.dtype, kind="ExternalOutput").ap()
            sdma(lambda e, t=t, dram_ap=dram_ap: e.dma_start(out=t, in_=dram_ap), r=r, tag="out")

        gdma(lambda e: e.dma_start(out=id_bf[:], in_=c_ident), w=[("id",)])
        sdma(lambda e: e.dma_start(out=trib[:], in_=c_trib), w=[("trib",)])
        sdma(lambda e: e.dma_start(out=totb[:], in_=c_totb), w=[("totb",)])
        gdma(lambda e: e.dma_start(out=i64rep[:], in_=c_i64), w=[("i64",)])
        gdma(lambda e: e.dma_start(out=ones_bf[:], in_=c_ones), w=[("ones",)])
        sdma(lambda e: e.dma_start(out=sel_sb[:], in_=sel_in), w=[("sel",)])
        sdma(lambda e: e.dma_start(out=sel1m_sb[:], in_=sel1m_in), w=[("sel1m",)])
        sdma(lambda e: e.dma_start(out=dtb_rep[:], in_=dtb.unsqueeze(0).to_broadcast([128, NH])),
             w=[("dtbrep",)])
        sdma(lambda e: e.dma_start(out=dtbcol[:], in_=dtb.unsqueeze(1)), w=[("dtbcol",)])
        sdma(lambda e: e.dma_start(out=D_rep[:], in_=dd.unsqueeze(0).to_broadcast([128, NH])),
             w=[("drep",)])
        sdma(lambda e: e.dma_start(out=arow[:], in_=alog.unsqueeze(0)), w=[("arow",)])
        sdma(lambda e: e.dma_start(out=acolf[:], in_=alog.unsqueeze(1)), w=[("acolf",)])
        gdma(lambda e: e.dma_start(out=w_dt[:].rearrange("p (k h) -> p k h", h=32),
                                   in_=winT[:, 4224:4256].rearrange("(k p) h -> p k h", p=128)),
             w=[("wdt",)])
        gdma(lambda e: e.dma_start(out=w_r[:].rearrange("p (k h) -> p k h", h=8),
                                   in_=wrout.rearrange("(k p) h -> p k h", p=128)),
             w=[("wr",)])
        sdma(lambda e: e.dma_start(out=cw_sb[:].rearrange("p (s k) -> p s k", k=4),
                                   in_=convw.rearrange("(s p) k -> p s k", p=128)),
             w=[("cw",)])
        sdma(lambda e: e.dma_start(out=cb_sb[:], in_=convb), w=[("cb",)])
        sdma(lambda e: e.dma_start(out=nw_sb[:], in_=nw), w=[("nw",)])
        act(lambda e: e.activation(arow[:], arow[:], AF.Exp), r=[("arow",)], w=[("arow",)])
        dve(lambda e: e.tensor_scalar_mul(arow[:], arow[:], -1.0), r=[("arow",)], w=[("arow",)])
        sdma(lambda e: e.dma_start(out=a_d, in_=arow[:]), r=[("arow",)], w=[("a_d",)])
        sdma(lambda e: e.dma_start(out=A_rep[:], in_=a_d.to_broadcast([128, NH])),
             r=[("a_d",)], w=[("arep",)])
        act(lambda e: e.activation(acolf[:], acolf[:], AF.Exp), r=[("acolf",)], w=[("acolf",)])
        dve(lambda e: e.tensor_scalar_mul(acolf[:], acolf[:], -1.0), r=[("acolf",)], w=[("acolf",)])
        dve(lambda e: e.tensor_tensor(
            diagD[:].rearrange("p (h l) -> p h l", l=HD),
            i64rep[:].rearrange("p (h l) -> p h l", l=HD),
            D_rep[:].unsqueeze(2).to_broadcast([128, NH, HD]), ALU.mult),
            r=[("i64",), ("drep",)], w=[("diagD",)])

        def xn_col(k, t):
            return k * XN_W + 3 + t

        def layernorm(u_ap, xn_buf, xnkey):
            ubufs = (u_sb0, u_sb1)
            ukeys = (("u0",), ("u0",))
            for tt in range(NT):
                ub, uk = ubufs[tt % 2], ukeys[tt % 2]
                sdma(lambda e, tt=tt, ub=ub: e.dma_start(
                    out=ub[:], in_=u_ap[tt * 128:(tt + 1) * 128, :]), w=[uk])
                dve(lambda e, ub=ub: e.reduce_sum(stats[:, 0:1], ub[:], axis=AX.X, negate=True),
                    r=[uk], w=[("st0",)])
                dve(lambda e: e.tensor_scalar_mul(stats[:, 0:1], stats[:, 0:1], 1.0 / DM),
                    r=[("st0",)], w=[("st0",)], sync=True)
                dve(lambda e, ub=ub: e.tensor_scalar_add(ub[:], ub[:], stats[:, 0:1]),
                    r=[uk, ("st0",)], w=[uk], sync=True)
                act(lambda e, ub=ub: e.activation(sq_scr[:], ub[:], AF.Square,
                                                  accum_out=stats[:, 1:2]),
                    r=[uk], w=[("sq",), ("st1",)])
                dve(lambda e: e.tensor_scalar(stats[:, 2:3], stats[:, 1:2], 1.0 / DM, 1e-5,
                                              ALU.mult, ALU.add),
                    r=[("st1",)], w=[("st2",)])
                act(lambda e: e.activation(stats[:, 3:4], stats[:, 2:3], AF.Sqrt),
                    r=[("st2",)], w=[("st3",)])
                dve(lambda e: e.reciprocal(stats[:, 4:5], stats[:, 3:4]),
                    r=[("st3",)], w=[("st4",)])
                dve(lambda e, ub=ub: e.tensor_scalar_mul(
                    sq_scr[:].bitcast(BF16)[:, 0:1024], ub[:], stats[:, 4:5]),
                    r=[uk, ("st4",)], w=[("sq",)], sync=True)
                for kb in range(0, KD, 4):
                    bank = 6 + (kb // 4)
                    for k in range(kb, kb + 4):
                        pe(lambda e, k=k, kb=kb, bank=bank: e.transpose(
                            PSB[:, bank * 1024 + (k - kb) * 128: bank * 1024 + (k - kb + 1) * 128],
                            sq_scr[:].bitcast(BF16)[:, k * 128:(k + 1) * 128], id_bf[:]),
                           r=[("sq",), ("id",)], w=[("PS", bank)])
                    dve(lambda e, tt=tt, kb=kb, bank=bank, xn_buf=xn_buf: e.tensor_copy(
                        _mkv(xn_buf[:], [(XN_W, 4), (1, 128)], kb * XN_W + 3 + tt * 128),
                        PSB[:, bank * 1024: bank * 1024 + 512]),
                        r=[("PS", bank)], w=[xnkey])

        def softplus_tm():
            dtbv = _mkv(dtb_rep[:], [(0, NT), (1, NH)], 0)
            dve(lambda e: e.tensor_tensor(scrA[:].rearrange("p (t h) -> p t h", h=NH),
                                          dtraw[:].rearrange("p (t h) -> p t h", h=NH),
                                          dtbv, ALU.add),
                r=[("dtraw",), ("dtbrep",)], w=[("scrA",)])
            act(lambda e: e.activation(scrB[:], scrA[:], AF.Abs),
                r=[("scrA",)], w=[("scrB",)])
            act(lambda e: e.activation(scrB[:], scrB[:], AF.Exp, scale=-1.0),
                r=[("scrB",)], w=[("scrB",)])
            act(lambda e: e.activation(scrB[:], scrB[:], AF.Ln, bias=1.0),
                r=[("scrB",)], w=[("scrB",)])
            dve(lambda e: e.tensor_scalar_max(scrA[:], scrA[:], 0.0),
                r=[("scrA",)], w=[("scrA",)])
            dve(lambda e: e.tensor_add(dt_tm[:], scrA[:], scrB[:]),
                r=[("scrA",), ("scrB",)], w=[("dt_tm",)])

        def dt_router_pass(xn_buf, xnkey, main):
            for tt in range(NT):
                for k in range(KD):
                    pe(lambda e, k=k, tt=tt, xn_buf=xn_buf: e.matmul(
                        PS[:, tt * 32:(tt + 1) * 32],
                        _mkv(xn_buf[:], [(1, 128)], xn_col(k, tt * 128)),
                        w_dt[:, k * 32:(k + 1) * 32],
                        start=(k == 0 and tt == 0), stop=(k == KD - 1)),
                       r=[xnkey, ("wdt",)], w=[("PS", 0)])
                    if main:
                        pe(lambda e, k=k, tt=tt, xn_buf=xn_buf: e.matmul(
                            PS[:, 1024 + tt * 8: 1024 + (tt + 1) * 8],
                            _mkv(xn_buf[:], [(1, 128)], xn_col(k, tt * 128)),
                            w_r[:, k * 8:(k + 1) * 8],
                            start=(k == 0 and tt == 0), stop=(k == KD - 1)),
                           r=[xnkey, ("wr",)], w=[("PS", 2)])
            dve(lambda e: e.tensor_copy(dtraw[:], PS[:, 0:512]), r=[("PS", 0)], w=[("dtraw",)])
            softplus_tm()
            arepv = _mkv(A_rep[:], [(0, NT), (1, NH)], 0)
            dve(lambda e: e.tensor_tensor(scrA[:].rearrange("p (t h) -> p t h", h=NH),
                                          dt_tm[:].rearrange("p (t h) -> p t h", h=NH),
                                          arepv, ALU.mult),
                r=[("dt_tm",), ("arep",)], w=[("scrA",)])
            for j in range(NT):
                pe(lambda e, j=j: e.matmul(PS[:, j * 32:(j + 1) * 32], trib[:],
                                           scrA[:, j * 32:(j + 1) * 32],
                                           start=(j == 0), stop=True),
                   r=[("trib",), ("scrA",)], w=[("PS", 0)])
                pe(lambda e, j=j: e.matmul(PS[:, 512 + j * 32: 512 + (j + 1) * 32], totb[:],
                                           scrA[:, j * 32:(j + 1) * 32],
                                           start=(j == 0), stop=True),
                   r=[("totb",), ("scrA",)], w=[("PS", 1)])
            dve(lambda e: e.tensor_copy(cs_sb[:], PS[:, 0:512]), r=[("PS", 0)], w=[("cs",)])
            dve(lambda e: e.tensor_copy(tot_sb[:], PS[:, 512:1024]), r=[("PS", 1)], w=[("tot",)])
            if main:
                act(lambda e: e.activation(sdo_sb[:], cs_sb[:], AF.Exp), r=[("cs",)], w=[("sdo",)])
            pool(lambda e: e.tensor_sub(ds_sb[:], tot_sb[:], cs_sb[:]),
                 r=[("tot",), ("cs",)], w=[("ds",)])
            act(lambda e: e.activation(ds_sb[:], ds_sb[:], AF.Exp), r=[("ds",)], w=[("ds",)])
            pool(lambda e: e.tensor_mul(dtds[:], dt_tm[:], ds_sb[:]),
                 r=[("dt_tm",), ("ds",)], w=[("dtds",)])
            sdma(lambda e: e.dma_start(
                out=bass.AP(tot_d.tensor, tot_d.offset, [[1, 1], [64, NT], [1, NH]]),
                in_=tot_sb[0:1, :].rearrange("o (j h) -> o j h", h=NH)),
                r=[("tot",)], w=[("tot_d",)])
            sdma(lambda e: e.dma_start(
                out=bass.AP(tot_d.tensor, tot_d.offset + NH, [[1, 1], [64, NT], [1, NH]]),
                in_=tot_sb[64:65, :].rearrange("o (j h) -> o j h", h=NH)),
                r=[("tot",)], w=[("tot_d",)])
            sdma(lambda e: e.dma_start(out=la[:], in_=tot_d.to_broadcast([64, NCH * NH])),
                 r=[("tot_d",)], w=[("la",)])
            act(lambda e: e.activation(la[:], la[:], AF.Exp), r=[("la",)], w=[("la",)])

        def router_softmax_wgt():
            dve(lambda e: e.tensor_copy(logits[:], PS[:, 1024:1152]),
                r=[("PS", 2)], w=[("logits",), ("scrB",)])
            for tt in range(NT):
                dve(lambda e, tt=tt: e.reduce_max(stats[:, 5:6], logits[:, tt * 8:(tt + 1) * 8],
                                                  axis=AX.X),
                    r=[("logits",)], w=[("st5",)])
                dve(lambda e, tt=tt: e.tensor_scalar_sub(probs[:, tt * 8:(tt + 1) * 8],
                                                         logits[:, tt * 8:(tt + 1) * 8],
                                                         stats[:, 5:6]),
                    r=[("logits",), ("st5",)], w=[("probs",)], sync=True)
            act(lambda e: e.activation(probs[:], probs[:], AF.Exp), r=[("probs",)], w=[("probs",)])
            for tt in range(NT):
                dve(lambda e, tt=tt: e.reduce_sum(stats[:, 6:7], probs[:, tt * 8:(tt + 1) * 8],
                                                  axis=AX.X),
                    r=[("probs",)], w=[("st6",)])
                dve(lambda e: e.reciprocal(stats[:, 7:8], stats[:, 6:7]), r=[("st6",)], w=[("st7",)],
                    sync=True)
                dve(lambda e, tt=tt: e.tensor_scalar_mul(probs[:, tt * 8:(tt + 1) * 8],
                                                         probs[:, tt * 8:(tt + 1) * 8],
                                                         stats[:, 7:8]),
                    r=[("probs",), ("st7",)], w=[("probs",)], sync=True)
            dve(lambda e: e.memset(cnt_sb[:], 0.0), w=[("cnt",), ("scrB",)])
            for ee in range(8):
                for e2 in range(8):
                    if e2 == ee:
                        continue
                    op = ALU.is_ge if e2 < ee else ALU.is_gt
                    dve(lambda e, ee=ee, e2=e2, op=op: e.tensor_tensor(
                        tmpc[:], _mkv(probs[:], [(8, NT)], e2),
                        _mkv(probs[:], [(8, NT)], ee), op),
                        r=[("probs",)], w=[("tmpc",)])
                    pool(lambda e, ee=ee: e.tensor_add(
                        _mkv(cnt_sb[:], [(8, NT)], ee),
                        _mkv(cnt_sb[:], [(8, NT)], ee), tmpc[:]),
                        r=[("cnt",), ("tmpc",)], w=[("cnt",)])
            dve(lambda e: e.tensor_single_scalar(wgt[:], cnt_sb[:], 1.5, ALU.is_lt),
                r=[("cnt",)], w=[("wgt",)])
            dve(lambda e: e.tensor_mul(wgt[:], wgt[:], probs[:]),
                r=[("wgt",), ("probs",)], w=[("wgt",)])
            dve(lambda e: e.tensor_copy(wgt_bf[:], wgt[:]), r=[("wgt",)], w=[("wgtbf",)])
            for tt in range(NT):
                bank = 6 + (tt % 2)
                pe(lambda e, tt=tt, bank=bank: e.transpose(
                    PSB[0:8, bank * 1024: bank * 1024 + 128],
                    wgt_bf[:, tt * 8:(tt + 1) * 8], id_bf[:]),
                   r=[("wgtbf",), ("id",)], w=[("PS", bank)])
                dve(lambda e, tt=tt, bank=bank: e.tensor_copy(
                    wgtf[:, tt * 128:(tt + 1) * 128],
                    PSB[0:8, bank * 1024: bank * 1024 + 128]),
                    r=[("PS", bank)], w=[("wgtf",), ("dtraw",), ("scrA",)])

        def dtfm_g_pass(xn_buf, xnkey):
            sdma(lambda e: e.dma_start(out=gmask_sb[:], in_=c_gmask),
                 w=[("gmask",), ("xnp",)])
            for k in range(KD):
                for n in range(4):
                    pe(lambda e, k=k, n=n, xn_buf=xn_buf: e.matmul(
                        PS[0:NH, n * 512:(n + 1) * 512],
                        w_dt[:, k * 32:(k + 1) * 32],
                        _mkv(xn_buf[:], [(1, 512)], xn_col(k, n * 512)),
                        start=(k == 0), stop=(k == KD - 1)),
                       r=[xnkey, ("wdt",)], w=[("PS", n)])
            dve(lambda e: e.tensor_copy(dtfm_a[:], PS[0:NH, 0:2048]),
                r=psk(0, 2048), w=[("dtfma",), ("xnp",)])
            tap("dtfmraw", dtfm_a[:], [("dtfma",)])
            dve(lambda e: e.tensor_scalar_add(dtfm_a[:], dtfm_a[:], dtbcol[:, 0:1]),
                r=[("dtfma",), ("dtbcol",)], w=[("dtfma",)])
            tap("dtfm1", dtfm_a[:], [("dtfma",)])
            act(lambda e: e.activation(dtfm_b[:], dtfm_a[:], AF.Abs),
                r=[("dtfma",)], w=[("dtfmb",), ("xnp",)])
            act(lambda e: e.activation(dtfm_b[:], dtfm_b[:], AF.Exp, scale=-1.0),
                r=[("dtfmb",)], w=[("dtfmb",)])
            act(lambda e: e.activation(dtfm_b[:], dtfm_b[:], AF.Ln, bias=1.0),
                r=[("dtfmb",)], w=[("dtfmb",)])
            tap("dtfm2", dtfm_b[:], [("dtfmb",)])
            dve(lambda e: e.tensor_scalar_max(dtfm_a[:], dtfm_a[:], 0.0),
                r=[("dtfma",)], w=[("dtfma",)])
            dve(lambda e: e.tensor_add(dtfm_a[:], dtfm_a[:], dtfm_b[:]),
                r=[("dtfma",), ("dtfmb",)], w=[("dtfma",)])
            tap("dtfm3", dtfm_a[:], [("dtfma",)])
            dve(lambda e: e.tensor_scalar_mul(dtfm_a[:], dtfm_a[:], acolf[:, 0:1]),
                r=[("dtfma",), ("acolf",)], w=[("dtfma",)])
            tap("dtfm4", dtfm_a[:], [("dtfma",)])
            act(lambda e: e.activation(dtfm_a[:], dtfm_a[:], AF.Exp),
                r=[("dtfma",)], w=[("dtfma",)])
            tap("dtfm5", dtfm_a[:], [("dtfma",)])
            dve(lambda e: e.tensor_mul(dtfm_a[:], dtfm_a[:], gmask_sb[:]),
                r=[("dtfma",), ("gmask",)], w=[("dtfma",)])
            tap("gmask", gmask_sb[:], [("gmask",)])
            tap("dtfm6", dtfm_a[:], [("dtfma",)])
            sdma(lambda e: e.dma_start(out=g_d, in_=dtfm_a[:]), r=[("dtfma",)], w=[("g_d",)])

        def z_pass(xn_buf, xnkey):
            gdma(lambda e: e.dma_start(
                out=w_z[:].rearrange("p (k z) -> p k z", z=2048),
                in_=winT[:, 0:DI].rearrange("(k p) z -> p k z", p=128)),
                w=[("w_z",)])
            zbufs = (zs_b0, zs_b1)
            zkeys = (("zs0",), ("zs1",))
            for tt in range(NT):
                par = tt % 2
                pscol = par * 2048
                pkeys = psk(pscol, pscol + 2048)
                for k in range(KD):
                    for n in range(4):
                        pe(lambda e, k=k, tt=tt, n=n, pscol=pscol, xn_buf=xn_buf: e.matmul(
                            PS[:, pscol + n * 512: pscol + (n + 1) * 512],
                            _mkv(xn_buf[:], [(1, 128)], xn_col(k, tt * 128)),
                            w_z[:, k * 2048 + n * 512: k * 2048 + (n + 1) * 512],
                            start=(k == 0), stop=(k == KD - 1)),
                           r=[xnkey, ("w_z",)], w=[pkeys[n]])
                zbuf, zkey = zbufs[par], zkeys[par]
                act(lambda e, pscol=pscol, zbuf=zbuf: e.activation(
                    zbuf[:], PS[:, pscol:pscol + 2048], AF.Silu),
                    r=pkeys, w=[zkey])
                sdma(lambda e, tt=tt, zbuf=zbuf: e.dma_start(
                    out=zs_d[tt * 128:(tt + 1) * 128, :], in_=zbuf[:]),
                    r=[zkey], w=[("zs_d", tt)])

        def half_pipeline(u_ap, xn_buf, xnkey, main):
            layernorm(u_ap, xn_buf, xnkey)
            if not main:
                for k in range(KD):
                    dve(lambda e, k=k, xn_buf=xn_buf: e.memset(
                        _mkv(xn_buf[:], [(1, 3)], k * XN_W), 0.0),
                        w=[xnkey])
            else:
                dve(lambda e: e.tensor_copy(_mkv(xn_buf[:], [(XN_W, KD), (1, 3)], 0),
                                            _mkv(xnp[:], [(XN_W, KD), (1, 3)], XN_W - 3)),
                    r=[("xnp",)], w=[xnkey])
            dt_router_pass(xn_buf, xnkey, main)
            if main:
                tap("xn", xn_buf[:], [xnkey])
                tap("dtraw", dtraw[:], [("dtraw",)])
                tap("dt_tm", dt_tm[:], [("dt_tm",)])
                tap("cs", cs_sb[:], [("cs",)])
                tap("tot", tot_sb[:], [("tot",)])
                tap("sdo", sdo_sb[:], [("sdo",)])
                tap("ds", ds_sb[:], [("ds",)])
                tap("dtds", dtds[:], [("dtds",)])
                tap("la", la[:], [("la",)])
                router_softmax_wgt()
                tap("logits", logits[:], [("logits",)])
                tap("probs", probs[:], [("probs",)])
                tap("wgt", wgt[:], [("wgt",)])
                tap("wgtf", wgtf[:], [("wgtf",)])
                dtfm_g_pass(xn_buf, xnkey)
                tapd("g_d", g_d, [("g_d",)])
                z_pass(xn_buf, xnkey)
                tapd("zs_d", zs_d, [("zs_d", t) for t in range(NT)])
                pb.barrier()
            if not main:
                gdma(lambda e: e.dma_start(out=S0[:], in_=init_in), w=[("S", 0)])
            else:
                gdma(lambda e: e.dma_start(out=S1[:], in_=init_in), w=[("S", 1)])
                dve(lambda e: e.tensor_scalar_mul(S1[:], S1[:], sel1m_sb[:, 0:1]),
                    r=[("S", 1), ("sel1m",)], w=[("S", 1)])
                dve(lambda e: e.scalar_tensor_tensor(S0[:], S0[:], sel_sb[:, 0:1], S1[:],
                                                     ALU.mult, ALU.add),
                    r=[("S", 0), ("sel",), ("S", 1)], w=[("S", 0)])
                tap("Sinit", S0[:], [("S", 0)])

            wsls = (wsl0, wsl1)
            pres = (pre0, pre1)
            accs = (acc0, acc1)
            xss = (xs0, xs1)
            def emit_slabs(tb):
                rp = (tb % 2) if not main else 0
                rbuf = (xr, yr)[rp] if not main else xr
                tb0 = 6 if main else 2
                for slab in range(NSLAB):
                    sp = slab % 2
                    wk = ("wsl", sp)
                    pk = ("pre", sp)
                    ak = ("acc",)
                    xk2 = ("xs", sp)
                    gdma(lambda e, slab=slab, sp=sp: e.dma_start(
                        out=wsls[sp][:].rearrange("p (k c) -> p k c", c=128),
                        in_=winT[:, 2048 + slab * 128: 2048 + (slab + 1) * 128]
                        .rearrange("(k p) c -> p k c", p=128)),
                        w=[wk])
                    for k in range(KD):
                        pe(lambda e, k=k, tb=tb, sp=sp, xn_buf=xn_buf: e.matmul(
                            PS[:, 512:515],
                            wsls[sp][:, k * 128:(k + 1) * 128],
                            _mkv(xn_buf[:], [(1, 3)], k * XN_W + tb * 512),
                            start=(k == 0), stop=(k == KD - 1)),
                           r=[wk, xnkey], w=[("PS", 1)])
                        pe(lambda e, k=k, tb=tb, sp=sp, xn_buf=xn_buf: e.matmul(
                            PS[:, 0:512],
                            wsls[sp][:, k * 128:(k + 1) * 128],
                            _mkv(xn_buf[:], [(1, 512)], xn_col(k, tb * 512)),
                            start=(k == 0), stop=(k == KD - 1)),
                           r=[wk, xnkey], w=[("PS", 0)])
                    dve(lambda e, sp=sp: e.tensor_copy(pres[sp][:, 0:3], PS[:, 512:515]),
                        r=[("PS", 1)], w=[pk])
                    dve(lambda e, sp=sp: e.tensor_copy(pres[sp][:, 3:515], PS[:, 0:512]),
                        r=[("PS", 0)], w=[pk])
                    dve(lambda e, sp=sp, slab=slab: e.tensor_scalar(
                        accs[sp][:], pres[sp][:, 0:512], cw_sb[:, slab * 4: slab * 4 + 1],
                        None, ALU.mult),
                        r=[pk, ("cw",)], w=[ak])
                    for kk in range(1, 4):
                        dve(lambda e, sp=sp, slab=slab, kk=kk: e.scalar_tensor_tensor(
                            accs[sp][:], pres[sp][:, kk:kk + 512],
                            cw_sb[:, slab * 4 + kk: slab * 4 + kk + 1],
                            accs[sp][:], ALU.mult, ALU.add),
                            r=[pk, ("cw",), ak], w=[ak])
                    bank = tb0 + sp
                    if slab < 16:
                        act(lambda e, sp=sp, slab=slab: e.activation(
                            xss[sp][:], accs[sp][:], AF.Silu, bias=cb_sb[:, slab:slab + 1]),
                            r=[ak, ("cb",)], w=[xk2])
                        for q in range(4):
                            pe(lambda e, sp=sp, q=q, bank=bank: e.transpose(
                                PSB[:, bank * 1024 + q * 128: bank * 1024 + (q + 1) * 128],
                                xss[sp][:, q * 128:(q + 1) * 128], id_bf[:]),
                               r=[xk2, ("id",)], w=[("PS", bank)])
                        dve(lambda e, slab=slab, bank=bank, rbuf=rbuf: e.tensor_copy(
                            _mkv(rbuf[:], [(2048, 4), (1, 128)], slab * 128),
                            PSB[:, bank * 1024: bank * 1024 + 512]),
                            r=[("PS", bank)], w=[("xr", rp)])
                    else:
                        act(lambda e, sp=sp, slab=slab, tb=tb: e.activation(
                            bc_fm[:, tb * 512:(tb + 1) * 512], accs[sp][:], AF.Silu,
                            bias=cb_sb[:, slab:slab + 1]),
                            r=[ak, ("cb",)], w=[("bcfm", rp)])
                        for q in range(4):
                            pe(lambda e, tb=tb, q=q, bank=bank: e.transpose(
                                PSB[:, bank * 1024 + q * 128: bank * 1024 + (q + 1) * 128],
                                bc_fm[:, tb * 512 + q * 128: tb * 512 + (q + 1) * 128],
                                id_bf[:]),
                               r=[("bcfm", rp), ("id",)], w=[("PS", bank)])
                        dve(lambda e, bank=bank, rp=rp: e.tensor_copy(
                            (bct, bct1)[rp][:, 0:512],
                            PSB[:, bank * 1024: bank * 1024 + 512]),
                            r=[("PS", bank)], w=[("bct", rp)])
                        sdma(lambda e, tb=tb: e.dma_start(
                            out=c_fm0[:, tb * 512:(tb + 1) * 512],
                            in_=bc_fm[64:128, tb * 512:(tb + 1) * 512]),
                            r=[("bcfm", rp)], w=[("cfm0", rp)])

            def emit_chunks(tb):
                rp = (tb % 2) if not main else 0
                rbuf = (xr, yr)[rp] if not main else xr
                for j2 in range(8):
                    c = tb * 8 + j2
                    par = c % 2
                    base = par * 64
                    tile = c // 2
                    slot = tile % 4
                    kSc = ("S", c % 2)
                    kSn = ("S", (c + 1) % 2)
                    if main:
                        sdma(lambda e, c=c, base=base: e.dma_start(
                            out=zb_b[base:base + 64, :], in_=zs_d[c * 64:(c + 1) * 64, :]),
                            r=[("zs_d", c // 2)], w=[("zb", par)])
                        sdma(lambda e, c=c, base=base: e.dma_start(
                            out=gr_b[base:base + 64, :].rearrange("p (h l) -> p h l", l=64),
                            in_=bass.AP(g_d.tensor, g_d.offset + c * 64,
                                        [[0, 64], [T, NH], [1, 64]])),
                            r=[("g_d",)], w=[("gr", par)])
                        pool(lambda e, base=base, slot=slot, tile=tile: e.tensor_tensor(
                            x2b[base:base + 64, :].rearrange("p (h q) -> p h q", q=HD),
                            _mkv(xr[base:base + 64, :], [(64, NH), (1, HD)], slot * 2048),
                            _mkv(dt_tm[base:base + 64, :], [(1, NH), (0, HD)], tile * 32),
                            ALU.mult),
                            r=[("xr", rp), ("dt_tm",)], w=[("x2", par)])
                    pool(lambda e, base=base, slot=slot, tile=tile, rbuf=rbuf: e.tensor_tensor(
                        xdb[base:base + 64, :].rearrange("p (h q) -> p h q", q=HD),
                        _mkv(rbuf[base:base + 64, :], [(64, NH), (1, HD)], slot * 2048),
                        _mkv(dtds[base:base + 64, :], [(1, NH), (0, HD)], tile * 32),
                        ALU.mult),
                        r=[("xr", rp), ("dtds",)], w=[("xd", par)])
                    for n in range(4):
                        pe(lambda e, base=base, slot=slot, n=n, rp=rp: e.matmul(
                            PS[0:64, 2048 + n * 512: 2048 + (n + 1) * 512],
                            (bct, bct1)[rp][base:base + 64, slot * 128: slot * 128 + 64],
                            xdb[base:base + 64, n * 512:(n + 1) * 512],
                            start=True, stop=True, tile_position=(base, 0)),
                           r=[("bct", rp), ("xd", par)], w=[("PS", 4 + n)])
                    if main:
                        pe(lambda e, c=c, base=base: e.matmul(
                            PS[base:base + 64, 0:64],
                            bc_fm[0:64, c * 64:(c + 1) * 64],
                            c_fm0[:, c * 64:(c + 1) * 64],
                            start=True, stop=True, tile_position=(0, base)),
                           r=[("bcfm", rp), ("cfm0", rp)], w=[("PS", 0)])
                        dve(lambda e, base=base: e.tensor_tensor_scan(
                            lt_b[base:base + 64, :], gr_b[base:base + 64, :],
                            i64rep[base:base + 64, :], 0.0, ALU.mult, ALU.add),
                            r=[("gr", par), ("i64",)], w=[("lt", par)])
                        dve(lambda e, base=base: e.tensor_tensor(
                            gl_b[base:base + 64, :].rearrange("p (h l) -> p h l", l=64),
                            lt_b[base:base + 64, :].rearrange("p (h l) -> p h l", l=64),
                            _mkv(PS[base:base + 64, :], [(0, NH), (1, 64)], 0),
                            ALU.mult),
                            r=[("lt", par), ("PS", 0)], w=[("gl", par)])
                        for h in range(NH):
                            pe(lambda e, c=c, base=base, h=h: e.matmul(
                                PS[base:base + 64, h * 64:(h + 1) * 64],
                                c_fm0[:, c * 64:(c + 1) * 64],
                                (S0, S1)[c % 2][:, h * 64:(h + 1) * 64],
                                start=(h % 8 == 0), stop=False, tile_position=(0, base)),
                               r=[("cfm0", rp), kSc], w=[("PS", h // 8)])
                        dve(lambda e, base=base, tile=tile: e.tensor_tensor(
                            PS[base:base + 64, 0:2048].rearrange("p (h q) -> p h q", q=HD),
                            PS[base:base + 64, 0:2048].rearrange("p (h q) -> p h q", q=HD),
                            _mkv(sdo_sb[base:base + 64, :], [(1, NH), (0, HD)], tile * 32),
                            ALU.mult),
                            r=psk(0, 2048) + [("sdo",)], w=psk(0, 2048))
                        for h in range(NH):
                            pe(lambda e, base=base, h=h: e.matmul(
                                PS[base:base + 64, h * 64:(h + 1) * 64],
                                gl_b[base:base + 64, h * 64:(h + 1) * 64],
                                x2b[base:base + 64, h * 64:(h + 1) * 64],
                                start=False, stop=False, tile_position=(base, base)),
                               r=[("gl", par), ("x2", par)], w=[("PS", h // 8)])
                        for h in range(NH):
                            pe(lambda e, base=base, h=h, slot=slot: e.matmul(
                                PS[base:base + 64, h * 64:(h + 1) * 64],
                                diagD[base:base + 64, h * 64:(h + 1) * 64],
                                _mkv(xr[base:base + 64, :], [(1, 64)], slot * 2048 + h * 64),
                                start=False, stop=True, tile_position=(base, base)),
                               r=[("diagD",), ("xr", rp)], w=[("PS", h // 8)])
                    pool(lambda e, c=c: e.tensor_tensor(
                        (S0, S1)[(c + 1) % 2][:].rearrange("p (h q) -> p h q", q=HD),
                        (S0, S1)[c % 2][:].rearrange("p (h q) -> p h q", q=HD),
                        _mkv(la[:], [(1, NH), (0, HD)], c * 32),
                        ALU.mult),
                        r=[kSc, ("la",)], w=[kSn])
                    dve(lambda e, c=c: e.tensor_add(
                        (S0, S1)[(c + 1) % 2][:], (S0, S1)[(c + 1) % 2][:],
                        PS[0:64, 2048:4096]),
                        r=[kSn] + psk(2048, 4096), w=[kSn])
                    if main:
                        dve(lambda e, base=base, slot=slot: e.tensor_tensor(
                            _mkv(yr[base:base + 64, :], [(1, 2048)], slot * 2048),
                            PS[base:base + 64, 0:2048], zb_b[base:base + 64, :],
                            ALU.mult),
                            r=psk(0, 2048) + [("zb", par)], w=[("yr", slot)])

            def emit_ynorm(tb):
                if main:
                    for q in range(4):
                        tt = tb * 4 + q
                        yk = ("yr", q)
                        act(lambda e, q=q: e.activation(
                            sq_scr[:].bitcast(BF16)[:, 0:2048],
                            yr[:, q * 2048:(q + 1) * 2048], AF.Square,
                            accum_out=stats[:, 8:9]),
                            r=[yk], w=[("sq",), ("st8",)])
                        dve(lambda e: e.tensor_scalar(stats[:, 9:10], stats[:, 8:9],
                                                      1.0 / DI, 1e-5, ALU.mult, ALU.add),
                            r=[("st8",)], w=[("st9",)])
                        act(lambda e: e.activation(stats[:, 10:11], stats[:, 9:10], AF.Sqrt),
                            r=[("st9",)], w=[("st10",)])
                        dve(lambda e: e.reciprocal(stats[:, 11:12], stats[:, 10:11]),
                            r=[("st10",)], w=[("st11",)])
                        dve(lambda e, q=q: e.tensor_scalar_mul(
                            yr[:, q * 2048:(q + 1) * 2048],
                            yr[:, q * 2048:(q + 1) * 2048], stats[:, 11:12]),
                            r=[yk, ("st11",)], w=[yk], sync=True)
                        for kb in range(0, 16, 4):
                            bank = 6 + (kb // 4) % 2
                            for k in range(kb, kb + 4):
                                pe(lambda e, q=q, k=k, kb=kb, bank=bank: e.transpose(
                                    PSB[:, bank * 1024 + (k - kb) * 128:
                                        bank * 1024 + (k - kb + 1) * 128],
                                    yr[:, q * 2048 + k * 128: q * 2048 + (k + 1) * 128],
                                    id_bf[:]),
                                   r=[yk, ("id",)], w=[("PS", bank)])
                            dve(lambda e, kb=kb, bank=bank: e.tensor_copy(
                                yft[:, kb * 128:(kb + 4) * 128],
                                PSB[:, bank * 1024: bank * 1024 + 512]),
                                r=[("PS", bank)], w=[("yft",)])
                        sdma(lambda e, tt=tt: e.dma_start(
                            out=bass.AP(yfm_d.tensor, yfm_d.offset + tt * 128,
                                        [[T, 128], [128 * T, 16], [1, 128]]),
                            in_=yft[:].rearrange("p (k c) -> p k c", c=128)),
                            r=[("yft",)], w=[("yfm", tt)])

            if main:
                for tb in range(NTB):
                    emit_slabs(tb)
                    emit_chunks(tb)
                    emit_ynorm(tb)
            else:
                emit_slabs(0)
                for tb in range(NTB):
                    if tb + 1 < NTB:
                        emit_slabs(tb + 1)
                    emit_chunks(tb)

        half_pipeline(up_in, xnp, ("xnp",), main=False)
        pb.barrier()
        half_pipeline(u_in, xn, ("xn",), main=True)
        tapd("yfm_d", yfm_d, [("yfm", t) for t in range(NT)])
        pb.barrier()

        gdma(lambda e: e.dma_start(out=b2_bf[:], in_=b2_in), w=[("b2",)])
        for th in range(2):
            for q in range(8):
                tt = th * 8 + q
                pcol = 1024 + (q % 2) * 1024
                pkeys = psk(pcol, pcol + 1024)
                for n in range(2):
                    pe(lambda e, tt=tt, n=n, pcol=pcol: e.matmul(
                        PS[:, pcol + n * 512: pcol + (n + 1) * 512],
                        wgtf[:, tt * 128:(tt + 1) * 128],
                        b2_bf[:, n * 512:(n + 1) * 512],
                        start=True, stop=True),
                       r=[("wgtf",), ("b2",)], w=[pkeys[n]])
                dve(lambda e, q=q, pcol=pcol: e.tensor_copy(
                    ma[:, q * DM:(q + 1) * DM], PS[:, pcol:pcol + 1024]),
                    r=pkeys, w=[("ma", q)])
                if DEBUG and th == 0 and q == 0:
                    tap("mab2", ma[:, 0:1024], [("ma", 0)])
            for ee in range(8):
                for qq in range(4):
                    wp = qq % 2
                    k1, k2 = ("w1q", wp), ("w2q", wp)
                    gdma(lambda e, ee=ee, qq=qq, wp=wp: e.dma_start(
                        out=w1q[wp][:].rearrange("p (k j) -> p k j", j=1024),
                        in_=bass.AP(w1_in.tensor,
                                    w1_in.offset + ee * DM * 4096 + qq * 1024,
                                    [[4096, 128], [128 * 4096, 8], [1, 1024]])),
                        w=[k1])
                    gdma(lambda e, ee=ee, qq=qq, wp=wp: e.dma_start(
                        out=w2q[wp][:].rearrange("p (k j) -> p k j", j=1024),
                        in_=bass.AP(w2_in.tensor,
                                    w2_in.offset + ee * 4096 * DM + qq * 1024 * DM,
                                    [[DM, 128], [128 * DM, 8], [1, 1024]])),
                        w=[k2])
                    if qq == 0:
                        sdma(lambda e, ee=ee: e.dma_start(out=b1c[:], in_=b1_in[ee]),
                             w=[("b1",)])
                    def w1_block(t5, wp=wp, k1=k1):
                        tok0 = th * 1024 + t5 * 512
                        hp = t5 % 2
                        hqb = (hq0, hq1)[hp]
                        hk = ("hq", hp)
                        for wv in range(4):
                            pw = wv % 2
                            for mi in range(2):
                                mj = wv * 2 + mi
                                col = (pw * 2 + mi) * 512
                                for k in range(KD):
                                    pe(lambda e, k=k, mj=mj, col=col, tok0=tok0, wp=wp: e.matmul(
                                        PS[:, col:col + 512],
                                        w1q[wp][:, k * 1024 + mj * 128: k * 1024 + (mj + 1) * 128],
                                        _mkv(xn[:], [(1, 512)], xn_col(k, tok0)),
                                        start=(k == 0), stop=(k == KD - 1)),
                                       r=[k1, ("xn",)], w=[("PS", pw * 2 + mi)])
                            for mi in range(2):
                                mj = wv * 2 + mi
                                col = (pw * 2 + mi) * 512
                                act(lambda e, mj=mj, col=col, hqb=hqb, pw=pw, mi=mi, qq=qq: e.activation(
                                    hqb[:, mj * 512:(mj + 1) * 512], PS[:, col:col + 512],
                                    AF.Gelu_apprx_tanh,
                                    bias=b1c[:, qq * 8 + mj: qq * 8 + mj + 1]),
                                    r=[("PS", pw * 2 + mi), ("b1",)], w=[hk])

                    def w2_block(t5, wp=wp, k2=k2, ee=ee, qq=qq):
                        hp = t5 % 2
                        hqb = (hq0, hq1)[hp]
                        hk = ("hq", hp)
                        for tsub in range(4):
                            po = tsub % 2
                            ocol = 2048 + po * 1024
                            okeys = psk(ocol, ocol + 1024)
                            for mj in range(8):
                                for n in range(2):
                                    pe(lambda e, mj=mj, n=n, tsub=tsub, ocol=ocol, wp=wp,
                                       hqb=hqb: e.matmul(
                                        PS[:, ocol + n * 512: ocol + (n + 1) * 512],
                                        hqb[:, mj * 512 + tsub * 128: mj * 512 + (tsub + 1) * 128],
                                        w2q[wp][:, mj * 1024 + n * 512: mj * 1024 + (n + 1) * 512],
                                        start=(mj == 0), stop=(mj == 7)),
                                       r=[hk, k2], w=[okeys[n]])
                            tt = th * 8 + t5 * 4 + tsub
                            qm = t5 * 4 + tsub
                            dve(lambda e, tt=tt, qm=qm, ocol=ocol, ee=ee: e.scalar_tensor_tensor(
                                ma[:, qm * DM:(qm + 1) * DM],
                                PS[:, ocol:ocol + 1024],
                                wgt[:, tt * 8 + ee: tt * 8 + ee + 1],
                                ma[:, qm * DM:(qm + 1) * DM],
                                ALU.mult, ALU.add),
                                r=okeys + [("wgt",), ("ma", qm)], w=[("ma", qm)])

                    for t5 in range(3):
                        if t5 < 2:
                            w1_block(t5)
                        if t5 >= 1:
                            w2_block(t5 - 1)
            pb.barrier()
            gdma(lambda e: e.dma_start(
                out=wout_bf[:].rearrange("p (k c) -> p k c", c=1024),
                in_=woutT.rearrange("(k p) c -> p k c", p=128)),
                w=[("woutbf",)])
            for k in range(16):
                dve(lambda e, k=k: e.tensor_scalar_mul(
                    wout_bf[:, k * 1024:(k + 1) * 1024],
                    wout_bf[:, k * 1024:(k + 1) * 1024], nw_sb[:, k:k + 1]),
                    r=[("woutbf",), ("nw",)], w=[("woutbf",)])
            for q in range(8):
                tt = th * 8 + q
                par = q % 2
                pcol = par * 1024
                pkeys = psk(pcol, pcol + 1024)
                ub = (u_sb0, u_sb1)[par]
                ob = (out_sb0, out_sb1)[par]
                uk = ("u0",)
                ok = (("o0",), ("o1",))[par]
                sdma(lambda e, tt=tt: e.dma_start(
                    out=yft2[:].rearrange("p (k c) -> p k c", c=128),
                    in_=bass.AP(yfm_d.tensor, yfm_d.offset + tt * 128,
                                [[T, 128], [128 * T, 16], [1, 128]])),
                    r=[("yfm", tt)], w=[("yft2",)])
                sdma(lambda e, tt=tt, ub=ub: e.dma_start(
                    out=ub[:], in_=u_in[tt * 128:(tt + 1) * 128, :]), w=[uk])
                for k in range(16):
                    for n in range(2):
                        pe(lambda e, k=k, n=n, pcol=pcol: e.matmul(
                            PS[:, pcol + n * 512: pcol + (n + 1) * 512],
                            yft2[:, k * 128:(k + 1) * 128],
                            wout_bf[:, k * 1024 + n * 512: k * 1024 + (n + 1) * 512],
                            start=(k == 0), stop=(k == 15)),
                           r=[("yft2",), ("woutbf",)], w=[pkeys[n]])
                dve(lambda e, q=q, ub=ub, ob=ob: e.tensor_add(
                    ob[:], ub[:], ma[:, q * DM:(q + 1) * DM]),
                    r=[uk, ("ma", q)], w=[ok])
                dve(lambda e, ob=ob, pcol=pcol: e.tensor_add(
                    ob[:], ob[:], PS[:, pcol:pcol + 1024]),
                    r=[ok] + pkeys, w=[ok])
                sdma(lambda e, tt=tt, ob=ob: e.dma_start(
                    out=out_d[tt * 128:(tt + 1) * 128, :], in_=ob[:]),
                    r=[ok], w=[("outd", tt)], tag="out")
            if th == 0:
                pb.barrier()

        pb.finalize(block, sems)
    return nc


_NC_CACHE = None


def _get_nc():
    global _NC_CACHE
    if _NC_CACHE is None:
        _NC_CACHE = build_nc()
    return _NC_CACHE


def _consts():
    ident = np.eye(128, dtype=np.float32)
    trib = np.zeros((128, 128), np.float32)
    totb = np.zeros((128, 128), np.float32)
    for h2 in range(2):
        sl = slice(h2 * 64, (h2 + 1) * 64)
        trib[sl, sl] = np.triu(np.ones((64, 64), np.float32))
        totb[sl, sl] = 1.0
    i64 = np.tile(np.eye(64, dtype=np.float32), (2, 32))
    ones = np.ones((1, 512), np.float32)
    return ident, trib, totb, i64, ones


def kernel(L0_attn_u, L0_attn_in_proj_w, L0_attn_conv_w, L0_attn_conv_b,
           L0_attn_init_states, L0_attn_dt_bias, L0_attn_A_log, L0_attn_D,
           L0_attn_norm_w, L0_attn_out_proj_w, L0_ffn_w_router, L0_ffn_w1,
           L0_ffn_b1, L0_ffn_w2, L0_ffn_b2):
    f = lambda x: np.ascontiguousarray(np.asarray(x, dtype=np.float32))
    u = f(L0_attn_u)
    winT = np.ascontiguousarray(f(L0_attn_in_proj_w).T)
    init_s = np.ascontiguousarray(
        f(L0_attn_init_states).transpose(2, 0, 1).reshape(64, NH * HD))
    woutT = np.ascontiguousarray(f(L0_attn_out_proj_w).T)
    ident, trib, totb, i64, ones = _consts()
    gmask = np.ones((NH, T), np.float32)
    gmask[:, ::64] = 0.0

    convb_st = np.ascontiguousarray(f(L0_attn_conv_b).reshape(NSLAB, 128).T)
    nw_st = np.ascontiguousarray(f(L0_attn_norm_w).reshape(16, 128).T)
    shared = dict(winT=winT, convw=f(L0_attn_conv_w), convb=convb_st,
                  dtb=f(L0_attn_dt_bias), alog=f(L0_attn_A_log), dd=f(L0_attn_D),
                  nw=nw_st, init=init_s, woutT=woutT,
                  wrout=f(L0_ffn_w_router), w1=f(L0_ffn_w1),
                  b1=np.ascontiguousarray(f(L0_ffn_b1).reshape(8, 32, 128).transpose(0, 2, 1)),
                  w2=f(L0_ffn_w2), b2=f(L0_ffn_b2), c_ident=ident, c_trib=trib,
                  c_totb=totb, c_i64=i64, c_ones=ones, c_gmask=gmask)
    in_maps = []
    zeros_u = np.zeros((T, DM), np.float32)
    for b in range(4):
        for half in range(2):
            m = dict(shared)
            m["u"] = np.ascontiguousarray(u[b, half * T:(half + 1) * T])
            m["up"] = np.ascontiguousarray(u[b, 0:T]) if half == 1 else zeros_u
            s = 1.0 if half == 1 else 0.0
            m["sel"] = np.full((64, 1), s, np.float32)
            m["sel1m"] = np.full((64, 1), 1.0 - s, np.float32)
            in_maps.append(m)

    nc = _get_nc()
    res = run_bass_kernel_spmd(nc, in_maps, core_ids=list(range(8)))
    out = np.empty((4, 4096, 1024), np.float32)
    for b in range(4):
        for half in range(2):
            out[b, half * T:(half + 1) * T] = res.results[b * 2 + half]["out"]
    return out




# version 09
# speedup vs baseline: 1.1420x; 1.0491x over previous
import numpy as np
import concourse.bass as bass
import concourse.mybir as mybir
from concourse.bass_utils import run_bass_kernel_spmd
from contextlib import ExitStack

F32 = mybir.dt.float32
BF16 = mybir.dt.bfloat16
AF = mybir.ActivationFunctionType
ALU = mybir.AluOpType
AX = mybir.AxisListType

T = 2048
NT = 16
NTB = 4
NCH = 32
CH = 64
NH = 32
HD = 64
DM = 1024
DI = 2048
KD = 8
NSLAB = 17
XN_W = 2051

DEBUG = False
STREAMS = ("pe", "dve", "act", "pool", "sdma", "gdma")
BODY = {"pe": "tensor", "dve": "vector", "act": "scalar",
        "pool": "gpsimd", "gdma": "gpsimd", "sdma": "sync"}


class PB:
    def __init__(self):
        self.ops = []
        self.lastw = {}
        self.readers = {}

    def emit(self, stream, fn, r=(), w=(), dep=(), tag="", sync=False):
        i = len(self.ops)
        deps = set(dep)
        for k in r:
            lw = self.lastw.get(k)
            if lw is not None:
                deps.add(lw)
        for k in w:
            lw = self.lastw.get(k)
            if lw is not None:
                deps.add(lw)
            deps.update(self.readers.get(k, ()))
        self.ops.append(dict(stream=stream, fn=fn, deps=deps, tag=tag, sync=sync,
                             is_dma=stream in ("sdma", "gdma")))
        for k in r:
            self.readers.setdefault(k, set()).add(i)
        for k in w:
            self.lastw[k] = i
            self.readers[k] = set()
        return i

    def barrier(self):
        last = {}
        for i, op in enumerate(self.ops):
            last[op["stream"]] = i
        for s in STREAMS:
            deps = [last[t] for t in STREAMS if t != s and t in last]
            self.emit(s, lambda e: e.nop(), dep=deps, tag="bar")

    def finalize(self, block, sems):
        ops = self.ops
        need = [False] * len(ops)
        for i, op in enumerate(ops):
            kept = []
            for d in op["deps"]:
                dop = ops[d]
                same_body = BODY[dop["stream"]] == BODY[op["stream"]]
                if same_body and not dop["is_dma"] and not op["sync"]:
                    continue
                kept.append(d)
                need[d] = True
            op["kept"] = kept
        for i, op in enumerate(ops):
            if op["tag"] == "out" or op["is_dma"]:
                need[i] = True
        NDS = 8
        cnt = {s: 0 for s in STREAMS}
        dcnt = {"sdma": 0, "gdma": 0}
        for i, op in enumerate(ops):
            if op["is_dma"]:
                n = dcnt[op["stream"]]
                dcnt[op["stream"]] += 1
                op["semk"] = (op["stream"], n % NDS)
                op["val"] = 16 * (n // NDS + 1)
                op["prewait"] = 16 * (n // NDS)
                op["inc"] = True
            else:
                if need[i]:
                    cnt[op["stream"]] += 1
                op["semk"] = (op["stream"], 0)
                op["val"] = cnt[op["stream"]]
                op["inc"] = need[i]
        finals = [(("sdma", j), 16 * ((dcnt["sdma"] + NDS - 1 - j) // NDS))
                  for j in range(min(NDS, dcnt["sdma"]))]

        body_ops = {}
        for i, op in enumerate(ops):
            body_ops.setdefault(BODY[op["stream"]], []).append(i)

        def run_body(eng, idxs):
            water = {}
            for i in idxs:
                op = ops[i]
                for d in op["kept"]:
                    dop = ops[d]
                    v = dop["val"]
                    sk = dop["semk"]
                    if water.get(sk, 0) < v:
                        eng.wait_ge(sems[sk], v)
                        water[sk] = v
                if op["is_dma"] and op["prewait"] > 0:
                    sk = op["semk"]
                    if water.get(sk, 0) < op["prewait"]:
                        eng.wait_ge(sems[sk], op["prewait"])
                        water[sk] = op["prewait"]
                ins = op["fn"](eng)
                if op["inc"]:
                    ins.then_inc(sems[op["semk"]], 16 if op["is_dma"] else 1)

        @block.tensor
        def _(t):
            run_body(t, body_ops.get("tensor", []))

        @block.vector
        def _(v):
            run_body(v, body_ops.get("vector", []))

        @block.scalar
        def _(s):
            run_body(s, body_ops.get("scalar", []))

        @block.gpsimd
        def _(g):
            run_body(g, body_ops.get("gpsimd", []))

        @block.sync
        def _(sy):
            run_body(sy, body_ops.get("sync", []))
            for sk, v in finals:
                if v > 0:
                    sy.wait_ge(sems[sk], v)


def _mkv(base_ap, dims, off):
    newap = [list(base_ap.ap[0])] + [[s, c] for (s, c) in dims]
    return bass.AP(base_ap.tensor, base_ap.offset + off, newap)


def build_nc():
    nc = bass.Bass("TRN2", target_bir_lowering=False)
    din = lambda n, s: nc.dram_tensor(n, s, F32, kind="ExternalInput").ap()

    u_in = din("u", [T, DM])
    up_in = din("up", [T, DM])
    winT = din("winT", [DM, 4256])
    convw = din("convw", [2176, 4])
    convb = din("convb", [128, NSLAB])
    dtb = din("dtb", [NH])
    alog = din("alog", [NH])
    dd = din("dd", [NH])
    nw = din("nw", [128, 16])
    init_in = din("init", [64, NH * HD])
    woutT = din("woutT", [DI, DM])
    wrout = din("wrout", [DM, 8])
    w1_in = din("w1", [8, DM, 4096])
    b1_in = din("b1", [8, 128, 32])
    w2_in = din("w2", [8, 4096, DM])
    b2_in = din("b2", [8, DM])
    c_ident = din("c_ident", [128, 128])
    c_trib = din("c_trib", [128, 128])
    c_totb = din("c_totb", [128, 128])
    c_i64 = din("c_i64", [128, DI])
    c_ones = din("c_ones", [1, 512])
    c_gmask = din("c_gmask", [NH, T])
    sel_in = din("sel", [64, 1])
    sel1m_in = din("sel1m", [64, 1])

    out_d = nc.dram_tensor("out", [T, DM], F32, kind="ExternalOutput").ap()

    zs_d = nc.dram_tensor("zs_d", [T, DI], BF16, kind="Internal").ap()
    yfm_d = nc.dram_tensor("yfm_d", [DI, T], BF16, kind="Internal").ap()
    g_d = nc.dram_tensor("g_d", [NH, T], F32, kind="Internal").ap()
    tot_d = nc.dram_tensor("tot_d", [1, NCH * NH], F32, kind="Internal").ap()
    a_d = nc.dram_tensor("a_d", [1, NH], F32, kind="Internal").ap()

    pb = PB()
    with ExitStack() as ctx:
        sb = lambda n, s, d: ctx.enter_context(nc.sbuf_tensor(n, s, d))
        xn = sb("xn", [128, KD * XN_W], BF16)
        arena1 = sb("arena1", [128, KD * XN_W], BF16)
        arena2 = sb("arena2", [128, 32768], BF16)
        arena3 = sb("arena3", [128, 10240], F32)
        dtraw = arena3[:, 0:512]
        scrA = arena3[:, 512:1024]
        scrB = arena3[:, 1024:1536]
        dt_tm = arena3[:, 1536:2048]
        cs_sb = arena3[:, 2048:2560]
        tot_sb = arena3[:, 2560:3072]
        sdo_sb = arena3[:, 3072:3584]
        ds_sb = arena3[:, 3584:4096]
        dtds = arena3[:, 4096:4608]
        la = arena3[0:64, 4608:5632]
        dtb_rep = sb("dtb_rep", [128, NH], F32)
        A_rep = sb("A_rep", [128, NH], F32)
        D_rep = sb("D_rep", [128, NH], F32)
        arow = sb("arow", [1, NH], F32)
        acolf = sb("acolf", [NH, 1], F32)
        dtbcol = sb("dtbcol", [NH, 1], F32)
        cw_sb = sb("cw_sb", [128, NSLAB * 4], F32)
        cb_sb = sb("cb_sb", [128, NSLAB], F32)
        pre0 = sb("pre0", [128, 515], BF16)
        pre1 = sb("pre1", [128, 515], BF16)
        acc0 = sb("acc0", [128, 512], F32)
        acc1 = acc0
        xs0 = sb("xs0", [128, 512], BF16)
        xs1 = sb("xs1", [128, 512], BF16)
        wsl0 = sb("wsl0", [128, 1024], BF16)
        wsl1 = sb("wsl1", [128, 1024], BF16)
        bc_fm = sb("bc_fm", [128, T], BF16)
        c_fm0 = sb("c_fm0", [64, T], BF16)
        bct = sb("bct", [128, 512], BF16)
        u_sb0 = sb("u_sb0", [128, DM], F32)
        u_sb1 = u_sb0
        sq_scr = sb("sq_scr", [128, DM], F32)
        stats = sb("stats", [128, 16], F32)
        id_bf = sb("id_bf", [128, 128], BF16)
        trib = sb("trib", [128, 128], F32)
        totb = sb("totb", [128, 128], F32)
        i64rep = sb("i64rep", [128, DI], BF16)
        diagD = sb("diagD", [128, DI], BF16)
        ones_bf = sb("ones_bf", [1, 512], BF16)
        sel_sb = sb("sel_sb", [64, 1], F32)
        sel1m_sb = sb("sel1m_sb", [64, 1], F32)
        w_dt = sb("w_dt", [128, KD * 32], BF16)
        w_r = sb("w_r", [128, KD * 8], BF16)
        logits = arena3[:, 1152:1280]
        probs = arena3[:, 1280:1408]
        cnt_sb = arena3[:, 1024:1152]
        tmpc = sb("tmpc", [128, 16], F32)
        wgt = arena3[:, 1408:1536]
        wgt_bf = sb("wgt_bf", [128, 128], BF16)
        a3b = arena3[:].bitcast(BF16)
        S0 = a3b[0:64, 11264:13312]
        S1 = a3b[0:64, 13312:15360]
        wgtf = a3b[0:8, 0:2048]
        b2_bf = a3b[0:8, 8192:9216]
        b1c = arena3[:, 9728:9760]
        hq0 = a3b[:, 4096:8192]
        hq1 = a3b[:, 9216:13312]
        nw_sb = sb("nw_sb", [128, 16], F32)
        PS = ctx.enter_context(nc.psum_tensor("PS", [128, 4096], F32))
        PSB = PS.bitcast(BF16)

        sems = {}
        for s in STREAMS:
            if s in ("sdma", "gdma"):
                for j in range(8):
                    sems[(s, j)] = ctx.enter_context(nc.semaphore(f"sem_{s}{j}"))
            else:
                sems[(s, 0)] = ctx.enter_context(nc.semaphore("sem_" + s))
        block = ctx.enter_context(nc.Block())

        xnp = arena1[:, 0:KD * XN_W]
        a1f = arena1[:].bitcast(F32)
        dtfm_a = a1f[:, 0:2048][0:NH, :]
        dtfm_b = a1f[:, 2048:4096][0:NH, :]
        gmask_sb = a1f[:, 4096:6144][0:NH, :]
        ma = a1f[:, 0:8192]
        xr = arena2[:, 0:8192]
        yr = arena2[:, 8192:16384]
        lt_b = arena2[:, 16384:18432]
        bct1 = arena2[:, 17408:17920]
        gl_b = arena2[:, 18432:20480]
        zb_b = arena2[:, 20480:22528]
        x2b = arena2[:, 22528:24576]
        xdb = arena2[:, 24576:26624]
        gr_b = arena2[:].bitcast(F32)[:, 13312:15360]
        yft = arena2[:, 30720:32768]
        w1q = (arena2[:, 0:8192], arena2[:, 8192:16384])
        w2q = (arena2[:, 16384:24576], arena2[:, 24576:32768])
        w_z = arena2[:, 0:16384]
        zs_b0 = arena2[:, 16384:18432]
        zs_b1 = arena2[:, 18432:20480]
        wout_bf = arena2[:, 0:16384]
        yft2 = arena2[:, 16384:18432]
        out_sb0 = arena2[:].bitcast(F32)[:, 9216:10240]
        out_sb1 = arena2[:].bitcast(F32)[:, 10240:11264]

        def dve(fn, r=(), w=(), tag="", sync=False):
            return pb.emit("dve", fn, r, w, tag=tag, sync=sync)

        def act(fn, r=(), w=()):
            return pb.emit("act", fn, r, w)

        def pool(fn, r=(), w=()):
            return pb.emit("pool", fn, r, w)

        def pe(fn, r=(), w=()):
            return pb.emit("pe", fn, r, w)

        def sdma(fn, r=(), w=(), tag=""):
            return pb.emit("sdma", fn, r, w, tag=tag)

        def gdma(fn, r=(), w=()):
            return pb.emit("gdma", fn, r, w)

        def psk(lo, hi):
            return [("PS", b) for b in range(lo // 512, (hi + 511) // 512)]

        def tap(name, ap, r, dt=None):
            if not DEBUG:
                return
            shape = list(ap.shape)
            t = nc.dram_tensor("dbg_" + name, shape, ap.dtype, kind="ExternalOutput").ap()
            sdma(lambda e, ap=ap, t=t: e.dma_start(out=t, in_=ap), r=r, tag="out")

        def tapd(name, dram_ap, r):
            if not DEBUG:
                return
            shape = list(dram_ap.shape)
            t = nc.dram_tensor("dbg_" + name, shape, dram_ap.dtype, kind="ExternalOutput").ap()
            sdma(lambda e, t=t, dram_ap=dram_ap: e.dma_start(out=t, in_=dram_ap), r=r, tag="out")

        gdma(lambda e: e.dma_start(out=id_bf[:], in_=c_ident), w=[("id",)])
        sdma(lambda e: e.dma_start(out=trib[:], in_=c_trib), w=[("trib",)])
        sdma(lambda e: e.dma_start(out=totb[:], in_=c_totb), w=[("totb",)])
        gdma(lambda e: e.dma_start(out=i64rep[:], in_=c_i64), w=[("i64",)])
        gdma(lambda e: e.dma_start(out=ones_bf[:], in_=c_ones), w=[("ones",)])
        sdma(lambda e: e.dma_start(out=sel_sb[:], in_=sel_in), w=[("sel",)])
        sdma(lambda e: e.dma_start(out=sel1m_sb[:], in_=sel1m_in), w=[("sel1m",)])
        sdma(lambda e: e.dma_start(out=dtb_rep[:], in_=dtb.unsqueeze(0).to_broadcast([128, NH])),
             w=[("dtbrep",)])
        sdma(lambda e: e.dma_start(out=dtbcol[:], in_=dtb.unsqueeze(1)), w=[("dtbcol",)])
        sdma(lambda e: e.dma_start(out=D_rep[:], in_=dd.unsqueeze(0).to_broadcast([128, NH])),
             w=[("drep",)])
        sdma(lambda e: e.dma_start(out=arow[:], in_=alog.unsqueeze(0)), w=[("arow",)])
        sdma(lambda e: e.dma_start(out=acolf[:], in_=alog.unsqueeze(1)), w=[("acolf",)])
        gdma(lambda e: e.dma_start(out=w_dt[:].rearrange("p (k h) -> p k h", h=32),
                                   in_=winT[:, 4224:4256].rearrange("(k p) h -> p k h", p=128)),
             w=[("wdt",)])
        gdma(lambda e: e.dma_start(out=w_r[:].rearrange("p (k h) -> p k h", h=8),
                                   in_=wrout.rearrange("(k p) h -> p k h", p=128)),
             w=[("wr",)])
        sdma(lambda e: e.dma_start(out=cw_sb[:].rearrange("p (s k) -> p s k", k=4),
                                   in_=convw.rearrange("(s p) k -> p s k", p=128)),
             w=[("cw",)])
        sdma(lambda e: e.dma_start(out=cb_sb[:], in_=convb), w=[("cb",)])
        sdma(lambda e: e.dma_start(out=nw_sb[:], in_=nw), w=[("nw",)])
        act(lambda e: e.activation(arow[:], arow[:], AF.Exp), r=[("arow",)], w=[("arow",)])
        dve(lambda e: e.tensor_scalar_mul(arow[:], arow[:], -1.0), r=[("arow",)], w=[("arow",)])
        sdma(lambda e: e.dma_start(out=a_d, in_=arow[:]), r=[("arow",)], w=[("a_d",)])
        sdma(lambda e: e.dma_start(out=A_rep[:], in_=a_d.to_broadcast([128, NH])),
             r=[("a_d",)], w=[("arep",)])
        act(lambda e: e.activation(acolf[:], acolf[:], AF.Exp), r=[("acolf",)], w=[("acolf",)])
        dve(lambda e: e.tensor_scalar_mul(acolf[:], acolf[:], -1.0), r=[("acolf",)], w=[("acolf",)])
        dve(lambda e: e.tensor_tensor(
            diagD[:].rearrange("p (h l) -> p h l", l=HD),
            i64rep[:].rearrange("p (h l) -> p h l", l=HD),
            D_rep[:].unsqueeze(2).to_broadcast([128, NH, HD]), ALU.mult),
            r=[("i64",), ("drep",)], w=[("diagD",)])

        def xn_col(k, t):
            return k * XN_W + 3 + t

        def layernorm(u_ap, xn_buf, xnkey):
            ubufs = (u_sb0, u_sb1)
            ukeys = (("u0",), ("u0",))
            for tt in range(NT):
                ub, uk = ubufs[tt % 2], ukeys[tt % 2]
                sdma(lambda e, tt=tt, ub=ub: e.dma_start(
                    out=ub[:], in_=u_ap[tt * 128:(tt + 1) * 128, :]), w=[uk])
                dve(lambda e, ub=ub: e.reduce_sum(stats[:, 0:1], ub[:], axis=AX.X, negate=True),
                    r=[uk], w=[("st0",)])
                dve(lambda e: e.tensor_scalar_mul(stats[:, 0:1], stats[:, 0:1], 1.0 / DM),
                    r=[("st0",)], w=[("st0",)], sync=True)
                dve(lambda e, ub=ub: e.tensor_scalar_add(ub[:], ub[:], stats[:, 0:1]),
                    r=[uk, ("st0",)], w=[uk], sync=True)
                act(lambda e, ub=ub: e.activation(sq_scr[:], ub[:], AF.Square,
                                                  accum_out=stats[:, 1:2]),
                    r=[uk], w=[("sq",), ("st1",)])
                dve(lambda e: e.tensor_scalar(stats[:, 2:3], stats[:, 1:2], 1.0 / DM, 1e-5,
                                              ALU.mult, ALU.add),
                    r=[("st1",)], w=[("st2",)])
                act(lambda e: e.activation(stats[:, 3:4], stats[:, 2:3], AF.Sqrt),
                    r=[("st2",)], w=[("st3",)])
                dve(lambda e: e.reciprocal(stats[:, 4:5], stats[:, 3:4]),
                    r=[("st3",)], w=[("st4",)])
                dve(lambda e, ub=ub: e.tensor_scalar_mul(
                    sq_scr[:].bitcast(BF16)[:, 0:1024], ub[:], stats[:, 4:5]),
                    r=[uk, ("st4",)], w=[("sq",)], sync=True)
                for kb in range(0, KD, 4):
                    bank = 6 + (kb // 4)
                    for k in range(kb, kb + 4):
                        pe(lambda e, k=k, kb=kb, bank=bank: e.transpose(
                            PSB[:, bank * 1024 + (k - kb) * 128: bank * 1024 + (k - kb + 1) * 128],
                            sq_scr[:].bitcast(BF16)[:, k * 128:(k + 1) * 128], id_bf[:]),
                           r=[("sq",), ("id",)], w=[("PS", bank)])
                    dve(lambda e, tt=tt, kb=kb, bank=bank, xn_buf=xn_buf: e.tensor_copy(
                        _mkv(xn_buf[:], [(XN_W, 4), (1, 128)], kb * XN_W + 3 + tt * 128),
                        PSB[:, bank * 1024: bank * 1024 + 512]),
                        r=[("PS", bank)], w=[xnkey])

        def softplus_tm():
            dtbv = _mkv(dtb_rep[:], [(0, NT), (1, NH)], 0)
            dve(lambda e: e.tensor_tensor(scrA[:].rearrange("p (t h) -> p t h", h=NH),
                                          dtraw[:].rearrange("p (t h) -> p t h", h=NH),
                                          dtbv, ALU.add),
                r=[("dtraw",), ("dtbrep",)], w=[("scrA",)])
            act(lambda e: e.activation(scrB[:], scrA[:], AF.Abs),
                r=[("scrA",)], w=[("scrB",)])
            act(lambda e: e.activation(scrB[:], scrB[:], AF.Exp, scale=-1.0),
                r=[("scrB",)], w=[("scrB",)])
            act(lambda e: e.activation(scrB[:], scrB[:], AF.Ln, bias=1.0),
                r=[("scrB",)], w=[("scrB",)])
            dve(lambda e: e.tensor_scalar_max(scrA[:], scrA[:], 0.0),
                r=[("scrA",)], w=[("scrA",)])
            dve(lambda e: e.tensor_add(dt_tm[:], scrA[:], scrB[:]),
                r=[("scrA",), ("scrB",)], w=[("dt_tm",)])

        def dt_router_pass(xn_buf, xnkey, main):
            for tt in range(NT):
                for k in range(KD):
                    pe(lambda e, k=k, tt=tt, xn_buf=xn_buf: e.matmul(
                        PS[:, tt * 32:(tt + 1) * 32],
                        _mkv(xn_buf[:], [(1, 128)], xn_col(k, tt * 128)),
                        w_dt[:, k * 32:(k + 1) * 32],
                        start=(k == 0 and tt == 0), stop=(k == KD - 1)),
                       r=[xnkey, ("wdt",)], w=[("PS", 0)])
                    if main:
                        pe(lambda e, k=k, tt=tt, xn_buf=xn_buf: e.matmul(
                            PS[:, 1024 + tt * 8: 1024 + (tt + 1) * 8],
                            _mkv(xn_buf[:], [(1, 128)], xn_col(k, tt * 128)),
                            w_r[:, k * 8:(k + 1) * 8],
                            start=(k == 0 and tt == 0), stop=(k == KD - 1)),
                           r=[xnkey, ("wr",)], w=[("PS", 2)])
            dve(lambda e: e.tensor_copy(dtraw[:], PS[:, 0:512]), r=[("PS", 0)], w=[("dtraw",)])
            softplus_tm()
            arepv = _mkv(A_rep[:], [(0, NT), (1, NH)], 0)
            dve(lambda e: e.tensor_tensor(scrA[:].rearrange("p (t h) -> p t h", h=NH),
                                          dt_tm[:].rearrange("p (t h) -> p t h", h=NH),
                                          arepv, ALU.mult),
                r=[("dt_tm",), ("arep",)], w=[("scrA",)])
            for j in range(NT):
                pe(lambda e, j=j: e.matmul(PS[:, j * 32:(j + 1) * 32], trib[:],
                                           scrA[:, j * 32:(j + 1) * 32],
                                           start=(j == 0), stop=True),
                   r=[("trib",), ("scrA",)], w=[("PS", 0)])
                pe(lambda e, j=j: e.matmul(PS[:, 512 + j * 32: 512 + (j + 1) * 32], totb[:],
                                           scrA[:, j * 32:(j + 1) * 32],
                                           start=(j == 0), stop=True),
                   r=[("totb",), ("scrA",)], w=[("PS", 1)])
            dve(lambda e: e.tensor_copy(cs_sb[:], PS[:, 0:512]), r=[("PS", 0)], w=[("cs",)])
            dve(lambda e: e.tensor_copy(tot_sb[:], PS[:, 512:1024]), r=[("PS", 1)], w=[("tot",)])
            if main:
                act(lambda e: e.activation(sdo_sb[:], cs_sb[:], AF.Exp), r=[("cs",)], w=[("sdo",)])
            pool(lambda e: e.tensor_sub(ds_sb[:], tot_sb[:], cs_sb[:]),
                 r=[("tot",), ("cs",)], w=[("ds",)])
            act(lambda e: e.activation(ds_sb[:], ds_sb[:], AF.Exp), r=[("ds",)], w=[("ds",)])
            pool(lambda e: e.tensor_mul(dtds[:], dt_tm[:], ds_sb[:]),
                 r=[("dt_tm",), ("ds",)], w=[("dtds",)])
            sdma(lambda e: e.dma_start(
                out=bass.AP(tot_d.tensor, tot_d.offset, [[1, 1], [64, NT], [1, NH]]),
                in_=tot_sb[0:1, :].rearrange("o (j h) -> o j h", h=NH)),
                r=[("tot",)], w=[("tot_d",)])
            sdma(lambda e: e.dma_start(
                out=bass.AP(tot_d.tensor, tot_d.offset + NH, [[1, 1], [64, NT], [1, NH]]),
                in_=tot_sb[64:65, :].rearrange("o (j h) -> o j h", h=NH)),
                r=[("tot",)], w=[("tot_d",)])
            sdma(lambda e: e.dma_start(out=la[:], in_=tot_d.to_broadcast([64, NCH * NH])),
                 r=[("tot_d",)], w=[("la",)])
            act(lambda e: e.activation(la[:], la[:], AF.Exp), r=[("la",)], w=[("la",)])

        def router_softmax_wgt():
            dve(lambda e: e.tensor_copy(logits[:], PS[:, 1024:1152]),
                r=[("PS", 2)], w=[("logits",), ("scrB",)])
            for tt in range(NT):
                dve(lambda e, tt=tt: e.reduce_max(stats[:, 5:6], logits[:, tt * 8:(tt + 1) * 8],
                                                  axis=AX.X),
                    r=[("logits",)], w=[("st5",)])
                dve(lambda e, tt=tt: e.tensor_scalar_sub(probs[:, tt * 8:(tt + 1) * 8],
                                                         logits[:, tt * 8:(tt + 1) * 8],
                                                         stats[:, 5:6]),
                    r=[("logits",), ("st5",)], w=[("probs",)], sync=True)
            act(lambda e: e.activation(probs[:], probs[:], AF.Exp), r=[("probs",)], w=[("probs",)])
            for tt in range(NT):
                dve(lambda e, tt=tt: e.reduce_sum(stats[:, 6:7], probs[:, tt * 8:(tt + 1) * 8],
                                                  axis=AX.X),
                    r=[("probs",)], w=[("st6",)])
                dve(lambda e: e.reciprocal(stats[:, 7:8], stats[:, 6:7]), r=[("st6",)], w=[("st7",)],
                    sync=True)
                dve(lambda e, tt=tt: e.tensor_scalar_mul(probs[:, tt * 8:(tt + 1) * 8],
                                                         probs[:, tt * 8:(tt + 1) * 8],
                                                         stats[:, 7:8]),
                    r=[("probs",), ("st7",)], w=[("probs",)], sync=True)
            dve(lambda e: e.memset(cnt_sb[:], 0.0), w=[("cnt",), ("scrB",)])
            for ee in range(8):
                for e2 in range(8):
                    if e2 == ee:
                        continue
                    op = ALU.is_ge if e2 < ee else ALU.is_gt
                    dve(lambda e, ee=ee, e2=e2, op=op: e.tensor_tensor(
                        tmpc[:], _mkv(probs[:], [(8, NT)], e2),
                        _mkv(probs[:], [(8, NT)], ee), op),
                        r=[("probs",)], w=[("tmpc",)])
                    pool(lambda e, ee=ee: e.tensor_add(
                        _mkv(cnt_sb[:], [(8, NT)], ee),
                        _mkv(cnt_sb[:], [(8, NT)], ee), tmpc[:]),
                        r=[("cnt",), ("tmpc",)], w=[("cnt",)])
            dve(lambda e: e.tensor_single_scalar(wgt[:], cnt_sb[:], 1.5, ALU.is_lt),
                r=[("cnt",)], w=[("wgt",)])
            dve(lambda e: e.tensor_mul(wgt[:], wgt[:], probs[:]),
                r=[("wgt",), ("probs",)], w=[("wgt",)])
            dve(lambda e: e.tensor_copy(wgt_bf[:], wgt[:]), r=[("wgt",)], w=[("wgtbf",)])
            for tt in range(NT):
                bank = 6 + (tt % 2)
                pe(lambda e, tt=tt, bank=bank: e.transpose(
                    PSB[0:8, bank * 1024: bank * 1024 + 128],
                    wgt_bf[:, tt * 8:(tt + 1) * 8], id_bf[:]),
                   r=[("wgtbf",), ("id",)], w=[("PS", bank)])
                dve(lambda e, tt=tt, bank=bank: e.tensor_copy(
                    wgtf[:, tt * 128:(tt + 1) * 128],
                    PSB[0:8, bank * 1024: bank * 1024 + 128]),
                    r=[("PS", bank)], w=[("wgtf",), ("dtraw",), ("scrA",)])

        def dtfm_g_pass(xn_buf, xnkey):
            sdma(lambda e: e.dma_start(out=gmask_sb[:], in_=c_gmask),
                 w=[("gmask",), ("xnp",)])
            for k in range(KD):
                for n in range(4):
                    pe(lambda e, k=k, n=n, xn_buf=xn_buf: e.matmul(
                        PS[0:NH, n * 512:(n + 1) * 512],
                        w_dt[:, k * 32:(k + 1) * 32],
                        _mkv(xn_buf[:], [(1, 512)], xn_col(k, n * 512)),
                        start=(k == 0), stop=(k == KD - 1)),
                       r=[xnkey, ("wdt",)], w=[("PS", n)])
            dve(lambda e: e.tensor_copy(dtfm_a[:], PS[0:NH, 0:2048]),
                r=psk(0, 2048), w=[("dtfma",), ("xnp",)])
            tap("dtfmraw", dtfm_a[:], [("dtfma",)])
            dve(lambda e: e.tensor_scalar_add(dtfm_a[:], dtfm_a[:], dtbcol[:, 0:1]),
                r=[("dtfma",), ("dtbcol",)], w=[("dtfma",)])
            tap("dtfm1", dtfm_a[:], [("dtfma",)])
            act(lambda e: e.activation(dtfm_b[:], dtfm_a[:], AF.Abs),
                r=[("dtfma",)], w=[("dtfmb",), ("xnp",)])
            act(lambda e: e.activation(dtfm_b[:], dtfm_b[:], AF.Exp, scale=-1.0),
                r=[("dtfmb",)], w=[("dtfmb",)])
            act(lambda e: e.activation(dtfm_b[:], dtfm_b[:], AF.Ln, bias=1.0),
                r=[("dtfmb",)], w=[("dtfmb",)])
            tap("dtfm2", dtfm_b[:], [("dtfmb",)])
            dve(lambda e: e.tensor_scalar_max(dtfm_a[:], dtfm_a[:], 0.0),
                r=[("dtfma",)], w=[("dtfma",)])
            dve(lambda e: e.tensor_add(dtfm_a[:], dtfm_a[:], dtfm_b[:]),
                r=[("dtfma",), ("dtfmb",)], w=[("dtfma",)])
            tap("dtfm3", dtfm_a[:], [("dtfma",)])
            dve(lambda e: e.tensor_scalar_mul(dtfm_a[:], dtfm_a[:], acolf[:, 0:1]),
                r=[("dtfma",), ("acolf",)], w=[("dtfma",)])
            tap("dtfm4", dtfm_a[:], [("dtfma",)])
            act(lambda e: e.activation(dtfm_a[:], dtfm_a[:], AF.Exp),
                r=[("dtfma",)], w=[("dtfma",)])
            tap("dtfm5", dtfm_a[:], [("dtfma",)])
            dve(lambda e: e.tensor_mul(dtfm_a[:], dtfm_a[:], gmask_sb[:]),
                r=[("dtfma",), ("gmask",)], w=[("dtfma",)])
            tap("gmask", gmask_sb[:], [("gmask",)])
            tap("dtfm6", dtfm_a[:], [("dtfma",)])
            sdma(lambda e: e.dma_start(out=g_d, in_=dtfm_a[:]), r=[("dtfma",)], w=[("g_d",)])

        def z_pass(xn_buf, xnkey):
            gdma(lambda e: e.dma_start(
                out=w_z[:].rearrange("p (k z) -> p k z", z=2048),
                in_=winT[:, 0:DI].rearrange("(k p) z -> p k z", p=128)),
                w=[("w_z",)])
            zbufs = (zs_b0, zs_b1)
            zkeys = (("zs0",), ("zs1",))
            for tt in range(NT):
                par = tt % 2
                pscol = par * 2048
                pkeys = psk(pscol, pscol + 2048)
                for k in range(KD):
                    for n in range(4):
                        pe(lambda e, k=k, tt=tt, n=n, pscol=pscol, xn_buf=xn_buf: e.matmul(
                            PS[:, pscol + n * 512: pscol + (n + 1) * 512],
                            _mkv(xn_buf[:], [(1, 128)], xn_col(k, tt * 128)),
                            w_z[:, k * 2048 + n * 512: k * 2048 + (n + 1) * 512],
                            start=(k == 0), stop=(k == KD - 1)),
                           r=[xnkey, ("w_z",)], w=[pkeys[n]])
                zbuf, zkey = zbufs[par], zkeys[par]
                act(lambda e, pscol=pscol, zbuf=zbuf: e.activation(
                    zbuf[:], PS[:, pscol:pscol + 2048], AF.Silu),
                    r=pkeys, w=[zkey])
                sdma(lambda e, tt=tt, zbuf=zbuf: e.dma_start(
                    out=zs_d[tt * 128:(tt + 1) * 128, :], in_=zbuf[:]),
                    r=[zkey], w=[("zs_d", tt)])

        def half_pipeline(u_ap, xn_buf, xnkey, main):
            layernorm(u_ap, xn_buf, xnkey)
            if not main:
                for k in range(KD):
                    dve(lambda e, k=k, xn_buf=xn_buf: e.memset(
                        _mkv(xn_buf[:], [(1, 3)], k * XN_W), 0.0),
                        w=[xnkey])
            else:
                dve(lambda e: e.tensor_copy(_mkv(xn_buf[:], [(XN_W, KD), (1, 3)], 0),
                                            _mkv(xnp[:], [(XN_W, KD), (1, 3)], XN_W - 3)),
                    r=[("xnp",)], w=[xnkey])
            dt_router_pass(xn_buf, xnkey, main)
            if main:
                tap("xn", xn_buf[:], [xnkey])
                tap("dtraw", dtraw[:], [("dtraw",)])
                tap("dt_tm", dt_tm[:], [("dt_tm",)])
                tap("cs", cs_sb[:], [("cs",)])
                tap("tot", tot_sb[:], [("tot",)])
                tap("sdo", sdo_sb[:], [("sdo",)])
                tap("ds", ds_sb[:], [("ds",)])
                tap("dtds", dtds[:], [("dtds",)])
                tap("la", la[:], [("la",)])
                router_softmax_wgt()
                tap("logits", logits[:], [("logits",)])
                tap("probs", probs[:], [("probs",)])
                tap("wgt", wgt[:], [("wgt",)])
                tap("wgtf", wgtf[:], [("wgtf",)])
                dtfm_g_pass(xn_buf, xnkey)
                tapd("g_d", g_d, [("g_d",)])
                z_pass(xn_buf, xnkey)
                tapd("zs_d", zs_d, [("zs_d", t) for t in range(NT)])
                pb.barrier()
            if not main:
                gdma(lambda e: e.dma_start(out=S0[:], in_=init_in), w=[("S", 0)])
            else:
                gdma(lambda e: e.dma_start(out=S1[:], in_=init_in), w=[("S", 1)])
                dve(lambda e: e.tensor_scalar_mul(S1[:], S1[:], sel1m_sb[:, 0:1]),
                    r=[("S", 1), ("sel1m",)], w=[("S", 1)])
                dve(lambda e: e.scalar_tensor_tensor(S0[:], S0[:], sel_sb[:, 0:1], S1[:],
                                                     ALU.mult, ALU.add),
                    r=[("S", 0), ("sel",), ("S", 1)], w=[("S", 0)])
                tap("Sinit", S0[:], [("S", 0)])

            wsls = (wsl0, wsl1)
            pres = (pre0, pre1)
            accs = (acc0, acc1)
            xss = (xs0, xs1)
            def emit_slabs(tb):
                rp = (tb % 2) if not main else 0
                rbuf = (xr, yr)[rp] if not main else xr
                tb0 = 6 if main else 2
                for slab in range(NSLAB):
                    sp = slab % 2
                    wk = ("wsl", sp)
                    pk = ("pre", sp)
                    ak = ("acc",)
                    xk2 = ("xs", sp)
                    gdma(lambda e, slab=slab, sp=sp: e.dma_start(
                        out=wsls[sp][:].rearrange("p (k c) -> p k c", c=128),
                        in_=winT[:, 2048 + slab * 128: 2048 + (slab + 1) * 128]
                        .rearrange("(k p) c -> p k c", p=128)),
                        w=[wk])
                    for k in range(KD):
                        pe(lambda e, k=k, tb=tb, sp=sp, xn_buf=xn_buf: e.matmul(
                            PS[:, 512:515],
                            wsls[sp][:, k * 128:(k + 1) * 128],
                            _mkv(xn_buf[:], [(1, 3)], k * XN_W + tb * 512),
                            start=(k == 0), stop=(k == KD - 1)),
                           r=[wk, xnkey], w=[("PS", 1)])
                        pe(lambda e, k=k, tb=tb, sp=sp, xn_buf=xn_buf: e.matmul(
                            PS[:, 0:512],
                            wsls[sp][:, k * 128:(k + 1) * 128],
                            _mkv(xn_buf[:], [(1, 512)], xn_col(k, tb * 512)),
                            start=(k == 0), stop=(k == KD - 1)),
                           r=[wk, xnkey], w=[("PS", 0)])
                    dve(lambda e, sp=sp: e.tensor_copy(pres[sp][:, 0:3], PS[:, 512:515]),
                        r=[("PS", 1)], w=[pk])
                    dve(lambda e, sp=sp: e.tensor_copy(pres[sp][:, 3:515], PS[:, 0:512]),
                        r=[("PS", 0)], w=[pk])
                    dve(lambda e, sp=sp, slab=slab: e.tensor_scalar(
                        accs[sp][:], pres[sp][:, 0:512], cw_sb[:, slab * 4: slab * 4 + 1],
                        None, ALU.mult),
                        r=[pk, ("cw",)], w=[ak])
                    for kk in range(1, 4):
                        dve(lambda e, sp=sp, slab=slab, kk=kk: e.scalar_tensor_tensor(
                            accs[sp][:], pres[sp][:, kk:kk + 512],
                            cw_sb[:, slab * 4 + kk: slab * 4 + kk + 1],
                            accs[sp][:], ALU.mult, ALU.add),
                            r=[pk, ("cw",), ak], w=[ak])
                    bank = tb0 + sp
                    if slab < 16:
                        act(lambda e, sp=sp, slab=slab: e.activation(
                            xss[sp][:], accs[sp][:], AF.Silu, bias=cb_sb[:, slab:slab + 1]),
                            r=[ak, ("cb",)], w=[xk2])
                        for q in range(4):
                            pe(lambda e, sp=sp, q=q, bank=bank: e.transpose(
                                PSB[:, bank * 1024 + q * 128: bank * 1024 + (q + 1) * 128],
                                xss[sp][:, q * 128:(q + 1) * 128], id_bf[:]),
                               r=[xk2, ("id",)], w=[("PS", bank)])
                        dve(lambda e, slab=slab, bank=bank, rbuf=rbuf: e.tensor_copy(
                            _mkv(rbuf[:], [(2048, 4), (1, 128)], slab * 128),
                            PSB[:, bank * 1024: bank * 1024 + 512]),
                            r=[("PS", bank)], w=[("xr", rp)])
                    else:
                        act(lambda e, sp=sp, slab=slab, tb=tb: e.activation(
                            bc_fm[:, tb * 512:(tb + 1) * 512], accs[sp][:], AF.Silu,
                            bias=cb_sb[:, slab:slab + 1]),
                            r=[ak, ("cb",)], w=[("bcfm", rp)])
                        for q in range(4):
                            pe(lambda e, tb=tb, q=q, bank=bank: e.transpose(
                                PSB[:, bank * 1024 + q * 128: bank * 1024 + (q + 1) * 128],
                                bc_fm[:, tb * 512 + q * 128: tb * 512 + (q + 1) * 128],
                                id_bf[:]),
                               r=[("bcfm", rp), ("id",)], w=[("PS", bank)])
                        dve(lambda e, bank=bank, rp=rp: e.tensor_copy(
                            (bct, bct1)[rp][:, 0:512],
                            PSB[:, bank * 1024: bank * 1024 + 512]),
                            r=[("PS", bank)], w=[("bct", rp)])
                        sdma(lambda e, tb=tb: e.dma_start(
                            out=c_fm0[:, tb * 512:(tb + 1) * 512],
                            in_=bc_fm[64:128, tb * 512:(tb + 1) * 512]),
                            r=[("bcfm", rp)], w=[("cfm0", rp)])

            def emit_chunks(tb):
                rp = (tb % 2) if not main else 0
                rbuf = (xr, yr)[rp] if not main else xr
                for j2 in range(8):
                    c = tb * 8 + j2
                    par = c % 2
                    base = par * 64
                    tile = c // 2
                    slot = tile % 4
                    kSc = ("S", c % 2)
                    kSn = ("S", (c + 1) % 2)
                    if main:
                        sdma(lambda e, c=c, base=base: e.dma_start(
                            out=zb_b[base:base + 64, :], in_=zs_d[c * 64:(c + 1) * 64, :]),
                            r=[("zs_d", c // 2)], w=[("zb", par)])
                        sdma(lambda e, c=c, base=base: e.dma_start(
                            out=gr_b[base:base + 64, :].rearrange("p (h l) -> p h l", l=64),
                            in_=bass.AP(g_d.tensor, g_d.offset + c * 64,
                                        [[0, 64], [T, NH], [1, 64]])),
                            r=[("g_d",)], w=[("gr", par)])
                        pool(lambda e, base=base, slot=slot, tile=tile: e.tensor_tensor(
                            x2b[base:base + 64, :].rearrange("p (h q) -> p h q", q=HD),
                            _mkv(xr[base:base + 64, :], [(64, NH), (1, HD)], slot * 2048),
                            _mkv(dt_tm[base:base + 64, :], [(1, NH), (0, HD)], tile * 32),
                            ALU.mult),
                            r=[("xr", rp), ("dt_tm",)], w=[("x2", par)])
                    pool(lambda e, base=base, slot=slot, tile=tile, rbuf=rbuf: e.tensor_tensor(
                        xdb[base:base + 64, :].rearrange("p (h q) -> p h q", q=HD),
                        _mkv(rbuf[base:base + 64, :], [(64, NH), (1, HD)], slot * 2048),
                        _mkv(dtds[base:base + 64, :], [(1, NH), (0, HD)], tile * 32),
                        ALU.mult),
                        r=[("xr", rp), ("dtds",)], w=[("xd", par)])
                    for n in range(4):
                        pe(lambda e, base=base, slot=slot, n=n, rp=rp: e.matmul(
                            PS[0:64, 2048 + n * 512: 2048 + (n + 1) * 512],
                            (bct, bct1)[rp][base:base + 64, slot * 128: slot * 128 + 64],
                            xdb[base:base + 64, n * 512:(n + 1) * 512],
                            start=True, stop=True, tile_position=(base, 0)),
                           r=[("bct", rp), ("xd", par)], w=[("PS", 4 + n)])
                    if main:
                        pe(lambda e, c=c, base=base: e.matmul(
                            PS[base:base + 64, 0:64],
                            bc_fm[0:64, c * 64:(c + 1) * 64],
                            c_fm0[:, c * 64:(c + 1) * 64],
                            start=True, stop=True, tile_position=(0, base)),
                           r=[("bcfm", rp), ("cfm0", rp)], w=[("PS", 0)])
                        dve(lambda e, base=base: e.tensor_tensor_scan(
                            lt_b[base:base + 64, :], gr_b[base:base + 64, :],
                            i64rep[base:base + 64, :], 0.0, ALU.mult, ALU.add),
                            r=[("gr", par), ("i64",)], w=[("lt", par)])
                        dve(lambda e, base=base: e.tensor_tensor(
                            gl_b[base:base + 64, :].rearrange("p (h l) -> p h l", l=64),
                            lt_b[base:base + 64, :].rearrange("p (h l) -> p h l", l=64),
                            _mkv(PS[base:base + 64, :], [(0, NH), (1, 64)], 0),
                            ALU.mult),
                            r=[("lt", par), ("PS", 0)], w=[("gl", par)])
                        for h in range(NH):
                            pe(lambda e, c=c, base=base, h=h: e.matmul(
                                PS[base:base + 64, h * 64:(h + 1) * 64],
                                c_fm0[:, c * 64:(c + 1) * 64],
                                (S0, S1)[c % 2][:, h * 64:(h + 1) * 64],
                                start=(h % 8 == 0), stop=False, tile_position=(0, base)),
                               r=[("cfm0", rp), kSc], w=[("PS", h // 8)])
                        dve(lambda e, base=base, tile=tile: e.tensor_tensor(
                            PS[base:base + 64, 0:2048].rearrange("p (h q) -> p h q", q=HD),
                            PS[base:base + 64, 0:2048].rearrange("p (h q) -> p h q", q=HD),
                            _mkv(sdo_sb[base:base + 64, :], [(1, NH), (0, HD)], tile * 32),
                            ALU.mult),
                            r=psk(0, 2048) + [("sdo",)], w=psk(0, 2048))
                        for h in range(NH):
                            pe(lambda e, base=base, h=h: e.matmul(
                                PS[base:base + 64, h * 64:(h + 1) * 64],
                                gl_b[base:base + 64, h * 64:(h + 1) * 64],
                                x2b[base:base + 64, h * 64:(h + 1) * 64],
                                start=False, stop=False, tile_position=(base, base)),
                               r=[("gl", par), ("x2", par)], w=[("PS", h // 8)])
                        for h in range(NH):
                            pe(lambda e, base=base, h=h, slot=slot: e.matmul(
                                PS[base:base + 64, h * 64:(h + 1) * 64],
                                diagD[base:base + 64, h * 64:(h + 1) * 64],
                                _mkv(xr[base:base + 64, :], [(1, 64)], slot * 2048 + h * 64),
                                start=False, stop=True, tile_position=(base, base)),
                               r=[("diagD",), ("xr", rp)], w=[("PS", h // 8)])
                    pool(lambda e, c=c: e.tensor_tensor(
                        (S0, S1)[(c + 1) % 2][:].rearrange("p (h q) -> p h q", q=HD),
                        (S0, S1)[c % 2][:].rearrange("p (h q) -> p h q", q=HD),
                        _mkv(la[:], [(1, NH), (0, HD)], c * 32),
                        ALU.mult),
                        r=[kSc, ("la",)], w=[kSn])
                    dve(lambda e, c=c: e.tensor_add(
                        (S0, S1)[(c + 1) % 2][:], (S0, S1)[(c + 1) % 2][:],
                        PS[0:64, 2048:4096]),
                        r=[kSn] + psk(2048, 4096), w=[kSn])
                    if main:
                        dve(lambda e, base=base, slot=slot: e.tensor_tensor(
                            _mkv(yr[base:base + 64, :], [(1, 2048)], slot * 2048),
                            PS[base:base + 64, 0:2048], zb_b[base:base + 64, :],
                            ALU.mult),
                            r=psk(0, 2048) + [("zb", par)], w=[("yr", slot)])

            def emit_ynorm(tb):
                if main:
                    for q in range(4):
                        tt = tb * 4 + q
                        yk = ("yr", q)
                        act(lambda e, q=q: e.activation(
                            sq_scr[:].bitcast(BF16)[:, 0:2048],
                            yr[:, q * 2048:(q + 1) * 2048], AF.Square,
                            accum_out=stats[:, 8:9]),
                            r=[yk], w=[("sq",), ("st8",)])
                        dve(lambda e: e.tensor_scalar(stats[:, 9:10], stats[:, 8:9],
                                                      1.0 / DI, 1e-5, ALU.mult, ALU.add),
                            r=[("st8",)], w=[("st9",)])
                        act(lambda e: e.activation(stats[:, 10:11], stats[:, 9:10], AF.Sqrt),
                            r=[("st9",)], w=[("st10",)])
                        dve(lambda e: e.reciprocal(stats[:, 11:12], stats[:, 10:11]),
                            r=[("st10",)], w=[("st11",)])
                        dve(lambda e, q=q: e.tensor_scalar_mul(
                            yr[:, q * 2048:(q + 1) * 2048],
                            yr[:, q * 2048:(q + 1) * 2048], stats[:, 11:12]),
                            r=[yk, ("st11",)], w=[yk], sync=True)
                        for kb in range(0, 16, 4):
                            bank = 6 + (kb // 4) % 2
                            for k in range(kb, kb + 4):
                                pe(lambda e, q=q, k=k, kb=kb, bank=bank: e.transpose(
                                    PSB[:, bank * 1024 + (k - kb) * 128:
                                        bank * 1024 + (k - kb + 1) * 128],
                                    yr[:, q * 2048 + k * 128: q * 2048 + (k + 1) * 128],
                                    id_bf[:]),
                                   r=[yk, ("id",)], w=[("PS", bank)])
                            dve(lambda e, kb=kb, bank=bank: e.tensor_copy(
                                yft[:, kb * 128:(kb + 4) * 128],
                                PSB[:, bank * 1024: bank * 1024 + 512]),
                                r=[("PS", bank)], w=[("yft",)])
                        sdma(lambda e, tt=tt: e.dma_start(
                            out=bass.AP(yfm_d.tensor, yfm_d.offset + tt * 128,
                                        [[T, 128], [128 * T, 16], [1, 128]]),
                            in_=yft[:].rearrange("p (k c) -> p k c", c=128)),
                            r=[("yft",)], w=[("yfm", tt)])

            if main:
                for tb in range(NTB):
                    emit_slabs(tb)
                    emit_chunks(tb)
                    emit_ynorm(tb)
            else:
                emit_slabs(0)
                for tb in range(NTB):
                    if tb + 1 < NTB:
                        emit_slabs(tb + 1)
                    emit_chunks(tb)

        half_pipeline(up_in, xnp, ("xnp",), main=False)
        pb.barrier()
        half_pipeline(u_in, xn, ("xn",), main=True)
        tapd("yfm_d", yfm_d, [("yfm", t) for t in range(NT)])
        pb.barrier()

        gdma(lambda e: e.dma_start(out=b2_bf[:], in_=b2_in), w=[("b2",)])
        for th in range(2):
            for q in range(8):
                tt = th * 8 + q
                pcol = 1024 + (q % 2) * 1024
                pkeys = psk(pcol, pcol + 1024)
                for n in range(2):
                    pe(lambda e, tt=tt, n=n, pcol=pcol: e.matmul(
                        PS[:, pcol + n * 512: pcol + (n + 1) * 512],
                        wgtf[:, tt * 128:(tt + 1) * 128],
                        b2_bf[:, n * 512:(n + 1) * 512],
                        start=True, stop=True),
                       r=[("wgtf",), ("b2",)], w=[pkeys[n]])
                dve(lambda e, q=q, pcol=pcol: e.tensor_copy(
                    ma[:, q * DM:(q + 1) * DM], PS[:, pcol:pcol + 1024]),
                    r=pkeys, w=[("ma", q)])
                if DEBUG and th == 0 and q == 0:
                    tap("mab2", ma[:, 0:1024], [("ma", 0)])
            for ee in range(8):
                for qq in range(4):
                    wp = qq % 2
                    k1, k2 = ("w1q", wp), ("w2q", wp)
                    gdma(lambda e, ee=ee, qq=qq, wp=wp: e.dma_start(
                        out=w1q[wp][:].rearrange("p (k j) -> p k j", j=1024),
                        in_=bass.AP(w1_in.tensor,
                                    w1_in.offset + ee * DM * 4096 + qq * 1024,
                                    [[4096, 128], [128 * 4096, 8], [1, 1024]])),
                        w=[k1])
                    gdma(lambda e, ee=ee, qq=qq, wp=wp: e.dma_start(
                        out=w2q[wp][:].rearrange("p (k j) -> p k j", j=1024),
                        in_=bass.AP(w2_in.tensor,
                                    w2_in.offset + ee * 4096 * DM + qq * 1024 * DM,
                                    [[DM, 128], [128 * DM, 8], [1, 1024]])),
                        w=[k2])
                    if qq == 0:
                        sdma(lambda e, ee=ee: e.dma_start(out=b1c[:], in_=b1_in[ee]),
                             w=[("b1",)])
                    def w1_block(t5, wp=wp, k1=k1):
                        tok0 = th * 1024 + t5 * 512
                        hp = t5 % 2
                        hqb = (hq0, hq1)[hp]
                        hk = ("hq", hp)
                        for wv in range(4):
                            pw = wv % 2
                            for mi in range(2):
                                mj = wv * 2 + mi
                                col = (pw * 2 + mi) * 512
                                for k in range(KD):
                                    pe(lambda e, k=k, mj=mj, col=col, tok0=tok0, wp=wp: e.matmul(
                                        PS[:, col:col + 512],
                                        w1q[wp][:, k * 1024 + mj * 128: k * 1024 + (mj + 1) * 128],
                                        _mkv(xn[:], [(1, 512)], xn_col(k, tok0)),
                                        start=(k == 0), stop=(k == KD - 1)),
                                       r=[k1, ("xn",)], w=[("PS", pw * 2 + mi)])
                            for mi in range(2):
                                mj = wv * 2 + mi
                                col = (pw * 2 + mi) * 512
                                act(lambda e, mj=mj, col=col, hqb=hqb, pw=pw, mi=mi, qq=qq: e.activation(
                                    hqb[:, mj * 512:(mj + 1) * 512], PS[:, col:col + 512],
                                    AF.Gelu_apprx_tanh,
                                    bias=b1c[:, qq * 8 + mj: qq * 8 + mj + 1]),
                                    r=[("PS", pw * 2 + mi), ("b1",)], w=[hk])

                    def w2_block(t5, wp=wp, k2=k2, ee=ee, qq=qq):
                        hp = t5 % 2
                        hqb = (hq0, hq1)[hp]
                        hk = ("hq", hp)
                        for tsub in range(4):
                            po = tsub % 2
                            ocol = 2048 + po * 1024
                            okeys = psk(ocol, ocol + 1024)
                            for mj in range(8):
                                for n in range(2):
                                    pe(lambda e, mj=mj, n=n, tsub=tsub, ocol=ocol, wp=wp,
                                       hqb=hqb: e.matmul(
                                        PS[:, ocol + n * 512: ocol + (n + 1) * 512],
                                        hqb[:, mj * 512 + tsub * 128: mj * 512 + (tsub + 1) * 128],
                                        w2q[wp][:, mj * 1024 + n * 512: mj * 1024 + (n + 1) * 512],
                                        start=(mj == 0), stop=(mj == 7)),
                                       r=[hk, k2], w=[okeys[n]])
                            tt = th * 8 + t5 * 4 + tsub
                            qm = t5 * 4 + tsub
                            dve(lambda e, tt=tt, qm=qm, ocol=ocol, ee=ee: e.scalar_tensor_tensor(
                                ma[:, qm * DM:(qm + 1) * DM],
                                PS[:, ocol:ocol + 1024],
                                wgt[:, tt * 8 + ee: tt * 8 + ee + 1],
                                ma[:, qm * DM:(qm + 1) * DM],
                                ALU.mult, ALU.add),
                                r=okeys + [("wgt",), ("ma", qm)], w=[("ma", qm)])

                    for t5 in range(3):
                        if t5 < 2:
                            w1_block(t5)
                        if t5 >= 1:
                            w2_block(t5 - 1)
            pb.barrier()
            gdma(lambda e: e.dma_start(
                out=wout_bf[:].rearrange("p (k c) -> p k c", c=1024),
                in_=woutT.rearrange("(k p) c -> p k c", p=128)),
                w=[("woutbf",)])
            for k in range(16):
                dve(lambda e, k=k: e.tensor_scalar_mul(
                    wout_bf[:, k * 1024:(k + 1) * 1024],
                    wout_bf[:, k * 1024:(k + 1) * 1024], nw_sb[:, k:k + 1]),
                    r=[("woutbf",), ("nw",)], w=[("woutbf",)])
            for q in range(8):
                tt = th * 8 + q
                par = q % 2
                pcol = par * 1024
                pkeys = psk(pcol, pcol + 1024)
                ub = (u_sb0, u_sb1)[par]
                ob = (out_sb0, out_sb1)[par]
                uk = ("u0",)
                ok = (("o0",), ("o1",))[par]
                sdma(lambda e, tt=tt: e.dma_start(
                    out=yft2[:].rearrange("p (k c) -> p k c", c=128),
                    in_=bass.AP(yfm_d.tensor, yfm_d.offset + tt * 128,
                                [[T, 128], [128 * T, 16], [1, 128]])),
                    r=[("yfm", tt)], w=[("yft2",)])
                sdma(lambda e, tt=tt, ub=ub: e.dma_start(
                    out=ub[:], in_=u_in[tt * 128:(tt + 1) * 128, :]), w=[uk])
                for k in range(16):
                    for n in range(2):
                        pe(lambda e, k=k, n=n, pcol=pcol: e.matmul(
                            PS[:, pcol + n * 512: pcol + (n + 1) * 512],
                            yft2[:, k * 128:(k + 1) * 128],
                            wout_bf[:, k * 1024 + n * 512: k * 1024 + (n + 1) * 512],
                            start=(k == 0), stop=(k == 15)),
                           r=[("yft2",), ("woutbf",)], w=[pkeys[n]])
                dve(lambda e, q=q, ub=ub, ob=ob: e.tensor_add(
                    ob[:], ub[:], ma[:, q * DM:(q + 1) * DM]),
                    r=[uk, ("ma", q)], w=[ok])
                dve(lambda e, ob=ob, pcol=pcol: e.tensor_add(
                    ob[:], ob[:], PS[:, pcol:pcol + 1024]),
                    r=[ok] + pkeys, w=[ok])
                sdma(lambda e, tt=tt, ob=ob: e.dma_start(
                    out=out_d[tt * 128:(tt + 1) * 128, :], in_=ob[:]),
                    r=[ok], w=[("outd", tt)], tag="out")
            if th == 0:
                pb.barrier()

        pb.finalize(block, sems)
    return nc


_NC_CACHE = None


def _get_nc():
    global _NC_CACHE
    if _NC_CACHE is None:
        _NC_CACHE = build_nc()
    return _NC_CACHE


def _consts():
    ident = np.eye(128, dtype=np.float32)
    trib = np.zeros((128, 128), np.float32)
    totb = np.zeros((128, 128), np.float32)
    for h2 in range(2):
        sl = slice(h2 * 64, (h2 + 1) * 64)
        trib[sl, sl] = np.triu(np.ones((64, 64), np.float32))
        totb[sl, sl] = 1.0
    i64 = np.tile(np.eye(64, dtype=np.float32), (2, 32))
    ones = np.ones((1, 512), np.float32)
    return ident, trib, totb, i64, ones


def kernel(L0_attn_u, L0_attn_in_proj_w, L0_attn_conv_w, L0_attn_conv_b,
           L0_attn_init_states, L0_attn_dt_bias, L0_attn_A_log, L0_attn_D,
           L0_attn_norm_w, L0_attn_out_proj_w, L0_ffn_w_router, L0_ffn_w1,
           L0_ffn_b1, L0_ffn_w2, L0_ffn_b2):
    f = lambda x: np.ascontiguousarray(np.asarray(x, dtype=np.float32))
    u = f(L0_attn_u)
    winT = np.ascontiguousarray(f(L0_attn_in_proj_w).T)
    init_s = np.ascontiguousarray(
        f(L0_attn_init_states).transpose(2, 0, 1).reshape(64, NH * HD))
    woutT = np.ascontiguousarray(f(L0_attn_out_proj_w).T)
    ident, trib, totb, i64, ones = _consts()
    gmask = np.ones((NH, T), np.float32)
    gmask[:, ::64] = 0.0

    convb_st = np.ascontiguousarray(f(L0_attn_conv_b).reshape(NSLAB, 128).T)
    nw_st = np.ascontiguousarray(f(L0_attn_norm_w).reshape(16, 128).T)
    shared = dict(winT=winT, convw=f(L0_attn_conv_w), convb=convb_st,
                  dtb=f(L0_attn_dt_bias), alog=f(L0_attn_A_log), dd=f(L0_attn_D),
                  nw=nw_st, init=init_s, woutT=woutT,
                  wrout=f(L0_ffn_w_router), w1=f(L0_ffn_w1),
                  b1=np.ascontiguousarray(f(L0_ffn_b1).reshape(8, 32, 128).transpose(0, 2, 1)),
                  w2=f(L0_ffn_w2), b2=f(L0_ffn_b2), c_ident=ident, c_trib=trib,
                  c_totb=totb, c_i64=i64, c_ones=ones, c_gmask=gmask)
    in_maps = []
    zeros_u = np.zeros((T, DM), np.float32)
    for b in range(4):
        for half in range(2):
            m = dict(shared)
            m["u"] = np.ascontiguousarray(u[b, half * T:(half + 1) * T])
            m["up"] = np.ascontiguousarray(u[b, 0:T]) if half == 1 else zeros_u
            s = 1.0 if half == 1 else 0.0
            m["sel"] = np.full((64, 1), s, np.float32)
            m["sel1m"] = np.full((64, 1), 1.0 - s, np.float32)
            in_maps.append(m)

    nc = _get_nc()
    res = run_bass_kernel_spmd(nc, in_maps, core_ids=list(range(8)))
    out = np.empty((4, 4096, 1024), np.float32)
    for b in range(4):
        for half in range(2):
            out[b, half * T:(half + 1) * T] = res.results[b * 2 + half]["out"]
    return out


